# version 19
# baseline (speedup 1.0000x reference)
import numpy as np
import contextlib
import concourse.bass as bass
import concourse.mybir as mybir
from concourse.bass_utils import run_bass_kernel_spmd

F32 = mybir.dt.float32
BF16 = mybir.dt.bfloat16
AF = mybir.ActivationFunctionType
ALU = mybir.AluOpType
AX = mybir.AxisListType

NB, T, C, J = 4, 2048, 1024, 8
TH, NCK, E, FP = 1024, 4, 16, 256
LN_EPS, EPS_GATE = 1e-5, 1e-6
NTT = T // 128


class Prog:
    def __init__(self, nc):
        self.nc = nc
        self.ops = {k: [] for k in ("v", "s", "t", "g", "y")}
        self.sems, self.cnt = {}, {}

    def sem(self, es, name):
        self.sems[name] = es.enter_context(self.nc.semaphore(name))
        self.cnt[name] = 0

    AUTO = {"v": "sv", "s": "ss", "g": "sg"}

    def add(self, eng, fn, dep=True):
        if eng in self.AUTO:
            sem = self.AUTO[eng]
            h = self.sems[sem]
            pre = self.cnt[sem]
            self.cnt[sem] += 1
            def wrapped(e, fn=fn, h=h, pre=pre, dep=dep):
                if dep and pre > 0:
                    e.wait_ge(h, pre)
                fn(e).then_inc(h, 1)
            self.ops[eng].append(wrapped)
        else:
            self.ops[eng].append(fn)

    def inc(self, eng, fn, sem, n=1):
        if eng in self.AUTO and sem == self.AUTO[eng]:
            self.add(eng, fn)
            return
        h = self.sems[sem]
        self.cnt[sem] += n
        self.ops[eng].append(lambda e, fn=fn, h=h, n=n: fn(e).then_inc(h, n))

    def wait(self, eng, sem, val=None):
        h = self.sems[sem]
        v = self.cnt[sem] if val is None else val
        if v > 0:
            self.ops[eng].append(lambda e, h=h, v=v: e.wait_ge(h, v))

    def val(self, sem):
        return self.cnt[sem]

    def replay(self, block):
        ops = self.ops

        @block.vector
        def _(v):
            for f in ops["v"]:
                f(v)

        @block.scalar
        def _(s):
            for f in ops["s"]:
                f(s)

        @block.tensor
        def _(t):
            for f in ops["t"]:
                f(t)

        @block.gpsimd
        def _(g):
            for f in ops["g"]:
                f(g)

        @block.sync
        def _(y):
            for f in ops["y"]:
                f(y)


def emit_ln(p, x_ap, out_ap, B):
    bst, mv, rstd = B["bst"], B["mv"], B["rstd"]
    p.add("v", lambda v: v.bn_stats(bst[:, 0:6], x_ap[:, 0:512]))
    p.add("v", lambda v: v.bn_stats(bst[:, 6:12], x_ap[:, 512:1024]))
    p.add("v", lambda v: v.bn_aggr(mv[:], bst[:]))
    p.wait("s", "sv")
    p.add("s", lambda s: s.activation(rstd[:], mv[:, 1:2], AF.Ln, bias=B["epst"][:]))
    p.add("s", lambda s: s.activation(rstd[:], rstd[:], AF.Exp, scale=-0.5))
    p.wait("v", "ss")
    p.add("v", lambda v: v.tensor_scalar(out_ap, x_ap, mv[:, 0:1], rstd[:], ALU.subtract, ALU.mult))


def mm_group(p, fns):
    for i, fn in enumerate(fns):
        if i == len(fns) - 1:
            p.inc("t", fn, "st")
        else:
            p.add("t", fn)


def build_l1():
    nc = bass.Bass("TRN2", target_bir_lowering=False)
    x_h = nc.dram_tensor("x", [NB, T, C], F32, kind="ExternalInput").ap()
    wkT_h = nc.dram_tensor("wkT", [C, C], F32, kind="ExternalInput").ap()
    wvT_h = nc.dram_tensor("wvT", [C, C], F32, kind="ExternalInput").ap()
    wrT_h = nc.dram_tensor("wrT", [C, C], F32, kind="ExternalInput").ap()
    woT_h = nc.dram_tensor("woT", [C, C], F32, kind="ExternalInput").ap()
    wg_h = nc.dram_tensor("wg", [C, E], F32, kind="ExternalInput").ap()
    bg_h = nc.dram_tensor("bg", [1, E], F32, kind="ExternalInput").ap()
    td_h = nc.dram_tensor("tdec", [C], F32, kind="ExternalInput").ap()
    tf_h = nc.dram_tensor("tfirst", [C], F32, kind="ExternalInput").ap()
    tmk_h = nc.dram_tensor("tmk", [C], F32, kind="ExternalInput").ap()
    tmv_h = nc.dram_tensor("tmv", [C], F32, kind="ExternalInput").ap()
    tmr_h = nc.dram_tensor("tmr", [C], F32, kind="ExternalInput").ap()
    ident_h = nc.dram_tensor("ident", [128, 128], F32, kind="ExternalInput").ap()
    iota1_h = nc.dram_tensor("iota1", [128, 256], BF16, kind="ExternalInput").ap()

    x1_h = nc.dram_tensor("x1", [NB, T, C], F32, kind="ExternalOutput").ap()
    h2fm_h = nc.dram_tensor("h2fm", [NB, 128, J, T], BF16, kind="ExternalOutput").ap()
    msk_h = nc.dram_tensor("maskedsc", [NB, T, E], F32, kind="ExternalOutput").ap()

    p = Prog(nc)
    es = contextlib.ExitStack()
    with es:
        sb = lambda n, sh, dt=BF16: es.enter_context(nc.sbuf_tensor(n, sh, dt))
        ps_ = lambda n, sh, dt=F32: es.enter_context(nc.psum_tensor(n, sh, dt))

        wk = sb("wk", [128, J, C])
        wv = sb("wv", [128, J, C])
        wr = sb("wr", [128, J, C])
        wo = sb("wo", [128, J, C])
        wgs = sb("wgs", [128, J, E])
        bgrow = sb("bgrow", [1, E])
        ones_col = sb("ones_col", [1, 128])
        idb = sb("idb", [128, 128])
        tmks = sb("tmks", [128, J], F32)
        tmvs = sb("tmvs", [128, J], F32)
        tmrs = sb("tmrs", [128, J], F32)
        us = sb("us", [128, J], F32)
        wpos = sb("wpos", [128, J], F32)
        lamall = sb("lamall", [128, J], F32)
        lam256 = sb("lam256", [128, J, NCK], F32)
        ewj = sb("ewj", [128, 256], F32)
        wneg = sb("wneg", [128, J], F32)
        iota1 = sb("iota1s", [128, 256])

        xt = sb("xt", [128, C], F32)
        htile = sb("htile", [128, C])
        B = dict(bst=sb("bst", [128, 12], F32), mv=sb("mv", [128, 2], F32),
                 rstd=sb("rstd", [128, 1], F32), epst=sb("epst", [128, 1], F32))
        hfm = sb("hfm", [128, J, T + 1])
        blk = sb("blk", [128, J, TH])
        blv = sb("blv", [128, J, TH])
        blr = sb("blr", [128, J, TH])
        kpl = sb("kpl", [128, 2, TH])
        vpl = sb("vpl", [128, 2, TH])
        srpl = sb("srpl", [128, 2, TH])
        eks = sb("eks", [128, TH + 1])
        ekvs = sb("ekvs", [128, TH + 1])
        kap = sb("kap", [128, 1], F32)
        negk = sb("negk", [128, 1], F32)
        umk = sb("umk", [128, 1], F32)
        ksc = sb("ksc", [128, 1], F32)
        gpa = sb("gpa", [128, 1], F32)
        gpb = sb("gpb", [128, 1], F32)
        kap_all = sb("kap_all", [128, J], F32)
        gA_all = sb("gA_all", [128, J], F32)
        gB_all = sb("gB_all", [128, J], F32)
        ekl_all = sb("ekl_all", [128, J], F32)
        ekvl_all = sb("ekvl_all", [128, J], F32)
        EA = sb("EA", [128, TH])
        EB = sb("EB", [128, TH])
        bl_d = sb("bl_d", [128, 512]).ap()
        lastA = sb("lastA", [128, NCK], F32)
        lastB = sb("lastB", [128, NCK], F32)
        GA = sb("GA", [128, NCK], F32)
        GB = sb("GB", [128, NCK], F32)
        euk = sb("euk", [128, TH])
        rwkv = sb("rwkv", [128, J, TH])
        afm = sb("afm", [128, J, 512])
        atile = sb("atile", [128, C])
        x1t = sb("x1t", [128, C], F32)
        h2t = atile
        stag3 = htile.ap().rearrange("p (j c) -> p j c", j=J)
        scst = sb("scst", [128, E], F32)
        mx8 = sb("mx8", [128, 8], F32)
        nmx = sb("nmx", [128, 1], F32)
        ssum = sb("ssum", [128, 1], F32)
        rsum = sb("rsum", [128, 1], F32)
        maskt = sb("maskt", [128, E], F32)
        mskd = sb("mskd", [128, E], F32)

        ps_proj = ps_("ps_proj", [128, 1024])
        ps_tp = ps_("ps_tp", [128, 2048], BF16)
        ps_wo = ps_("ps_wo", [128, 512])
        ps_gate = ps_("ps_gate", [128, E])

        for s_ in ("sv", "ss", "st", "sg", "qx", "qxa", "qxb", "qw", "qo", "qh"):
            p.sem(es, s_)
        block = es.enter_context(nc.Block())

        for w_s, w_hh in ((wk, wkT_h), (wv, wvT_h), (wr, wrT_h), (wo, woT_h)):
            p.inc("g", lambda g, a=w_s, b_=w_hh: g.dma_start(
                a[:], b_.rearrange("(kc p) d -> p kc d", p=128)), "qw", 16)
        p.inc("g", lambda g: g.dma_start(wgs[:], wg_h.rearrange("(kc p) e -> p kc e", p=128)), "qw", 16)
        p.inc("g", lambda g: g.dma_start(bgrow[:], bg_h[:]), "qw", 16)
        p.inc("g", lambda g: g.dma_start(idb[:], ident_h[:]), "qw", 16)
        for v_s, v_hh in ((tmks, tmk_h), (tmvs, tmv_h), (tmrs, tmr_h), (us, tf_h), (wpos, td_h)):
            p.inc("y", lambda y, a=v_s, b_=v_hh: y.dma_start(
                a[:], b_.rearrange("(j p) -> p j", p=128), allow_slow_non_contiguous=True), "qx", 16)
        p.inc("y", lambda y: y.dma_start(iota1[:], iota1_h[:]), "qx", 16)
        p.wait("s", "qx")
        p.inc("s", lambda s: s.activation(wpos[:], wpos[:], AF.Exp), "ss")
        p.wait("v", "ss")
        p.add("v", lambda v: v.memset(ones_col[:], 0))
        p.add("v", lambda v: v.tensor_scalar(ones_col[:], ones_col[:], 0.0, 1.0, ALU.mult, ALU.add))
        p.add("v", lambda v: v.memset(B["epst"][:], float(LN_EPS)))
        for j in range(J):
            p.add("v", lambda v, j=j: v.memset(lam256[:, j, :], 0))
            p.inc("v", lambda v, j=j: v.tensor_scalar(
                lam256[:, j, :], lam256[:, j, :], wpos[:, j:j + 1], None, ALU.subtract), "sv")
        p.wait("s", "sv")
        for j in range(J):
            p.inc("s", lambda s, j=j: s.activation(lam256[:, j, :], lam256[:, j, :], AF.Exp, scale=256.0), "ss")
        p.inc("s", lambda s: s.activation(lamall[:], wpos[:], AF.Exp, scale=-1.0), "ss")
        p.inc("v", lambda v: v.tensor_scalar(
            wneg[:], wpos[:], -1.0, None, ALU.mult), "sv")
        wneg_mark = p.val("sv")
        p.wait("v", "ss")
        p.inc("v", lambda v: v.memset(hfm[:], 0), "sv")
        p.wait("t", "sv")
        p.wait("t", "qw")
        p.wait("y", "sv")

        bank_mark = {0: ("ss", 0), 1: ("ss", 0)}
        tp_mark = {0: 0, 1: 0}
        wo_mark = [0]
        afm_free = [0]
        wkv_marks = {}
        last_wkv_sg = [0]
        xbuf_free = {0: 0, 1: 0}

        def emit_ln1_tile(b, ti):
            par = ti % 2
            xbuf = xt if par == 0 else x1t
            qsem = "qxa" if par == 0 else "qxb"
            p.wait("y", "sv", xbuf_free[par])
            p.inc("y", lambda y, b=b, ti=ti, xbuf=xbuf: y.dma_start(
                xbuf[:], x_h[b, 128 * ti:128 * (ti + 1), :]), qsem, 16)
            p.wait("v", qsem)
            p.wait("v", "st", p.val("st"))
            p.wait("v", "qh", p.val("qh"))
            emit_ln(p, xbuf[:], htile[:], B)
            xbuf_free[par] = p.val("sv")
            tpo = 1024 * par
            p.wait("t", "sv", p.val("sv"))
            p.wait("t", "ss", tp_mark[par])
            mm_group(p, [
                (lambda t, j=j, tpo=tpo: t.transpose(
                    ps_tp[:, tpo + 128 * j:tpo + 128 * (j + 1)],
                    htile[:, 128 * j:128 * (j + 1)], idb[:]))
                for j in range(J)])
            p.wait("s", "st")
            p.inc("s", lambda s, ti=ti, tpo=tpo: s.copy(
                hfm.ap()[:, :, 1 + 128 * ti:1 + 128 * (ti + 1)],
                ps_tp[:, tpo:tpo + 1024].rearrange("p (j c) -> p j c", j=J)), "ss")
            tp_mark[par] = p.val("ss")

        def emit_blends(b, h):
            p.wait("v", "st", p.val("st"))
            t0 = h * TH
            for tt in range(2):
                ts0 = t0 + 512 * tt
                osl = slice(512 * tt, 512 * (tt + 1))
                for j in range(J):
                    h_sl = hfm[:, j, 1 + ts0:1 + ts0 + 512]
                    hs_sl = hfm[:, j, ts0:ts0 + 512]
                    p.add("v", lambda v, h_sl=h_sl, hs_sl=hs_sl: v.tensor_tensor(
                        bl_d, h_sl, hs_sl, ALU.subtract))
                    p.add("v", lambda v, j=j, osl=osl, hs_sl=hs_sl: v.scalar_tensor_tensor(
                        blk[:, j, osl], bl_d, tmks[:, j:j + 1], hs_sl, ALU.mult, ALU.add))
                    p.add("v", lambda v, j=j, osl=osl, hs_sl=hs_sl: v.scalar_tensor_tensor(
                        blv[:, j, osl], bl_d, tmvs[:, j:j + 1], hs_sl, ALU.mult, ALU.add), dep=False)
                    p.add("v", lambda v, j=j, osl=osl, hs_sl=hs_sl: v.scalar_tensor_tensor(
                        blr[:, j, osl], bl_d, tmrs[:, j:j + 1], hs_sl, ALU.mult, ALU.add), dep=False)
            return p.val("sv")

        def emit_proj(step, bl_ready):
            h, j = divmod(step, J)
            par = step % 2
            pm = wkv_marks.get(step - 2)
            first = True
            for pi, (wmat, bl, dst, actf) in enumerate((
                    (wk, blk, kpl, AF.Copy), (wv, blv, vpl, AF.Copy),
                    (wr, blr, srpl, AF.Sigmoid))):
                for tt in range(2):
                    bank = (pi * 2 + tt) % 2
                    po = 512 * bank
                    p.wait("t", bank_mark[bank][0], bank_mark[bank][1])
                    if first:
                        p.wait("t", "sv", bl_ready)
                        if pm:
                            p.wait("t", "sv", pm["sg"])
                            p.wait("t", "ss", pm["ss"])
                        first = False
                    mm_group(p, [
                        (lambda t, wmat=wmat, bl=bl, kc=kc, j=j, po=po, tt=tt: t.matmul(
                            ps_proj[:, po:po + 512],
                            wmat[:, kc, 128 * j:128 * (j + 1)],
                            bl[:, kc, 512 * tt:512 * (tt + 1)],
                            start=(kc == 0), stop=(kc == J - 1)))
                        for kc in range(J)])
                    if actf is AF.Copy:
                        p.wait("v", "st")
                        p.inc("v", lambda v, dst=dst, par=par, po=po, tt=tt: v.tensor_copy(
                            dst[:, par, 512 * tt:512 * (tt + 1)], ps_proj[:, po:po + 512]), "sv")
                        bank_mark[bank] = ("sv", p.val("sv"))
                    else:
                        p.wait("s", "st")
                        p.inc("s", lambda s, dst=dst, par=par, po=po, actf=actf, tt=tt: s.activation(
                            dst[:, par, 512 * tt:512 * (tt + 1)], ps_proj[:, po:po + 512], actf), "ss")
                        bank_mark[bank] = ("ss", p.val("ss"))

        def emit_wkv(step, rwkv_free=0):
            h, j = divmod(step, J)
            par = step % 2
            hf = h
            kp = kpl[:, par, :]
            vp = vpl[:, par, :]
            srp = srpl[:, par, :]
            p.wait("s", "sv", max(last_wkv_sg[0], wneg_mark))
            p.add("s", lambda s, j=j: s.activation(
                ewj[:], iota1[:], AF.Exp, scale=wneg[:, j:j + 1]))
            p.wait("v", bank_mark[0][0], bank_mark[0][1])
            p.wait("v", bank_mark[1][0], bank_mark[1][1])
            p.add("v", lambda v, kp=kp: v.tensor_reduce(kap[:], kp, axis=AX.X, op=ALU.max))
            if hf == 1:
                p.add("v", lambda v, j=j: v.tensor_tensor(
                    ksc[:], kap_all[:, j:j + 1], kap[:], ALU.subtract))
            p.add("v", lambda v: v.tensor_scalar(negk[:], kap[:], -1.0, None, ALU.mult))
            p.add("v", lambda v, j=j: v.tensor_tensor(umk[:], us[:, j:j + 1], kap[:], ALU.subtract))
            p.add("v", lambda v, j=j: v.tensor_copy(kap_all[:, j:j + 1], kap[:]))
            p.wait("s", "sv")
            if hf == 1:
                p.add("s", lambda s: s.activation(ksc[:], ksc[:], AF.Exp))
            p.add("s", lambda s, kp=kp: s.activation(eks[:, 1:], kp, AF.Exp, bias=negk[:]))
            p.add("s", lambda s, kp=kp: s.activation(euk[:], kp, AF.Exp, bias=umk[:]))
            p.wait("v", "ss")
            p.add("v", lambda v, vp=vp: v.tensor_tensor(ekvs[:, 1:], eks[:, 1:], vp, ALU.mult))
            if hf == 0:
                p.add("v", lambda v: v.memset(eks[:, 0:1], 0))
                p.add("v", lambda v: v.memset(ekvs[:, 0:1], 0))
                p.add("v", lambda v: v.memset(gpa[:], 0))
                p.add("v", lambda v: v.memset(gpb[:], 0))
            else:
                p.add("v", lambda v, j=j: v.tensor_tensor(
                    eks[:, 0:1], ekl_all[:, j:j + 1], ksc[:], ALU.mult))
                p.add("v", lambda v, j=j: v.tensor_tensor(
                    ekvs[:, 0:1], ekvl_all[:, j:j + 1], ksc[:], ALU.mult))
                p.add("v", lambda v, j=j: v.tensor_tensor(
                    gpa[:], gA_all[:, j:j + 1], ksc[:], ALU.mult))
                p.add("v", lambda v, j=j: v.tensor_tensor(
                    gpb[:], gB_all[:, j:j + 1], ksc[:], ALU.mult))
            p.add("v", lambda v, j=j: v.tensor_copy(ekl_all[:, j:j + 1], eks[:, TH:TH + 1]))
            p.add("v", lambda v, j=j: v.tensor_copy(ekvl_all[:, j:j + 1], ekvs[:, TH:TH + 1]))
            lamb = lamall.ap()[:, j:j + 1].broadcast_to((128, 256))
            for c in range(NCK):
                sl = slice(256 * c, 256 * (c + 1))
                p.add("v", lambda v, sl=sl, lamb=lamb: v.tensor_tensor_scan(
                    EB[:, sl], lamb, eks[:, sl], 0.0, ALU.mult, ALU.add), dep=(c == 0))
            for c in range(NCK):
                sl = slice(256 * c, 256 * (c + 1))
                p.add("v", lambda v, sl=sl, lamb=lamb: v.tensor_tensor_scan(
                    EA[:, sl], lamb, ekvs[:, sl], 0.0, ALU.mult, ALU.add), dep=False)
            EBr = EB.ap().rearrange("p (c l) -> p c l", l=256)
            EAr = EA.ap().rearrange("p (c l) -> p c l", l=256)
            p.add("v", lambda v, EBr=EBr: v.tensor_copy(lastB[:], EBr[:, :, 255:256]))
            p.add("v", lambda v, EAr=EAr: v.tensor_copy(lastA[:], EAr[:, :, 255:256]))
            p.add("v", lambda v, j=j: v.scalar_tensor_tensor(
                lastB[:, 0:1], lam256[:, j, 0:1], gpb[:], lastB[:, 0:1], ALU.mult, ALU.add))
            p.add("v", lambda v, j=j: v.scalar_tensor_tensor(
                lastA[:, 0:1], lam256[:, j, 0:1], gpa[:], lastA[:, 0:1], ALU.mult, ALU.add))
            p.add("v", lambda v, j=j: v.tensor_tensor_scan(
                GB[:], lam256[:, j, :], lastB[:], 0.0, ALU.mult, ALU.add))
            p.add("v", lambda v, j=j: v.tensor_tensor_scan(
                GA[:], lam256[:, j, :], lastA[:], 0.0, ALU.mult, ALU.add))
            p.add("v", lambda v, j=j: v.tensor_copy(gB_all[:, j:j + 1], GB[:, NCK - 1:NCK]))
            p.add("v", lambda v, j=j: v.tensor_copy(gA_all[:, j:j + 1], GA[:, NCK - 1:NCK]))
            for c in range(NCK):
                sl = slice(256 * c, 256 * (c + 1))
                p.add("v", lambda v, sl=sl, c=c: v.scalar_tensor_tensor(
                    EB[:, sl], ewj[:],
                    (gpb[:] if c == 0 else GB[:, c - 1:c]), EB[:, sl], ALU.mult, ALU.add), dep=(c == 0))
            p.add("v", lambda v: v.tensor_tensor(EB[:], EB[:], euk[:], ALU.add))
            p.wait("s", "sv")
            p.add("s", lambda s: s.activation(EB[:], EB[:], AF.Ln))
            p.add("s", lambda s: s.activation(EB[:], EB[:], AF.Exp, scale=-1.0))
            for c in range(NCK):
                sl = slice(256 * c, 256 * (c + 1))
                p.add("v", lambda v, sl=sl, c=c: v.scalar_tensor_tensor(
                    EA[:, sl], ewj[:],
                    (gpa[:] if c == 0 else GA[:, c - 1:c]), EA[:, sl], ALU.mult, ALU.add), dep=(c == 0))
            p.add("v", lambda v, vp=vp: v.tensor_tensor(euk[:], euk[:], vp, ALU.mult))
            p.add("v", lambda v: v.tensor_tensor(EA[:], EA[:], euk[:], ALU.add))
            p.wait("v", "ss")
            p.add("v", lambda v: v.tensor_tensor(EA[:], EA[:], EB[:], ALU.mult))
            p.wait("v", "st", rwkv_free)
            p.add("v", lambda v, j=j, srp=srp: v.tensor_tensor(
                rwkv[:, j, :], EA[:], srp, ALU.mult))
            wkv_marks[step] = dict(sv=p.val("sv"), ss=p.val("ss"), sg=p.val("sv"))
            last_wkv_sg[0] = p.val("sv")

        def emit_woq(b, h, qq):
            p.wait("t", "sv", wkv_marks[h * J + J - 1]["sg"])
            for m in range(J):
                p.wait("t", "ss", wo_mark[0])
                mm_group(p, [
                    (lambda t, kc=kc, m=m, qq=qq: t.matmul(
                        ps_wo[:], wo[:, kc, 128 * m:128 * (m + 1)],
                        rwkv[:, kc, 512 * qq:512 * (qq + 1)],
                        start=(kc == 0), stop=(kc == J - 1)))
                    for kc in range(J)])
                p.wait("s", "st")
                p.wait("s", "st", afm_free[0])
                p.inc("s", lambda s, m=m: s.copy(afm[:, m, :], ps_wo[:]), "ss")
                wo_mark[0] = p.val("ss")
            return p.val("st")

        def emit_epi_tile(b, h, qq, ti4):
            ti = h * 8 + qq * 4 + ti4
            par = ti4 % 2
            tpo = 1024 * par
            p.wait("t", "ss", wo_mark[0])
            p.wait("t", "ss", tp_mark[par])
            mm_group(p, [
                (lambda t, j=j, ti4=ti4, tpo=tpo: t.transpose(
                    ps_tp[:, tpo + 128 * j:tpo + 128 * (j + 1)],
                    afm[:, j, 128 * ti4:128 * (ti4 + 1)], idb[:]))
                for j in range(J)])
            afm_free[0] = p.val("st")
            p.wait("s", "st")
            p.inc("s", lambda s, tpo=tpo: s.copy(atile[:], ps_tp[:, tpo:tpo + 1024]), "ss")
            tp_mark[par] = p.val("ss")
            atile_ss = p.val("ss")
            p.wait("y", "sv", xbuf_free[0])
            p.inc("y", lambda y, b=b, ti=ti: y.dma_start(
                xt[:], x_h[b, 128 * ti:128 * (ti + 1), :]), "qxa", 16)
            p.wait("v", "qxa")
            p.wait("v", "ss", atile_ss)
            p.wait("v", "qo", p.val("qo"))
            p.add("v", lambda v: v.tensor_tensor(x1t[:], xt[:], atile[:], ALU.add))
            xbuf_free[0] = p.val("sv")
            p.wait("y", "sv")
            p.inc("y", lambda y, b=b, ti=ti: y.dma_start(
                x1_h[b, 128 * ti:128 * (ti + 1), :], x1t[:]), "qo", 16)
            p.wait("v", "st", p.val("st"))
            p.wait("v", "qh", p.val("qh"))
            emit_ln(p, x1t[:], h2t[:], B)
            h2t_sv = p.val("sv")
            tpo2 = 1024 * ((ti4 + 1) % 2)
            p.wait("t", "sv", h2t_sv)
            p.wait("t", "ss", tp_mark[(ti4 + 1) % 2])
            mm_group(p, [
                (lambda t, j=j, tpo2=tpo2: t.transpose(
                    ps_tp[:, tpo2 + 128 * j:tpo2 + 128 * (j + 1)],
                    h2t[:, 128 * j:128 * (j + 1)], idb[:]))
                for j in range(J)])
            p.wait("s", "st")
            p.wait("s", "qh", p.val("qh"))
            p.inc("s", lambda s, tpo2=tpo2: s.copy(
                htile.ap(), ps_tp[:, tpo2:tpo2 + 1024]), "ss")
            tp_mark[(ti4 + 1) % 2] = p.val("ss")
            stag_ss = p.val("ss")
            p.wait("y", "ss", stag_ss)
            p.inc("y", lambda y, b=b, ti=ti: y.dma_start(
                h2fm_h[b, :, :, 128 * ti:128 * (ti + 1)], stag3), "qh", 16)
            p.wait("t", "ss", stag_ss)
            fns = [(lambda t, j=j: t.matmul(
                ps_gate[:], stag3[:, j, :], wgs[:, j, :], start=(j == 0), stop=False))
                for j in range(J)]
            fns.append(lambda t: t.matmul(ps_gate[:], ones_col[:], bgrow[:],
                                          start=False, stop=True))
            mm_group(p, fns)
            p.wait("v", "st")
            p.add("v", lambda v: v.max(mx8[:], ps_gate[:]))
            p.add("v", lambda v: v.tensor_scalar(nmx[:], mx8[:, 0:1], -1.0, None, ALU.mult))
            p.wait("s", "sv")
            p.add("s", lambda s: s.activation(scst[:], ps_gate[:], AF.Exp, bias=nmx[:]))
            p.wait("v", "ss")
            p.add("v", lambda v: v.reduce_sum(ssum[:], scst[:], axis=AX.X))
            p.add("v", lambda v: v.reciprocal(rsum[:], ssum[:]))
            p.add("v", lambda v: v.tensor_scalar(scst[:], scst[:], rsum[:], None, ALU.mult))
            p.add("v", lambda v: v.max(mx8[:], scst[:]))
            p.add("v", lambda v: v.tensor_scalar(
                maskt[:], scst[:], mx8[:, 1:2], None, ALU.is_ge))
            p.wait("v", "qo", p.val("qo"))
            p.add("v", lambda v: v.tensor_tensor(mskd[:], scst[:], maskt[:], ALU.mult))
            p.wait("y", "sv")
            p.inc("y", lambda y, b=b, ti=ti: y.dma_start(
                msk_h[b, 128 * ti:128 * (ti + 1), :], mskd[:]), "qo", 16)

        rwkv_free_st = [0]
        for b in range(NB):
            for ti in range(NTT):
                emit_ln1_tile(b, ti)
            bl0 = emit_blends(b, 0)
            emit_proj(0, bl0)
            for j in range(1, 8):
                emit_wkv(j - 1, rwkv_free_st[0])
                emit_proj(j, bl0)
            bl1 = emit_blends(b, 1)
            emit_wkv(7, rwkv_free_st[0])
            emit_proj(8, bl1)
            emit_woq(b, 0, 0)
            emit_proj(9, bl1)
            for t4 in range(4):
                emit_epi_tile(b, 0, 0, t4)
            wo_st = emit_woq(b, 0, 1)
            rwkv_free_st[0] = wo_st
            emit_wkv(8, rwkv_free_st[0])
            emit_proj(10, bl1)
            emit_epi_tile(b, 0, 1, 0)
            emit_epi_tile(b, 0, 1, 1)
            emit_wkv(9, rwkv_free_st[0])
            emit_proj(11, bl1)
            emit_epi_tile(b, 0, 1, 2)
            emit_epi_tile(b, 0, 1, 3)
            emit_wkv(10, rwkv_free_st[0])
            emit_wkv(11, rwkv_free_st[0])
            emit_proj(12, bl1)
            emit_wkv(12, rwkv_free_st[0])
            emit_proj(13, bl1)
            emit_wkv(13, rwkv_free_st[0])
            emit_proj(14, bl1)
            emit_wkv(14, rwkv_free_st[0])
            emit_proj(15, bl1)
            emit_wkv(15, rwkv_free_st[0])
            emit_woq(b, 1, 0)
            for t4 in range(4):
                emit_epi_tile(b, 1, 0, t4)
            wo_st = emit_woq(b, 1, 1)
            rwkv_free_st[0] = wo_st
            for t4 in range(4):
                emit_epi_tile(b, 1, 1, t4)
            wkv_marks.clear()

        p.wait("y", "qo")
        p.wait("y", "qh")
        p.replay(block)
    return nc


def build_l2():
    nc = bass.Bass("TRN2", target_bir_lowering=False)
    x1_h = nc.dram_tensor("x1", [NB, T, C], F32, kind="ExternalInput").ap()
    h2fm_h = nc.dram_tensor("h2fm", [NB, 128, J, T], BF16, kind="ExternalInput").ap()
    mall_h = nc.dram_tensor("mall", [32, T, E], F32, kind="ExternalInput").ap()
    mymsk_h = nc.dram_tensor("mymsk", [NB, T, E], F32, kind="ExternalInput").ap()
    w1_h = nc.dram_tensor("w1", [E, C, FP], F32, kind="ExternalInput").ap()
    w2_h = nc.dram_tensor("w2", [E, FP, C], F32, kind="ExternalInput").ap()
    b1_h = nc.dram_tensor("b1", [128, 2, E], F32, kind="ExternalInput").ap()
    b2_h = nc.dram_tensor("b2", [E, C], F32, kind="ExternalInput").ap()
    gam_h = nc.dram_tensor("gam", [1, C], F32, kind="ExternalInput").ap()
    bet_h = nc.dram_tensor("bet", [1, C], F32, kind="ExternalInput").ap()
    ident_h = nc.dram_tensor("ident", [128, 128], F32, kind="ExternalInput").ap()
    eyee_h = nc.dram_tensor("eyee", [16, 16, 128], BF16, kind="ExternalInput").ap()

    out_h = nc.dram_tensor("out", [NB, T, C], F32, kind="ExternalOutput").ap()
    zp_h = nc.dram_tensor("zpart", [NB, 128, J, T], F32, kind="Internal").ap()
    zfm_h = nc.dram_tensor("zfm", [NB, 128, J, T], BF16, kind="Internal").ap()

    p = Prog(nc)
    es = contextlib.ExitStack()
    with es:
        sb = lambda n, sh, dt=BF16: es.enter_context(nc.sbuf_tensor(n, sh, dt))
        ps_ = lambda n, sh, dt=F32: es.enter_context(nc.psum_tensor(n, sh, dt))

        idb = sb("idb", [128, 128])
        idf = sb("idf", [128, 128], F32)
        ones_col = sb("ones_col", [1, 128])
        gamt = sb("gamt", [128, C])
        bett = sb("bett", [128, C])
        gbrow = sb("gbrow", [1, C])
        b2s = sb("b2s", [16, C])
        b1s = sb("b1s", [128, 2, E], F32)
        masksb = sb("masksb", [128, 512])
        selbs = [sb(f"selb{b}", [16, T]) for b in range(NB)]
        eyes = sb("eyes", [16, 16, 128])
        zload0 = sb("zload0", [128, C])
        zload1 = sb("zload1", [128, C])
        znorm = sb("znorm", [128, C])
        x1t0 = sb("x1t0", [128, C], F32)
        x1t1 = sb("x1t1", [128, C], F32)
        o10 = sb("o10", [128, C], F32)
        o11 = sb("o11", [128, C], F32)
        t2 = sb("t2", [128, C])
        ztile = sb("ztile", [128, 512], F32)
        ztmp = sb("ztmp", [128, 512])
        B = dict(bst=sb("bst", [128, 12], F32), mv=sb("mv", [128, 2], F32),
                 rstd=sb("rstd", [128, 1], F32), epst=sb("epst", [128, 1], F32))
        h2s = sb("h2s", [128, J, T])
        Hm = sb("Hm", [128, 8, 2, 512])
        w1s = sb("w1s", [128, 8, J, FP])
        w2s = sb("w2s", [128, 8, 2, C])
        es2 = contextlib.ExitStack()
        sb2 = lambda n, sh, dt=BF16: es2.enter_context(nc.sbuf_tensor(n, sh, dt))
        mt32 = sb2("mt32", [128, 32, E], F32)
        dsum = sb2("dsum", [128, E], F32)
        rdt = sb2("rdt", [128, NTT, E], F32)
        routf = sb2("routf", [128, NTT, E], F32)
        rem = sb2("rem", [64, T], F32)
        rstage = sb2("rstage", [16, T], F32)
        nnz = sb2("nnz", [64, 1], F32)
        mxr = sb2("mxr", [64, 1], F32)
        tlo = sb2("tlo", [64, 1], F32)
        thi = sb2("thi", [64, 1], F32)
        tmid = sb2("tmid", [64, 1], F32)
        thet = sb2("thet", [64, 1], F32)
        cntb = sb2("cntb", [64, 1], F32)
        geb = sb2("geb", [64, 1], F32)
        cdl = sb2("cdl", [64, 1], F32)
        cdh = sb2("cdh", [64, 1], F32)
        selA = sb2("selA", [64, T])
        junk2 = selA
        zind = sb2("zind", [64, T])
        zrank = sb2("zrank", [64, T], mybir.dt.float16)
        zlast = sb2("zlast", [64, J], F32)
        zG = sb2("zG", [64, J], F32)
        onesT = sb2("onesT", [64, 256], F32)
        ones8 = sb2("ones8", [64, J], F32)
        Ke = sb2("Ke", [64, 1], F32)
        Fb = sb2("Fb", [64, 1], F32)
        Fc = sb2("Fc", [64, 1], F32)
        selB = sb2("selB", [64, T])

        ps_h = ps_("ps_h", [128, 2048])
        ps_o = ps_("ps_o", [128, 1024])
        ps_x = ps_("ps_x", [128, 512])
        ps_t = ps_("ps_t", [128, 1024], BF16)

        for s_ in ("sv", "ss", "st", "sg", "qx", "qxa", "qxb", "qw", "qb", "qo", "qo1", "qz"):
            p.sem(es, s_)
        block = es.enter_context(nc.Block())

        p.inc("g", lambda g: g.dma_start(idb[:], ident_h[:]), "qw", 16)
        p.inc("y", lambda y: y.dma_start(idf[:], ident_h[:]), "qx", 16)
        p.inc("g", lambda g: g.dma_start(b2s[:], b2_h[:]), "qw", 16)
        p.inc("y", lambda y: y.dma_start(eyes[:], eyee_h[:]), "qx", 16)
        p.inc("g", lambda g: g.dma_start(b1s[:], b1_h[:]), "qb", 16)
        p.wait("v", "qb")
        p.inc("v", lambda v: v.tensor_scalar(
            b1s.ap().rearrange("p a b -> p (a b)"), b1s.ap().rearrange("p a b -> p (a b)"),
            -256.0, None, ALU.add), "sv")
        p.wait("v", "qx")
        p.add("v", lambda v: v.memset(ones_col[:], 0))
        p.add("v", lambda v: v.tensor_scalar(ones_col[:], ones_col[:], 0.0, 1.0, ALU.mult, ALU.add))
        p.add("v", lambda v: v.memset(onesT[:], 0))
        p.add("v", lambda v: v.tensor_scalar(onesT[:], onesT[:], 0.0, 1.0, ALU.mult, ALU.add))
        p.add("v", lambda v: v.memset(B["epst"][:], float(LN_EPS)))
        p.add("v", lambda v: v.memset(ones8[:], 0))
        p.inc("v", lambda v: v.tensor_scalar(ones8[:], ones8[:], 0.0, 1.0, ALU.mult, ALU.add), "sv")
        p.wait("t", "sv", p.val("sv"))
        for src_h, dst, dty in ((gam_h, gamt, "bf"), (bet_h, bett, "f32")):
            p.inc("g", lambda g, src_h=src_h: g.dma_start(gbrow[:], src_h[:]), "qw", 16)
            p.wait("t", "qw")
            for half in range(2):
                p.wait("t", "ss", p.val("ss"))
                p.inc("t", lambda t, half=half: t.matmul(
                    ps_x[:], ones_col[:], gbrow[:, 512 * half:512 * (half + 1)],
                    start=True, stop=True), "st")
                p.wait("s", "st")
                p.inc("s", lambda s, dst=dst, half=half: s.copy(
                    dst[:, 512 * half:512 * (half + 1)], ps_x[:]), "ss")
            p.wait("g", "ss")

        for ee in range(8):
            p.inc("g", lambda g, ee=ee: g.dma_start(
                w1s[:, ee, :, :], w1_h[ee].rearrange("(kc p) f -> p kc f", p=128)), "qw", 16)
            p.inc("g", lambda g, ee=ee: g.dma_start(
                w2s[:, ee, :, :], w2_h[ee].rearrange("(fc p) d -> p fc d", p=128)), "qw", 16)
        for ti in range(NTT):
            p.wait("y", "sv", p.val("sv"))
            p.inc("y", lambda y, ti=ti: y.dma_start(
                mt32[:], mall_h[:, 128 * ti:128 * (ti + 1), :].rearrange("b p e -> p b e")), "qx", 16)
            p.wait("v", "qx")
            for lvl in (16, 8, 4, 2, 1):
                p.add("v", lambda v, lvl=lvl: v.tensor_tensor(
                    mt32.ap()[:, 0:lvl, :].rearrange("p b e -> p (b e)"),
                    mt32.ap()[:, 0:lvl, :].rearrange("p b e -> p (b e)"),
                    mt32.ap()[:, lvl:2 * lvl, :].rearrange("p b e -> p (b e)"), ALU.add))
            p.add("v", lambda v: v.tensor_scalar(dsum[:], mt32[:, 0, :], EPS_GATE, None, ALU.add))
            p.add("v", lambda v, ti=ti: v.reciprocal(rdt[:, ti, :], dsum[:]))
        for b in range(NB):
            p.wait("y", "sv", p.val("sv"))
            p.wait("y", "st", p.val("st"))
            p.inc("y", lambda y, b=b: y.dma_start(
                routf[:], mymsk_h[b].rearrange("(ti p) e -> p ti e", p=128)), "qx", 16)
            p.wait("v", "qx")
            for ti in range(NTT):
                p.add("v", lambda v, ti=ti: v.scalar_tensor_tensor(
                    routf[:, ti, :], routf[:, ti, :], 32.0, rdt[:, ti, :],
                    ALU.mult, ALU.mult), dep=(ti == 0))
            p.wait("t", "sv")
            for g8 in range(2):
                p.wait("t", "ss", p.val("ss"))
                mm_group(p, [
                    (lambda t, ti=ti, k=k: t.transpose(
                        ps_h.ap()[0:16, 128 * k:128 * (k + 1)], routf[:, ti, :], idf[:]))
                    for k, ti in enumerate(range(8 * g8, 8 * g8 + 8))])
                p.wait("s", "st")
                p.inc("s", lambda s, g8=g8: s.copy(
                    rstage[:, 1024 * g8:1024 * (g8 + 1)], ps_h.ap()[0:16, 0:1024]), "ss")
            p.wait("y", "ss")
            p.inc("y", lambda y, b=b: y.dma_start(rem[16 * b:16 * (b + 1), :], rstage[:]), "qx", 16)
        p.wait("v", "qx")
        p.wait("v", "ss")
        p.inc("v", lambda v: v.tensor_scalar(
            junk2[:], rem[:], 0.0, None, ALU.is_gt, ALU.add, accum_out=nnz[:]), "sv")
        p.add("v", lambda v: v.tensor_reduce(mxr[:], rem[:], axis=AX.X, op=ALU.max))
        p.wait("v", "sv")
        p.inc("v", lambda v: v.tensor_scalar(mxr[:], mxr[:], 1e-30, None, ALU.max), "sv")
        p.wait("s", "sv")
        p.inc("s", lambda s: s.activation(thi[:], mxr[:], AF.Ln), "ss")
        p.wait("v", "ss")
        p.add("v", lambda v: v.tensor_scalar(thi[:], thi[:], 0.01, None, ALU.add))
        p.add("v", lambda v: v.memset(tlo[:], 0))
        p.inc("v", lambda v: v.tensor_scalar(tlo[:], tlo[:], 0.0, -110.0, ALU.mult, ALU.add), "sv")
        for it in range(44):
            p.wait("v", "sv", p.val("sv"))
            p.add("v", lambda v: v.tensor_tensor(tmid[:], tlo[:], thi[:], ALU.add))
            p.inc("v", lambda v: v.tensor_scalar(tmid[:], tmid[:], 0.5, None, ALU.mult), "sv")
            p.wait("s", "sv")
            p.inc("s", lambda s: s.activation(thet[:], tmid[:], AF.Exp), "ss")
            p.wait("v", "ss")
            p.inc("v", lambda v: v.tensor_scalar(
                junk2[:], rem[:], thet[:], None, ALU.is_gt, ALU.add, accum_out=cntb[:]), "sv")
            p.wait("v", "sv")
            p.inc("v", lambda v: v.tensor_scalar(geb[:], cntb[:], 1024.0, None, ALU.is_ge), "sv")
            p.wait("v", "sv")
            p.add("v", lambda v: v.tensor_tensor(cdl[:], tmid[:], tlo[:], ALU.subtract))
            p.add("v", lambda v: v.scalar_tensor_tensor(tlo[:], cdl[:], geb[:], tlo[:], ALU.mult, ALU.add))
            p.add("v", lambda v: v.tensor_tensor(cdh[:], thi[:], tmid[:], ALU.subtract))
            p.inc("v", lambda v: v.scalar_tensor_tensor(thi[:], cdh[:], geb[:], tmid[:], ALU.mult, ALU.add), "sv")
        p.wait("s", "sv")
        p.inc("s", lambda s: s.activation(thet[:], tlo[:], AF.Exp), "ss")
        p.wait("v", "ss")
        p.inc("v", lambda v: v.tensor_scalar(selA[:], rem[:], thet[:], None, ALU.is_gt), "sv")
        p.inc("v", lambda v: v.tensor_scalar(zind[:], rem[:], 0.0, None, ALU.is_le), "sv")
        p.wait("v", "sv")
        for c in range(8):
            sl = slice(256 * c, 256 * (c + 1))
            fn = lambda v, sl=sl: v.tensor_tensor_scan(
                zrank[:, sl], onesT[:], zind[:, sl], 0.0, ALU.mult, ALU.add)
            if c == 7:
                p.inc("v", fn, "sv")
            else:
                p.add("v", fn)
        p.wait("v", "sv")
        zrr = zrank.ap().rearrange("p (c l) -> p c l", l=256)
        p.inc("v", lambda v, zrr=zrr: v.tensor_copy(zlast[:], zrr[:, :, 255:256]), "sv")
        p.wait("v", "sv")
        p.inc("v", lambda v: v.tensor_tensor_scan(zG[:], ones8[:], zlast[:], 0.0, ALU.mult, ALU.add), "sv")
        p.wait("v", "sv")
        for c in range(1, 8):
            sl = slice(256 * c, 256 * (c + 1))
            p.add("v", lambda v, sl=sl, c=c: v.tensor_scalar(
                zrank[:, sl], zrank[:, sl], zG[:, c - 1:c], None, ALU.add))
        p.add("v", lambda v: v.tensor_scalar(Ke[:], nnz[:], -1.0, 1024.0, ALU.mult, ALU.add))
        p.inc("v", lambda v: v.tensor_scalar(Fb[:], nnz[:], 1024.0, None, ALU.is_ge), "sv")
        p.wait("v", "sv")
        p.add("v", lambda v: v.tensor_scalar(Fc[:], Fb[:], -1.0, 1.0, ALU.mult, ALU.add))
        p.inc("v", lambda v: v.tensor_scalar(selB[:], zrank[:], Ke[:], None, ALU.is_le), "sv")
        p.wait("v", "sv")
        p.inc("v", lambda v: v.tensor_tensor(selB[:], selB[:], zind[:], ALU.mult), "sv")
        p.inc("v", lambda v: v.tensor_scalar(zind[:], zind[:], -1.0, 1.0, ALU.mult, ALU.add), "sv")
        p.inc("v", lambda v: v.tensor_tensor(selB[:], selB[:], zind[:], ALU.add), "sv")
        p.wait("v", "sv")
        p.add("v", lambda v: v.tensor_scalar(selA[:], selA[:], Fb[:], None, ALU.mult))
        p.inc("v", lambda v: v.tensor_scalar(selB[:], selB[:], Fc[:], None, ALU.mult), "sv")
        p.wait("v", "sv")
        p.inc("v", lambda v: v.tensor_tensor(selA[:], selA[:], selB[:], ALU.add), "sv")
        p.wait("y", "sv")
        for b in range(NB):
            p.inc("y", lambda y, b=b: y.dma_start(selbs[b][:], selA[16 * b:16 * (b + 1), :]), "qx", 16)

        ln3_marks = {"zl": {0: 0, 1: 0}, "x1": {0: 0, 1: 0}, "o1": {0: 0, 1: 0}, "t2": 0}

        def emit_ln3(b, tis):
            p.wait("y", "qz", p.val("qz"))
            for ti in tis:
                par = ti % 2
                zl = zload0 if par == 0 else zload1
                x1b = x1t0 if par == 0 else x1t1
                ob = o10 if par == 0 else o11
                qz2 = "qxa" if par == 0 else "qxb"
                p.wait("y", "st", ln3_marks["zl"][par])
                p.inc("y", lambda y, b=b, ti=ti, zl=zl: y.dma_start(
                    zl.ap().rearrange("p (j c) -> p j c", j=J),
                    zfm_h[b, :, :, 128 * ti:128 * (ti + 1)]), qz2, 16)
                p.wait("y", "sv", ln3_marks["x1"][par])
                p.inc("y", lambda y, b=b, ti=ti, x1b=x1b: y.dma_start(
                    x1b[:], x1_h[b, 128 * ti:128 * (ti + 1), :]), qz2, 16)
                p.wait("t", qz2)
                p.wait("t", "ss", p.val("ss"))
                mm_group(p, [
                    (lambda t, j=j, zl=zl: t.transpose(
                        ps_t[:, 128 * j:128 * (j + 1)], zl[:, 128 * j:128 * (j + 1)], idb[:]))
                    for j in range(J)])
                ln3_marks["zl"][par] = p.val("st")
                p.wait("s", "st")
                p.wait("s", "sv", ln3_marks["t2"])
                p.inc("s", lambda s: s.copy(t2[:], ps_t[:]), "ss")
                p.wait("v", "ss")
                emit_ln(p, t2[:], znorm[:], B)
                ln3_marks["t2"] = p.val("sv")
                p.wait("v", qz2)
                p.wait("v", "qo" if par == 0 else "qo1", ln3_marks["o1"][par])
                p.add("v", lambda v, x1b=x1b, ob=ob: v.tensor_tensor(ob[:], x1b[:], bett[:], ALU.add))
                p.add("v", lambda v: v.tensor_tensor(znorm[:], znorm[:], gamt[:], ALU.mult))
                p.add("v", lambda v, ob=ob: v.tensor_tensor(ob[:], ob[:], znorm[:], ALU.add))
                ln3_marks["x1"][par] = p.val("sv")
                p.wait("y", "sv")
                p.inc("y", lambda y, b=b, ti=ti, ob=ob: y.dma_start(
                    out_h[b, 128 * ti:128 * (ti + 1), :], ob[:]), "qo" if par == 0 else "qo1", 16)
                ln3_marks["o1"][par] = p.val("qo" if par == 0 else "qo1")

        es2.close()
        ffn_marks = {"mask": 0, "maskg": 0, "gelu": {}, "w2": {}}
        p.wait("t", "ss", p.val("ss"))
        for eh in range(2):
            if eh == 1:
                for ee in range(8):
                    e_g = 8 + ee
                    p.wait("g", "st", p.val("st"))
                    p.inc("g", lambda g, ee=ee, e_g=e_g: g.dma_start(
                        w1s[:, ee, :, :], w1_h[e_g].rearrange("(kc p) f -> p kc f", p=128)), "qw", 16)
                    p.inc("g", lambda g, ee=ee, e_g=e_g: g.dma_start(
                        w2s[:, ee, :, :], w2_h[e_g].rearrange("(fc p) d -> p fc d", p=128)), "qw", 16)
            p.wait("t", "qw")
            for b in range(NB):
                p.wait("y", "st", p.val("st"))
                p.wait("y", "qx", p.val("qx"))
                p.inc("y", lambda y, b=b: y.dma_start(
                    h2s.ap().rearrange("p j t -> p (j t)"),
                    h2fm_h[b].rearrange("p j t -> p (j t)")), "qx", 16)
                p.wait("t", "qx")
                for tt in range(4):
                    tsl = slice(512 * tt, 512 * (tt + 1))
                    for ee in range(8):
                        e_g = eh * 8 + ee
                        for fc in range(2):
                            po = 512 * (2 * (ee % 2) + fc)
                            p.wait("t", "ss", ffn_marks["gelu"].get((ee % 2, fc), 0))
                            fns = [
                                (lambda t, ee=ee, kc=kc, fc=fc, po=po, tsl=tsl: t.matmul(
                                    ps_h[:, po:po + 512], w1s[:, ee, kc, 128 * fc:128 * (fc + 1)],
                                    h2s[:, kc, tsl], start=(kc == 0), stop=False))
                                for kc in range(J)]
                            fns.append(lambda t, b=b, e_g=e_g, po=po, tsl=tsl: t.matmul(
                                ps_h[:, po:po + 512], eyes[:, e_g, :], selbs[b][:, tsl],
                                start=False, stop=True))
                            mm_group(p, fns)
                            p.wait("s", "st")
                            p.inc("s", lambda s, ee=ee, fc=fc, po=po, e_g=e_g: s.activation(
                                Hm[:, ee, fc, :], ps_h[:, po:po + 512], AF.Gelu,
                                bias=b1s[:, fc, e_g:e_g + 1]), "ss")
                            ffn_marks["gelu"][(ee % 2, fc)] = p.val("ss")
                    for grp in range(4):
                        p.wait("t", "ss", p.val("ss"))
                        p.wait("t", "sv", ffn_marks.get("oevac", 0))
                        for dc4 in range(2):
                            dc = grp * 2 + dc4
                            po = 512 * dc4
                            fns = []
                            for ee in range(8):
                                for fc in range(2):
                                    first = (ee == 0 and fc == 0)
                                    last_plain = (ee == 7 and fc == 1 and eh == 0)
                                    fns.append(lambda t, ee=ee, fc=fc, dc=dc, po=po,
                                               first=first, last_plain=last_plain: t.matmul(
                                        ps_o[:, po:po + 512],
                                        w2s[:, ee, fc, 128 * dc:128 * (dc + 1)],
                                        Hm[:, ee, fc, :],
                                        start=first, stop=last_plain))
                            if eh == 1:
                                fns.append(lambda t, b=b, dc=dc, po=po, tsl=tsl: t.matmul(
                                    ps_o[:, po:po + 512], b2s[:, 128 * dc:128 * (dc + 1)],
                                    selbs[b][:, tsl], start=False, stop=True))
                            mm_group(p, fns)
                        for dc4 in range(2):
                            dc = grp * 2 + dc4
                            po = 512 * dc4
                            if eh == 0:
                                p.wait("v", "st")
                                p.wait("v", "qz", p.val("qz"))
                                p.inc("v", lambda v, po=po: v.tensor_copy(ztile[:], ps_o[:, po:po + 512]), "sv")
                                ffn_marks["oevac"] = p.val("sv")
                                p.wait("y", "sv")
                                p.inc("y", lambda y, b=b, dc=dc, tt=tt: y.dma_start(
                                    zp_h[b, :, dc, 512 * tt:512 * (tt + 1)], ztile[:]), "qz", 16)
                            else:
                                p.wait("y", "sv", p.val("sv"))
                                p.wait("y", "qz", p.val("qz"))
                                p.inc("y", lambda y, b=b, dc=dc, tt=tt: y.dma_start(
                                    ztile[:], zp_h[b, :, dc, 512 * tt:512 * (tt + 1)]), "qz", 16)
                                p.wait("v", "qz")
                                p.wait("v", "st")
                                p.inc("v", lambda v, po=po: v.tensor_tensor(
                                    ztile[:], ztile[:], ps_o[:, po:po + 512], ALU.add), "sv")
                                ffn_marks["oevac"] = p.val("sv")
                                p.wait("v", "sv")
                                p.wait("v", "qz", p.val("qz"))
                                p.inc("v", lambda v, dc=dc, tsl=tsl: v.tensor_tensor(
                                    ztmp[:], ztile[:], h2s[:, dc, tsl], ALU.add), "sv")
                                p.wait("y", "sv")
                                p.inc("y", lambda y, b=b, dc=dc, tt=tt: y.dma_start(
                                    zfm_h[b, :, dc, 512 * tt:512 * (tt + 1)], ztmp[:]), "qz", 16)
                if eh == 1 and tt == 3:
                    emit_ln3(b, range(NTT))

        p.wait("y", "qo")
        p.wait("y", "qo1")
        p.replay(block)
    return nc


_NC_CACHE = {}


def _eyee():
    import ml_dtypes
    a = np.zeros((16, 16, 128), np.float32)
    for e in range(16):
        a[e, e, :] = 256.0
    return a.astype(ml_dtypes.bfloat16)


def kernel(L0_attn_x, L0_attn_time_decay, L0_attn_time_first, L0_attn_time_mix_k,
           L0_attn_time_mix_v, L0_attn_time_mix_r, L0_attn_Wk, L0_attn_Wv,
           L0_attn_Wr, L0_attn_Wo, L0_ffn_wg, L0_ffn_bg, L0_ffn_w1, L0_ffn_b1,
           L0_ffn_w2, L0_ffn_b2, L0_ffn_gamma, L0_ffn_beta):
    f = np.float32
    cc = np.ascontiguousarray
    x = cc(np.asarray(L0_attn_x, f))
    ident = np.eye(128, dtype=f)
    import ml_dtypes as _mld
    iota1 = np.tile(np.arange(1, 257, dtype=f)[None, :], (128, 1)).astype(_mld.bfloat16)

    if "l1" not in _NC_CACHE:
        _NC_CACHE["l1"] = build_l1()
        _NC_CACHE["l2"] = build_l2()
    nc1, nc2 = _NC_CACHE["l1"], _NC_CACHE["l2"]

    common1 = {
        "wkT": cc(np.asarray(L0_attn_Wk, f).T), "wvT": cc(np.asarray(L0_attn_Wv, f).T),
        "wrT": cc(np.asarray(L0_attn_Wr, f).T), "woT": cc(np.asarray(L0_attn_Wo, f).T),
        "wg": cc(np.asarray(L0_ffn_wg, f)), "bg": cc(np.asarray(L0_ffn_bg, f).reshape(1, E)),
        "tdec": cc(np.asarray(L0_attn_time_decay, f).reshape(C)),
        "tfirst": cc(np.asarray(L0_attn_time_first, f).reshape(C)),
        "tmk": cc(np.asarray(L0_attn_time_mix_k, f).reshape(C)),
        "tmv": cc(np.asarray(L0_attn_time_mix_v, f).reshape(C)),
        "tmr": cc(np.asarray(L0_attn_time_mix_r, f).reshape(C)),
        "ident": ident, "iota1": iota1,
    }
    in1 = [dict(common1, x=cc(x[4 * i:4 * i + 4])) for i in range(8)]
    res1 = run_bass_kernel_spmd(nc1, in1, core_ids=list(range(8)))
    mall = cc(np.concatenate([res1.results[i]["maskedsc"] for i in range(8)], axis=0))

    common2 = {
        "mall": mall,
        "w1": cc(np.asarray(L0_ffn_w1, f)), "w2": cc(np.asarray(L0_ffn_w2, f)),
        "b1": cc(np.asarray(L0_ffn_b1, f).reshape(E, 2, 128).transpose(2, 1, 0)),
        "b2": cc(np.asarray(L0_ffn_b2, f)),
        "eyee": _eyee(), "gam": cc(np.asarray(L0_ffn_gamma, f).reshape(1, C)),
        "bet": cc(np.asarray(L0_ffn_beta, f).reshape(1, C)),
        "ident": ident,
    }
    in2 = [dict(common2, x1=res1.results[i]["x1"], h2fm=res1.results[i]["h2fm"],
                mymsk=res1.results[i]["maskedsc"]) for i in range(8)]
    res2 = run_bass_kernel_spmd(nc2, in2, core_ids=list(range(8)))
    out = np.concatenate([res2.results[i]["out"] for i in range(8)], axis=0)
    return out.astype(np.float32)


# version 20
# speedup vs baseline: 1.0142x; 1.0142x over previous
import numpy as np
import contextlib
import concourse.bass as bass
import concourse.mybir as mybir
from concourse.bass_utils import run_bass_kernel_spmd

F32 = mybir.dt.float32
BF16 = mybir.dt.bfloat16
AF = mybir.ActivationFunctionType
ALU = mybir.AluOpType
AX = mybir.AxisListType

NB, T, C, J = 4, 2048, 1024, 8
TH, NCK, E, FP = 1024, 4, 16, 256
LN_EPS, EPS_GATE = 1e-5, 1e-6
NTT = T // 128


class Prog:
    def __init__(self, nc):
        self.nc = nc
        self.ops = {k: [] for k in ("v", "s", "t", "g", "y")}
        self.sems, self.cnt = {}, {}

    def sem(self, es, name):
        self.sems[name] = es.enter_context(self.nc.semaphore(name))
        self.cnt[name] = 0

    AUTO = {"v": "sv", "s": "ss", "g": "sg"}

    def add(self, eng, fn, dep=True):
        if eng in self.AUTO:
            sem = self.AUTO[eng]
            h = self.sems[sem]
            pre = self.cnt[sem]
            self.cnt[sem] += 1
            def wrapped(e, fn=fn, h=h, pre=pre, dep=dep):
                if dep and pre > 0:
                    e.wait_ge(h, pre)
                fn(e).then_inc(h, 1)
            self.ops[eng].append(wrapped)
        else:
            self.ops[eng].append(fn)

    def inc(self, eng, fn, sem, n=1):
        if eng in self.AUTO and sem == self.AUTO[eng]:
            self.add(eng, fn)
            return
        h = self.sems[sem]
        self.cnt[sem] += n
        self.ops[eng].append(lambda e, fn=fn, h=h, n=n: fn(e).then_inc(h, n))

    def wait(self, eng, sem, val=None):
        h = self.sems[sem]
        v = self.cnt[sem] if val is None else val
        if v > 0:
            self.ops[eng].append(lambda e, h=h, v=v: e.wait_ge(h, v))

    def val(self, sem):
        return self.cnt[sem]

    def replay(self, block):
        ops = self.ops

        @block.vector
        def _(v):
            for f in ops["v"]:
                f(v)

        @block.scalar
        def _(s):
            for f in ops["s"]:
                f(s)

        @block.tensor
        def _(t):
            for f in ops["t"]:
                f(t)

        @block.gpsimd
        def _(g):
            for f in ops["g"]:
                f(g)

        @block.sync
        def _(y):
            for f in ops["y"]:
                f(y)


def emit_ln(p, x_ap, out_ap, B):
    bst, mv, rstd = B["bst"], B["mv"], B["rstd"]
    p.add("v", lambda v: v.bn_stats(bst[:, 0:6], x_ap[:, 0:512]))
    p.add("v", lambda v: v.bn_stats(bst[:, 6:12], x_ap[:, 512:1024]))
    p.add("v", lambda v: v.bn_aggr(mv[:], bst[:]))
    p.wait("s", "sv")
    p.add("s", lambda s: s.activation(rstd[:], mv[:, 1:2], AF.Ln, bias=B["epst"][:]))
    p.add("s", lambda s: s.activation(rstd[:], rstd[:], AF.Exp, scale=-0.5))
    p.wait("v", "ss")
    p.add("v", lambda v: v.tensor_scalar(out_ap, x_ap, mv[:, 0:1], rstd[:], ALU.subtract, ALU.mult))


def mm_group(p, fns):
    for i, fn in enumerate(fns):
        if i == len(fns) - 1:
            p.inc("t", fn, "st")
        else:
            p.add("t", fn)


def build_l1():
    nc = bass.Bass("TRN2", target_bir_lowering=False)
    x_h = nc.dram_tensor("x", [NB, T, C], F32, kind="ExternalInput").ap()
    wkT_h = nc.dram_tensor("wkT", [C, C], F32, kind="ExternalInput").ap()
    wvT_h = nc.dram_tensor("wvT", [C, C], F32, kind="ExternalInput").ap()
    wrT_h = nc.dram_tensor("wrT", [C, C], F32, kind="ExternalInput").ap()
    woT_h = nc.dram_tensor("woT", [C, C], F32, kind="ExternalInput").ap()
    wg_h = nc.dram_tensor("wg", [C, E], F32, kind="ExternalInput").ap()
    bg_h = nc.dram_tensor("bg", [1, E], F32, kind="ExternalInput").ap()
    td_h = nc.dram_tensor("tdec", [C], F32, kind="ExternalInput").ap()
    tf_h = nc.dram_tensor("tfirst", [C], F32, kind="ExternalInput").ap()
    tmk_h = nc.dram_tensor("tmk", [C], F32, kind="ExternalInput").ap()
    tmv_h = nc.dram_tensor("tmv", [C], F32, kind="ExternalInput").ap()
    tmr_h = nc.dram_tensor("tmr", [C], F32, kind="ExternalInput").ap()
    ident_h = nc.dram_tensor("ident", [128, 128], F32, kind="ExternalInput").ap()
    iota1_h = nc.dram_tensor("iota1", [128, 256], BF16, kind="ExternalInput").ap()

    x1_h = nc.dram_tensor("x1", [NB, T, C], F32, kind="ExternalOutput").ap()
    h2fm_h = nc.dram_tensor("h2fm", [NB, 128, J, T], BF16, kind="ExternalOutput").ap()
    msk_h = nc.dram_tensor("maskedsc", [NB, T, E], F32, kind="ExternalOutput").ap()

    p = Prog(nc)
    es = contextlib.ExitStack()
    with es:
        sb = lambda n, sh, dt=BF16: es.enter_context(nc.sbuf_tensor(n, sh, dt))
        ps_ = lambda n, sh, dt=F32: es.enter_context(nc.psum_tensor(n, sh, dt))

        wk = sb("wk", [128, J, C])
        wv = sb("wv", [128, J, C])
        wr = sb("wr", [128, J, C])
        wo = sb("wo", [128, J, C])
        wgs = sb("wgs", [128, J, E])
        bgrow = sb("bgrow", [1, E])
        ones_col = sb("ones_col", [1, 128])
        idb = sb("idb", [128, 128])
        tmks = sb("tmks", [128, J], F32)
        tmvs = sb("tmvs", [128, J], F32)
        tmrs = sb("tmrs", [128, J], F32)
        us = sb("us", [128, J], F32)
        wpos = sb("wpos", [128, J], F32)
        lamall = sb("lamall", [128, J], F32)
        lam256 = sb("lam256", [128, J, NCK], F32)
        ewj = sb("ewj", [128, 256], F32)
        wneg = sb("wneg", [128, J], F32)
        iota1 = sb("iota1s", [128, 256])

        xt = sb("xt", [128, C], F32)
        htile = sb("htile", [128, C])
        B = dict(bst=sb("bst", [128, 12], F32), mv=sb("mv", [128, 2], F32),
                 rstd=sb("rstd", [128, 1], F32), epst=sb("epst", [128, 1], F32))
        hfm = sb("hfm", [128, J, T + 1])
        blk = sb("blk", [128, J, TH])
        blv = sb("blv", [128, J, TH])
        blr = sb("blr", [128, J, TH])
        kpl = sb("kpl", [128, 2, TH])
        vpl = sb("vpl", [128, 2, TH])
        srpl = sb("srpl", [128, 2, TH])
        eks = sb("eks", [128, TH + 1])
        ekvs = sb("ekvs", [128, TH + 1])
        kap = sb("kap", [128, 1], F32)
        nk = sb("nk", [128, 3], F32)
        us3 = sb("us3", [128, J, 3], F32)
        ksc = sb("ksc", [128, 1], F32)
        gpa = sb("gpa", [128, 1], F32)
        gpb = sb("gpb", [128, 1], F32)
        gA_all = sb("gA_all", [128, J], F32)
        gB_all = sb("gB_all", [128, J], F32)
        ekl_all = sb("ekl_all", [128, J], F32)
        ekvl_all = sb("ekvl_all", [128, J], F32)
        EA = sb("EA", [128, TH])
        EB = sb("EB", [128, TH])
        bl_d = sb("bl_d", [128, 512]).ap()
        lastA = sb("lastA", [128, NCK], F32)
        lastB = sb("lastB", [128, NCK], F32)
        GA = sb("GA", [128, NCK], F32)
        GB = sb("GB", [128, NCK], F32)
        euk = sb("euk", [128, TH])
        rwkv = sb("rwkv", [128, J, TH])
        afm = sb("afm", [128, J, 512])
        atile = sb("atile", [128, C])
        x1t = sb("x1t", [128, C], F32)
        h2t = atile
        stag3 = htile.ap().rearrange("p (j c) -> p j c", j=J)
        scst = sb("scst", [128, E], F32)
        mx8 = sb("mx8", [128, 8], F32)
        nmx = sb("nmx", [128, 1], F32)
        ssum = sb("ssum", [128, 1], F32)
        rsum = sb("rsum", [128, 1], F32)
        maskt = sb("maskt", [128, E], F32)
        mskd = sb("mskd", [128, E], F32)

        ps_proj = ps_("ps_proj", [128, 1024])
        ps_tp = ps_("ps_tp", [128, 2048], BF16)
        ps_wo = ps_("ps_wo", [128, 512])
        ps_gate = ps_("ps_gate", [128, E])

        for s_ in ("sv", "ss", "st", "sg", "qx", "qxa", "qxb", "qw", "qo", "qh"):
            p.sem(es, s_)
        block = es.enter_context(nc.Block())

        for w_s, w_hh in ((wk, wkT_h), (wv, wvT_h), (wr, wrT_h), (wo, woT_h)):
            p.inc("g", lambda g, a=w_s, b_=w_hh: g.dma_start(
                a[:], b_.rearrange("(kc p) d -> p kc d", p=128)), "qw", 16)
        p.inc("g", lambda g: g.dma_start(wgs[:], wg_h.rearrange("(kc p) e -> p kc e", p=128)), "qw", 16)
        p.inc("g", lambda g: g.dma_start(bgrow[:], bg_h[:]), "qw", 16)
        p.inc("g", lambda g: g.dma_start(idb[:], ident_h[:]), "qw", 16)
        for v_s, v_hh in ((tmks, tmk_h), (tmvs, tmv_h), (tmrs, tmr_h), (us, tf_h), (wpos, td_h)):
            p.inc("y", lambda y, a=v_s, b_=v_hh: y.dma_start(
                a[:], b_.rearrange("(j p) -> p j", p=128), allow_slow_non_contiguous=True), "qx", 16)
        p.inc("y", lambda y: y.dma_start(iota1[:], iota1_h[:]), "qx", 16)
        p.wait("s", "qx")
        p.inc("s", lambda s: s.activation(wpos[:], wpos[:], AF.Exp), "ss")
        p.wait("v", "ss")
        p.add("v", lambda v: v.memset(ones_col[:], 0))
        p.add("v", lambda v: v.tensor_scalar(ones_col[:], ones_col[:], 0.0, 1.0, ALU.mult, ALU.add))
        p.add("v", lambda v: v.memset(B["epst"][:], float(LN_EPS)))
        p.add("v", lambda v: v.memset(us3[:], 0))
        for j in range(J):
            p.add("v", lambda v, j=j: v.tensor_copy(us3[:, j, 1:2], us[:, j:j + 1]))
        for j in range(J):
            p.add("v", lambda v, j=j: v.memset(lam256[:, j, :], 0))
            p.inc("v", lambda v, j=j: v.tensor_scalar(
                lam256[:, j, :], lam256[:, j, :], wpos[:, j:j + 1], None, ALU.subtract), "sv")
        p.wait("s", "sv")
        for j in range(J):
            p.inc("s", lambda s, j=j: s.activation(lam256[:, j, :], lam256[:, j, :], AF.Exp, scale=256.0), "ss")
        p.inc("s", lambda s: s.activation(lamall[:], wpos[:], AF.Exp, scale=-1.0), "ss")
        p.inc("v", lambda v: v.tensor_scalar(
            wneg[:], wpos[:], -1.0, None, ALU.mult), "sv")
        wneg_mark = p.val("sv")
        p.wait("v", "ss")
        p.inc("v", lambda v: v.memset(hfm[:], 0), "sv")
        p.wait("t", "sv")
        p.wait("t", "qw")
        p.wait("y", "sv")

        bank_mark = {0: 0, 1: 0}
        tp_mark = {0: 0, 1: 0}
        wo_mark = [0]
        afm_free = [0]
        wkv_marks = {}
        last_wkv_sg = [0]
        xbuf_free = {0: 0, 1: 0}

        def emit_ln1_tile(b, ti):
            par = ti % 2
            xbuf = xt if par == 0 else x1t
            qsem = "qxa" if par == 0 else "qxb"
            p.wait("y", "sv", xbuf_free[par])
            p.inc("y", lambda y, b=b, ti=ti, xbuf=xbuf: y.dma_start(
                xbuf[:], x_h[b, 128 * ti:128 * (ti + 1), :]), qsem, 16)
            p.wait("v", qsem)
            p.wait("v", "st", p.val("st"))
            p.wait("v", "qh", p.val("qh"))
            emit_ln(p, xbuf[:], htile[:], B)
            xbuf_free[par] = p.val("sv")
            tpo = 1024 * par
            p.wait("t", "sv", p.val("sv"))
            p.wait("t", "ss", tp_mark[par])
            mm_group(p, [
                (lambda t, j=j, tpo=tpo: t.transpose(
                    ps_tp[:, tpo + 128 * j:tpo + 128 * (j + 1)],
                    htile[:, 128 * j:128 * (j + 1)], idb[:]))
                for j in range(J)])
            p.wait("s", "st")
            p.inc("s", lambda s, ti=ti, tpo=tpo: s.copy(
                hfm.ap()[:, :, 1 + 128 * ti:1 + 128 * (ti + 1)],
                ps_tp[:, tpo:tpo + 1024].rearrange("p (j c) -> p j c", j=J)), "ss")
            tp_mark[par] = p.val("ss")

        def emit_blends(b, h):
            p.wait("v", "st", p.val("st"))
            t0 = h * TH
            for tt in range(2):
                ts0 = t0 + 512 * tt
                osl = slice(512 * tt, 512 * (tt + 1))
                for j in range(J):
                    h_sl = hfm[:, j, 1 + ts0:1 + ts0 + 512]
                    hs_sl = hfm[:, j, ts0:ts0 + 512]
                    p.add("v", lambda v, h_sl=h_sl, hs_sl=hs_sl: v.tensor_tensor(
                        bl_d, h_sl, hs_sl, ALU.subtract))
                    p.add("v", lambda v, j=j, osl=osl, hs_sl=hs_sl: v.scalar_tensor_tensor(
                        blk[:, j, osl], bl_d, tmks[:, j:j + 1], hs_sl, ALU.mult, ALU.add))
                    p.add("v", lambda v, j=j, osl=osl, hs_sl=hs_sl: v.scalar_tensor_tensor(
                        blv[:, j, osl], bl_d, tmvs[:, j:j + 1], hs_sl, ALU.mult, ALU.add), dep=False)
                    p.add("v", lambda v, j=j, osl=osl, hs_sl=hs_sl: v.scalar_tensor_tensor(
                        blr[:, j, osl], bl_d, tmrs[:, j:j + 1], hs_sl, ALU.mult, ALU.add), dep=False)
            return p.val("sv")

        def emit_proj(step, bl_ready):
            h, j = divmod(step, J)
            par = step % 2
            pm = wkv_marks.get(step - 2)
            first = True
            for pi, (wmat, bl, dst, actf) in enumerate((
                    (wk, blk, kpl, AF.Copy), (wv, blv, vpl, AF.Copy),
                    (wr, blr, srpl, AF.Sigmoid))):
                for tt in range(2):
                    bank = (pi * 2 + tt) % 2
                    po = 512 * bank
                    p.wait("t", "ss", bank_mark[bank])
                    if first:
                        p.wait("t", "sv", bl_ready)
                        if pm:
                            p.wait("t", "sv", pm["sg"])
                            p.wait("t", "ss", pm["ss"])
                        first = False
                    mm_group(p, [
                        (lambda t, wmat=wmat, bl=bl, kc=kc, j=j, po=po, tt=tt: t.matmul(
                            ps_proj[:, po:po + 512],
                            wmat[:, kc, 128 * j:128 * (j + 1)],
                            bl[:, kc, 512 * tt:512 * (tt + 1)],
                            start=(kc == 0), stop=(kc == J - 1)))
                        for kc in range(J)])
                    p.wait("s", "st")
                    p.inc("s", lambda s, dst=dst, par=par, po=po, actf=actf, tt=tt: s.activation(
                        dst[:, par, 512 * tt:512 * (tt + 1)], ps_proj[:, po:po + 512], actf), "ss")
                    bank_mark[bank] = p.val("ss")

        def emit_wkv(step, rwkv_free=0):
            h, j = divmod(step, J)
            par = step % 2
            hf = h
            kp = kpl[:, par, :]
            vp = vpl[:, par, :]
            srp = srpl[:, par, :]
            p.wait("s", "sv", max(last_wkv_sg[0], wneg_mark))
            p.add("s", lambda s, j=j: s.activation(
                ewj[:], iota1[:], AF.Exp, scale=wneg[:, j:j + 1]))
            p.wait("v", "ss", bank_mark[0])
            p.wait("v", "ss", bank_mark[1])
            p.add("v", lambda v, kp=kp: v.tensor_reduce(kap[:], kp, axis=AX.X, op=ALU.max))
            if hf == 0:
                p.add("v", lambda v, j=j: v.tensor_copy(us3[:, j, 2:3], kap[:]))
                p.add("v", lambda v, j=j: v.tensor_scalar(
                    nk[:, 0:2], us3[:, j, 0:2], kap[:], None, ALU.subtract))
            else:
                p.add("v", lambda v, j=j: v.tensor_scalar(
                    nk[:], us3[:, j, :], kap[:], None, ALU.subtract))
            p.wait("s", "sv")
            if hf == 1:
                p.add("s", lambda s: s.activation(ksc[:], nk[:, 2:3], AF.Exp))
            p.add("s", lambda s, kp=kp: s.activation(eks[:, 1:], kp, AF.Exp, bias=nk.ap()[:, 0:1]))
            p.add("s", lambda s, kp=kp: s.activation(euk[:], kp, AF.Exp, bias=nk.ap()[:, 1:2]))
            p.wait("v", "ss")
            p.add("v", lambda v, vp=vp: v.tensor_tensor(ekvs[:, 1:], eks[:, 1:], vp, ALU.mult))
            if hf == 0:
                p.add("v", lambda v: v.memset(eks[:, 0:1], 0))
                p.add("v", lambda v: v.memset(ekvs[:, 0:1], 0))
                p.add("v", lambda v: v.memset(gpa[:], 0))
                p.add("v", lambda v: v.memset(gpb[:], 0))
            else:
                p.add("v", lambda v, j=j: v.tensor_tensor(
                    eks[:, 0:1], ekl_all[:, j:j + 1], ksc[:], ALU.mult))
                p.add("v", lambda v, j=j: v.tensor_tensor(
                    ekvs[:, 0:1], ekvl_all[:, j:j + 1], ksc[:], ALU.mult))
                p.add("v", lambda v, j=j: v.tensor_tensor(
                    gpa[:], gA_all[:, j:j + 1], ksc[:], ALU.mult))
                p.add("v", lambda v, j=j: v.tensor_tensor(
                    gpb[:], gB_all[:, j:j + 1], ksc[:], ALU.mult))
            p.add("v", lambda v, j=j: v.tensor_copy(ekl_all[:, j:j + 1], eks[:, TH:TH + 1]))
            p.add("v", lambda v, j=j: v.tensor_copy(ekvl_all[:, j:j + 1], ekvs[:, TH:TH + 1]))
            lamb = lamall.ap()[:, j:j + 1].broadcast_to((128, 256))
            for c in range(NCK):
                sl = slice(256 * c, 256 * (c + 1))
                p.add("v", lambda v, sl=sl, lamb=lamb: v.tensor_tensor_scan(
                    EB[:, sl], lamb, eks[:, sl], 0.0, ALU.mult, ALU.add), dep=(c == 0))
            for c in range(NCK):
                sl = slice(256 * c, 256 * (c + 1))
                p.add("v", lambda v, sl=sl, lamb=lamb: v.tensor_tensor_scan(
                    EA[:, sl], lamb, ekvs[:, sl], 0.0, ALU.mult, ALU.add), dep=False)
            EBr = EB.ap().rearrange("p (c l) -> p c l", l=256)
            EAr = EA.ap().rearrange("p (c l) -> p c l", l=256)
            p.add("v", lambda v, EBr=EBr: v.tensor_copy(lastB[:], EBr[:, :, 255:256]))
            p.add("v", lambda v, EAr=EAr: v.tensor_copy(lastA[:], EAr[:, :, 255:256]))
            p.add("v", lambda v, j=j: v.scalar_tensor_tensor(
                lastB[:, 0:1], lam256[:, j, 0:1], gpb[:], lastB[:, 0:1], ALU.mult, ALU.add))
            p.add("v", lambda v, j=j: v.scalar_tensor_tensor(
                lastA[:, 0:1], lam256[:, j, 0:1], gpa[:], lastA[:, 0:1], ALU.mult, ALU.add))
            p.add("v", lambda v, j=j: v.tensor_tensor_scan(
                GB[:], lam256[:, j, :], lastB[:], 0.0, ALU.mult, ALU.add))
            p.add("v", lambda v, j=j: v.tensor_tensor_scan(
                GA[:], lam256[:, j, :], lastA[:], 0.0, ALU.mult, ALU.add))
            p.add("v", lambda v, j=j: v.tensor_copy(gB_all[:, j:j + 1], GB[:, NCK - 1:NCK]))
            p.add("v", lambda v, j=j: v.tensor_copy(gA_all[:, j:j + 1], GA[:, NCK - 1:NCK]))
            for c in range(NCK):
                sl = slice(256 * c, 256 * (c + 1))
                p.add("v", lambda v, sl=sl, c=c: v.scalar_tensor_tensor(
                    EB[:, sl], ewj[:],
                    (gpb[:] if c == 0 else GB[:, c - 1:c]), EB[:, sl], ALU.mult, ALU.add), dep=(c == 0))
            p.add("v", lambda v: v.tensor_tensor(EB[:], EB[:], euk[:], ALU.add))
            p.wait("s", "sv")
            p.add("s", lambda s: s.activation(EB[:], EB[:], AF.Ln))
            p.add("s", lambda s: s.activation(EB[:], EB[:], AF.Exp, scale=-1.0))
            for c in range(NCK):
                sl = slice(256 * c, 256 * (c + 1))
                p.add("v", lambda v, sl=sl, c=c: v.scalar_tensor_tensor(
                    EA[:, sl], ewj[:],
                    (gpa[:] if c == 0 else GA[:, c - 1:c]), EA[:, sl], ALU.mult, ALU.add), dep=(c == 0))
            p.add("v", lambda v, vp=vp: v.tensor_tensor(euk[:], euk[:], vp, ALU.mult))
            p.add("v", lambda v: v.tensor_tensor(EA[:], EA[:], euk[:], ALU.add))
            p.wait("v", "ss")
            p.add("v", lambda v: v.tensor_tensor(EA[:], EA[:], EB[:], ALU.mult))
            p.wait("v", "st", rwkv_free)
            p.add("v", lambda v, j=j, srp=srp: v.tensor_tensor(
                rwkv[:, j, :], EA[:], srp, ALU.mult))
            wkv_marks[step] = dict(sv=p.val("sv"), ss=p.val("ss"), sg=p.val("sv"))
            last_wkv_sg[0] = p.val("sv")

        def emit_woq(b, h, qq):
            p.wait("t", "sv", wkv_marks[h * J + J - 1]["sg"])
            for m in range(J):
                p.wait("t", "ss", wo_mark[0])
                mm_group(p, [
                    (lambda t, kc=kc, m=m, qq=qq: t.matmul(
                        ps_wo[:], wo[:, kc, 128 * m:128 * (m + 1)],
                        rwkv[:, kc, 512 * qq:512 * (qq + 1)],
                        start=(kc == 0), stop=(kc == J - 1)))
                    for kc in range(J)])
                p.wait("s", "st")
                p.wait("s", "st", afm_free[0])
                p.inc("s", lambda s, m=m: s.copy(afm[:, m, :], ps_wo[:]), "ss")
                wo_mark[0] = p.val("ss")
            return p.val("st")

        def emit_epi_tile(b, h, qq, ti4):
            ti = h * 8 + qq * 4 + ti4
            par = ti4 % 2
            tpo = 1024 * par
            p.wait("t", "ss", wo_mark[0])
            p.wait("t", "ss", tp_mark[par])
            mm_group(p, [
                (lambda t, j=j, ti4=ti4, tpo=tpo: t.transpose(
                    ps_tp[:, tpo + 128 * j:tpo + 128 * (j + 1)],
                    afm[:, j, 128 * ti4:128 * (ti4 + 1)], idb[:]))
                for j in range(J)])
            afm_free[0] = p.val("st")
            p.wait("s", "st")
            p.inc("s", lambda s, tpo=tpo: s.copy(atile[:], ps_tp[:, tpo:tpo + 1024]), "ss")
            tp_mark[par] = p.val("ss")
            atile_ss = p.val("ss")
            p.wait("y", "sv", xbuf_free[0])
            p.inc("y", lambda y, b=b, ti=ti: y.dma_start(
                xt[:], x_h[b, 128 * ti:128 * (ti + 1), :]), "qxa", 16)
            p.wait("v", "qxa")
            p.wait("v", "ss", atile_ss)
            p.wait("v", "qo", p.val("qo"))
            p.add("v", lambda v: v.tensor_tensor(x1t[:], xt[:], atile[:], ALU.add))
            xbuf_free[0] = p.val("sv")
            p.wait("y", "sv")
            p.inc("y", lambda y, b=b, ti=ti: y.dma_start(
                x1_h[b, 128 * ti:128 * (ti + 1), :], x1t[:]), "qo", 16)
            p.wait("v", "st", p.val("st"))
            p.wait("v", "qh", p.val("qh"))
            emit_ln(p, x1t[:], h2t[:], B)
            h2t_sv = p.val("sv")
            tpo2 = 1024 * ((ti4 + 1) % 2)
            p.wait("t", "sv", h2t_sv)
            p.wait("t", "ss", tp_mark[(ti4 + 1) % 2])
            mm_group(p, [
                (lambda t, j=j, tpo2=tpo2: t.transpose(
                    ps_tp[:, tpo2 + 128 * j:tpo2 + 128 * (j + 1)],
                    h2t[:, 128 * j:128 * (j + 1)], idb[:]))
                for j in range(J)])
            p.wait("s", "st")
            p.wait("s", "qh", p.val("qh"))
            p.inc("s", lambda s, tpo2=tpo2: s.copy(
                htile.ap(), ps_tp[:, tpo2:tpo2 + 1024]), "ss")
            tp_mark[(ti4 + 1) % 2] = p.val("ss")
            stag_ss = p.val("ss")
            p.wait("y", "ss", stag_ss)
            p.inc("y", lambda y, b=b, ti=ti: y.dma_start(
                h2fm_h[b, :, :, 128 * ti:128 * (ti + 1)], stag3), "qh", 16)
            p.wait("t", "ss", stag_ss)
            fns = [(lambda t, j=j: t.matmul(
                ps_gate[:], stag3[:, j, :], wgs[:, j, :], start=(j == 0), stop=False))
                for j in range(J)]
            fns.append(lambda t: t.matmul(ps_gate[:], ones_col[:], bgrow[:],
                                          start=False, stop=True))
            mm_group(p, fns)
            p.wait("v", "st")
            p.add("v", lambda v: v.max(mx8[:], ps_gate[:]))
            p.add("v", lambda v: v.tensor_scalar(nmx[:], mx8[:, 0:1], -1.0, None, ALU.mult))
            p.wait("s", "sv")
            p.add("s", lambda s: s.activation(scst[:], ps_gate[:], AF.Exp, bias=nmx[:]))
            p.wait("v", "ss")
            p.add("v", lambda v: v.reduce_sum(ssum[:], scst[:], axis=AX.X))
            p.add("v", lambda v: v.reciprocal(rsum[:], ssum[:]))
            p.add("v", lambda v: v.tensor_scalar(scst[:], scst[:], rsum[:], None, ALU.mult))
            p.add("v", lambda v: v.max(mx8[:], scst[:]))
            p.add("v", lambda v: v.tensor_scalar(
                maskt[:], scst[:], mx8[:, 1:2], None, ALU.is_ge))
            p.wait("v", "qo", p.val("qo"))
            p.add("v", lambda v: v.tensor_tensor(mskd[:], scst[:], maskt[:], ALU.mult))
            p.wait("y", "sv")
            p.inc("y", lambda y, b=b, ti=ti: y.dma_start(
                msk_h[b, 128 * ti:128 * (ti + 1), :], mskd[:]), "qo", 16)

        rwkv_free_st = [0]
        for b in range(NB):
            for ti in range(NTT):
                emit_ln1_tile(b, ti)
            bl0 = emit_blends(b, 0)
            emit_proj(0, bl0)
            for j in range(1, 8):
                emit_wkv(j - 1, rwkv_free_st[0])
                emit_proj(j, bl0)
            bl1 = emit_blends(b, 1)
            emit_wkv(7, rwkv_free_st[0])
            emit_proj(8, bl1)
            emit_woq(b, 0, 0)
            emit_proj(9, bl1)
            for t4 in range(4):
                emit_epi_tile(b, 0, 0, t4)
            wo_st = emit_woq(b, 0, 1)
            rwkv_free_st[0] = wo_st
            emit_wkv(8, rwkv_free_st[0])
            emit_proj(10, bl1)
            emit_epi_tile(b, 0, 1, 0)
            emit_epi_tile(b, 0, 1, 1)
            emit_wkv(9, rwkv_free_st[0])
            emit_proj(11, bl1)
            emit_epi_tile(b, 0, 1, 2)
            emit_epi_tile(b, 0, 1, 3)
            emit_wkv(10, rwkv_free_st[0])
            emit_wkv(11, rwkv_free_st[0])
            emit_proj(12, bl1)
            emit_wkv(12, rwkv_free_st[0])
            emit_proj(13, bl1)
            emit_wkv(13, rwkv_free_st[0])
            emit_proj(14, bl1)
            emit_wkv(14, rwkv_free_st[0])
            emit_proj(15, bl1)
            emit_wkv(15, rwkv_free_st[0])
            emit_woq(b, 1, 0)
            for t4 in range(4):
                emit_epi_tile(b, 1, 0, t4)
            wo_st = emit_woq(b, 1, 1)
            rwkv_free_st[0] = wo_st
            for t4 in range(4):
                emit_epi_tile(b, 1, 1, t4)
            wkv_marks.clear()

        p.wait("y", "qo")
        p.wait("y", "qh")
        p.replay(block)
    return nc


def build_l2():
    nc = bass.Bass("TRN2", target_bir_lowering=False)
    x1_h = nc.dram_tensor("x1", [NB, T, C], F32, kind="ExternalInput").ap()
    h2fm_h = nc.dram_tensor("h2fm", [NB, 128, J, T], BF16, kind="ExternalInput").ap()
    mall_h = nc.dram_tensor("mall", [32, T, E], F32, kind="ExternalInput").ap()
    mymsk_h = nc.dram_tensor("mymsk", [NB, T, E], F32, kind="ExternalInput").ap()
    w1_h = nc.dram_tensor("w1", [E, C, FP], F32, kind="ExternalInput").ap()
    w2_h = nc.dram_tensor("w2", [E, FP, C], F32, kind="ExternalInput").ap()
    b1_h = nc.dram_tensor("b1", [128, 2, E], F32, kind="ExternalInput").ap()
    b2_h = nc.dram_tensor("b2", [E, C], F32, kind="ExternalInput").ap()
    gam_h = nc.dram_tensor("gam", [1, C], F32, kind="ExternalInput").ap()
    bet_h = nc.dram_tensor("bet", [1, C], F32, kind="ExternalInput").ap()
    ident_h = nc.dram_tensor("ident", [128, 128], F32, kind="ExternalInput").ap()
    eyee_h = nc.dram_tensor("eyee", [16, 16, 128], BF16, kind="ExternalInput").ap()

    out_h = nc.dram_tensor("out", [NB, T, C], F32, kind="ExternalOutput").ap()
    zp_h = nc.dram_tensor("zpart", [NB, 128, J, T], F32, kind="Internal").ap()
    zfm_h = nc.dram_tensor("zfm", [NB, 128, J, T], BF16, kind="Internal").ap()

    p = Prog(nc)
    es = contextlib.ExitStack()
    with es:
        sb = lambda n, sh, dt=BF16: es.enter_context(nc.sbuf_tensor(n, sh, dt))
        ps_ = lambda n, sh, dt=F32: es.enter_context(nc.psum_tensor(n, sh, dt))

        idb = sb("idb", [128, 128])
        idf = sb("idf", [128, 128], F32)
        ones_col = sb("ones_col", [1, 128])
        gamt = sb("gamt", [128, C])
        bett = sb("bett", [128, C])
        gbrow = sb("gbrow", [1, C])
        b2s = sb("b2s", [16, C])
        b1s = sb("b1s", [128, 2, E], F32)
        masksb = sb("masksb", [128, 512])
        selbs = [sb(f"selb{b}", [16, T]) for b in range(NB)]
        eyes = sb("eyes", [16, 16, 128])
        zload0 = sb("zload0", [128, C])
        zload1 = sb("zload1", [128, C])
        znorm = sb("znorm", [128, C])
        x1t0 = sb("x1t0", [128, C], F32)
        x1t1 = sb("x1t1", [128, C], F32)
        o10 = sb("o10", [128, C], F32)
        o11 = sb("o11", [128, C], F32)
        t2 = sb("t2", [128, C])
        ztile = sb("ztile", [128, 512], F32)
        ztmp = sb("ztmp", [128, 512])
        B = dict(bst=sb("bst", [128, 12], F32), mv=sb("mv", [128, 2], F32),
                 rstd=sb("rstd", [128, 1], F32), epst=sb("epst", [128, 1], F32))
        h2s = sb("h2s", [128, J, T])
        Hm = sb("Hm", [128, 8, 2, 512])
        w1s = sb("w1s", [128, 8, J, FP])
        w2s = sb("w2s", [128, 8, 2, C])
        es2 = contextlib.ExitStack()
        sb2 = lambda n, sh, dt=BF16: es2.enter_context(nc.sbuf_tensor(n, sh, dt))
        mt32 = sb2("mt32", [128, 32, E], F32)
        dsum = sb2("dsum", [128, E], F32)
        rdt = sb2("rdt", [128, NTT, E], F32)
        routf = sb2("routf", [128, NTT, E], F32)
        rem = sb2("rem", [64, T], F32)
        rstage = sb2("rstage", [16, T], F32)
        nnz = sb2("nnz", [64, 1], F32)
        mxr = sb2("mxr", [64, 1], F32)
        tlo = sb2("tlo", [64, 1], F32)
        thi = sb2("thi", [64, 1], F32)
        tmid = sb2("tmid", [64, 1], F32)
        thet = sb2("thet", [64, 1], F32)
        cntb = sb2("cntb", [64, 1], F32)
        geb = sb2("geb", [64, 1], F32)
        cdl = sb2("cdl", [64, 1], F32)
        cdh = sb2("cdh", [64, 1], F32)
        selA = sb2("selA", [64, T])
        junk2 = selA
        zind = sb2("zind", [64, T])
        zrank = sb2("zrank", [64, T], mybir.dt.float16)
        zlast = sb2("zlast", [64, J], F32)
        zG = sb2("zG", [64, J], F32)
        onesT = sb2("onesT", [64, 256], F32)
        ones8 = sb2("ones8", [64, J], F32)
        Ke = sb2("Ke", [64, 1], F32)
        Fb = sb2("Fb", [64, 1], F32)
        Fc = sb2("Fc", [64, 1], F32)
        selB = sb2("selB", [64, T])

        ps_h = ps_("ps_h", [128, 2048])
        ps_o = ps_("ps_o", [128, 1024])
        ps_x = ps_("ps_x", [128, 512])
        ps_t = ps_("ps_t", [128, 1024], BF16)

        for s_ in ("sv", "ss", "st", "sg", "qx", "qxa", "qxb", "qw", "qb", "qo", "qo1", "qz"):
            p.sem(es, s_)
        block = es.enter_context(nc.Block())

        p.inc("g", lambda g: g.dma_start(idb[:], ident_h[:]), "qw", 16)
        p.inc("y", lambda y: y.dma_start(idf[:], ident_h[:]), "qx", 16)
        p.inc("g", lambda g: g.dma_start(b2s[:], b2_h[:]), "qw", 16)
        p.inc("y", lambda y: y.dma_start(eyes[:], eyee_h[:]), "qx", 16)
        p.inc("g", lambda g: g.dma_start(b1s[:], b1_h[:]), "qb", 16)
        p.wait("v", "qb")
        p.inc("v", lambda v: v.tensor_scalar(
            b1s.ap().rearrange("p a b -> p (a b)"), b1s.ap().rearrange("p a b -> p (a b)"),
            -256.0, None, ALU.add), "sv")
        p.wait("v", "qx")
        p.add("v", lambda v: v.memset(ones_col[:], 0))
        p.add("v", lambda v: v.tensor_scalar(ones_col[:], ones_col[:], 0.0, 1.0, ALU.mult, ALU.add))
        p.add("v", lambda v: v.memset(onesT[:], 0))
        p.add("v", lambda v: v.tensor_scalar(onesT[:], onesT[:], 0.0, 1.0, ALU.mult, ALU.add))
        p.add("v", lambda v: v.memset(B["epst"][:], float(LN_EPS)))
        p.add("v", lambda v: v.memset(ones8[:], 0))
        p.inc("v", lambda v: v.tensor_scalar(ones8[:], ones8[:], 0.0, 1.0, ALU.mult, ALU.add), "sv")
        p.wait("t", "sv", p.val("sv"))
        for src_h, dst, dty in ((gam_h, gamt, "bf"), (bet_h, bett, "f32")):
            p.inc("g", lambda g, src_h=src_h: g.dma_start(gbrow[:], src_h[:]), "qw", 16)
            p.wait("t", "qw")
            for half in range(2):
                p.wait("t", "ss", p.val("ss"))
                p.inc("t", lambda t, half=half: t.matmul(
                    ps_x[:], ones_col[:], gbrow[:, 512 * half:512 * (half + 1)],
                    start=True, stop=True), "st")
                p.wait("s", "st")
                p.inc("s", lambda s, dst=dst, half=half: s.copy(
                    dst[:, 512 * half:512 * (half + 1)], ps_x[:]), "ss")
            p.wait("g", "ss")

        for ee in range(8):
            p.inc("g", lambda g, ee=ee: g.dma_start(
                w1s[:, ee, :, :], w1_h[ee].rearrange("(kc p) f -> p kc f", p=128)), "qw", 16)
            p.inc("g", lambda g, ee=ee: g.dma_start(
                w2s[:, ee, :, :], w2_h[ee].rearrange("(fc p) d -> p fc d", p=128)), "qw", 16)
        for ti in range(NTT):
            p.wait("y", "sv", p.val("sv"))
            p.inc("y", lambda y, ti=ti: y.dma_start(
                mt32[:], mall_h[:, 128 * ti:128 * (ti + 1), :].rearrange("b p e -> p b e")), "qx", 16)
            p.wait("v", "qx")
            for lvl in (16, 8, 4, 2, 1):
                p.add("v", lambda v, lvl=lvl: v.tensor_tensor(
                    mt32.ap()[:, 0:lvl, :].rearrange("p b e -> p (b e)"),
                    mt32.ap()[:, 0:lvl, :].rearrange("p b e -> p (b e)"),
                    mt32.ap()[:, lvl:2 * lvl, :].rearrange("p b e -> p (b e)"), ALU.add))
            p.add("v", lambda v: v.tensor_scalar(dsum[:], mt32[:, 0, :], EPS_GATE, None, ALU.add))
            p.add("v", lambda v, ti=ti: v.reciprocal(rdt[:, ti, :], dsum[:]))
        for b in range(NB):
            p.wait("y", "sv", p.val("sv"))
            p.wait("y", "st", p.val("st"))
            p.inc("y", lambda y, b=b: y.dma_start(
                routf[:], mymsk_h[b].rearrange("(ti p) e -> p ti e", p=128)), "qx", 16)
            p.wait("v", "qx")
            for ti in range(NTT):
                p.add("v", lambda v, ti=ti: v.scalar_tensor_tensor(
                    routf[:, ti, :], routf[:, ti, :], 32.0, rdt[:, ti, :],
                    ALU.mult, ALU.mult), dep=(ti == 0))
            p.wait("t", "sv")
            for g8 in range(2):
                p.wait("t", "ss", p.val("ss"))
                mm_group(p, [
                    (lambda t, ti=ti, k=k: t.transpose(
                        ps_h.ap()[0:16, 128 * k:128 * (k + 1)], routf[:, ti, :], idf[:]))
                    for k, ti in enumerate(range(8 * g8, 8 * g8 + 8))])
                p.wait("s", "st")
                p.inc("s", lambda s, g8=g8: s.copy(
                    rstage[:, 1024 * g8:1024 * (g8 + 1)], ps_h.ap()[0:16, 0:1024]), "ss")
            p.wait("y", "ss")
            p.inc("y", lambda y, b=b: y.dma_start(rem[16 * b:16 * (b + 1), :], rstage[:]), "qx", 16)
        p.wait("v", "qx")
        p.wait("v", "ss")
        p.inc("v", lambda v: v.tensor_scalar(
            junk2[:], rem[:], 0.0, None, ALU.is_gt, ALU.add, accum_out=nnz[:]), "sv")
        p.add("v", lambda v: v.tensor_reduce(mxr[:], rem[:], axis=AX.X, op=ALU.max))
        p.wait("v", "sv")
        p.inc("v", lambda v: v.tensor_scalar(mxr[:], mxr[:], 1e-30, None, ALU.max), "sv")
        p.wait("s", "sv")
        p.inc("s", lambda s: s.activation(thi[:], mxr[:], AF.Ln), "ss")
        p.wait("v", "ss")
        p.add("v", lambda v: v.tensor_scalar(thi[:], thi[:], 0.01, None, ALU.add))
        p.add("v", lambda v: v.memset(tlo[:], 0))
        p.inc("v", lambda v: v.tensor_scalar(tlo[:], tlo[:], 0.0, -110.0, ALU.mult, ALU.add), "sv")
        for it in range(44):
            p.wait("v", "sv", p.val("sv"))
            p.add("v", lambda v: v.tensor_tensor(tmid[:], tlo[:], thi[:], ALU.add))
            p.inc("v", lambda v: v.tensor_scalar(tmid[:], tmid[:], 0.5, None, ALU.mult), "sv")
            p.wait("s", "sv")
            p.inc("s", lambda s: s.activation(thet[:], tmid[:], AF.Exp), "ss")
            p.wait("v", "ss")
            p.inc("v", lambda v: v.tensor_scalar(
                junk2[:], rem[:], thet[:], None, ALU.is_gt, ALU.add, accum_out=cntb[:]), "sv")
            p.wait("v", "sv")
            p.inc("v", lambda v: v.tensor_scalar(geb[:], cntb[:], 1024.0, None, ALU.is_ge), "sv")
            p.wait("v", "sv")
            p.add("v", lambda v: v.tensor_tensor(cdl[:], tmid[:], tlo[:], ALU.subtract))
            p.add("v", lambda v: v.scalar_tensor_tensor(tlo[:], cdl[:], geb[:], tlo[:], ALU.mult, ALU.add))
            p.add("v", lambda v: v.tensor_tensor(cdh[:], thi[:], tmid[:], ALU.subtract))
            p.inc("v", lambda v: v.scalar_tensor_tensor(thi[:], cdh[:], geb[:], tmid[:], ALU.mult, ALU.add), "sv")
        p.wait("s", "sv")
        p.inc("s", lambda s: s.activation(thet[:], tlo[:], AF.Exp), "ss")
        p.wait("v", "ss")
        p.inc("v", lambda v: v.tensor_scalar(selA[:], rem[:], thet[:], None, ALU.is_gt), "sv")
        p.inc("v", lambda v: v.tensor_scalar(zind[:], rem[:], 0.0, None, ALU.is_le), "sv")
        p.wait("v", "sv")
        for c in range(8):
            sl = slice(256 * c, 256 * (c + 1))
            fn = lambda v, sl=sl: v.tensor_tensor_scan(
                zrank[:, sl], onesT[:], zind[:, sl], 0.0, ALU.mult, ALU.add)
            if c == 7:
                p.inc("v", fn, "sv")
            else:
                p.add("v", fn)
        p.wait("v", "sv")
        zrr = zrank.ap().rearrange("p (c l) -> p c l", l=256)
        p.inc("v", lambda v, zrr=zrr: v.tensor_copy(zlast[:], zrr[:, :, 255:256]), "sv")
        p.wait("v", "sv")
        p.inc("v", lambda v: v.tensor_tensor_scan(zG[:], ones8[:], zlast[:], 0.0, ALU.mult, ALU.add), "sv")
        p.wait("v", "sv")
        for c in range(1, 8):
            sl = slice(256 * c, 256 * (c + 1))
            p.add("v", lambda v, sl=sl, c=c: v.tensor_scalar(
                zrank[:, sl], zrank[:, sl], zG[:, c - 1:c], None, ALU.add))
        p.add("v", lambda v: v.tensor_scalar(Ke[:], nnz[:], -1.0, 1024.0, ALU.mult, ALU.add))
        p.inc("v", lambda v: v.tensor_scalar(Fb[:], nnz[:], 1024.0, None, ALU.is_ge), "sv")
        p.wait("v", "sv")
        p.add("v", lambda v: v.tensor_scalar(Fc[:], Fb[:], -1.0, 1.0, ALU.mult, ALU.add))
        p.inc("v", lambda v: v.tensor_scalar(selB[:], zrank[:], Ke[:], None, ALU.is_le), "sv")
        p.wait("v", "sv")
        p.inc("v", lambda v: v.tensor_tensor(selB[:], selB[:], zind[:], ALU.mult), "sv")
        p.inc("v", lambda v: v.tensor_scalar(zind[:], zind[:], -1.0, 1.0, ALU.mult, ALU.add), "sv")
        p.inc("v", lambda v: v.tensor_tensor(selB[:], selB[:], zind[:], ALU.add), "sv")
        p.wait("v", "sv")
        p.add("v", lambda v: v.tensor_scalar(selA[:], selA[:], Fb[:], None, ALU.mult))
        p.inc("v", lambda v: v.tensor_scalar(selB[:], selB[:], Fc[:], None, ALU.mult), "sv")
        p.wait("v", "sv")
        p.inc("v", lambda v: v.tensor_tensor(selA[:], selA[:], selB[:], ALU.add), "sv")
        p.wait("y", "sv")
        for b in range(NB):
            p.inc("y", lambda y, b=b: y.dma_start(selbs[b][:], selA[16 * b:16 * (b + 1), :]), "qx", 16)

        ln3_marks = {"zl": {0: 0, 1: 0}, "x1": {0: 0, 1: 0}, "o1": {0: 0, 1: 0}, "t2": 0}

        def emit_ln3(b, tis):
            p.wait("y", "qz", p.val("qz"))
            for ti in tis:
                par = ti % 2
                zl = zload0 if par == 0 else zload1
                x1b = x1t0 if par == 0 else x1t1
                ob = o10 if par == 0 else o11
                qz2 = "qxa" if par == 0 else "qxb"
                p.wait("y", "st", ln3_marks["zl"][par])
                p.inc("y", lambda y, b=b, ti=ti, zl=zl: y.dma_start(
                    zl.ap().rearrange("p (j c) -> p j c", j=J),
                    zfm_h[b, :, :, 128 * ti:128 * (ti + 1)]), qz2, 16)
                p.wait("y", "sv", ln3_marks["x1"][par])
                p.inc("y", lambda y, b=b, ti=ti, x1b=x1b: y.dma_start(
                    x1b[:], x1_h[b, 128 * ti:128 * (ti + 1), :]), qz2, 16)
                p.wait("t", qz2)
                p.wait("t", "ss", p.val("ss"))
                mm_group(p, [
                    (lambda t, j=j, zl=zl: t.transpose(
                        ps_t[:, 128 * j:128 * (j + 1)], zl[:, 128 * j:128 * (j + 1)], idb[:]))
                    for j in range(J)])
                ln3_marks["zl"][par] = p.val("st")
                p.wait("s", "st")
                p.wait("s", "sv", ln3_marks["t2"])
                p.inc("s", lambda s: s.copy(t2[:], ps_t[:]), "ss")
                p.wait("v", "ss")
                emit_ln(p, t2[:], znorm[:], B)
                ln3_marks["t2"] = p.val("sv")
                p.wait("v", qz2)
                p.wait("v", "qo" if par == 0 else "qo1", ln3_marks["o1"][par])
                p.add("v", lambda v, x1b=x1b, ob=ob: v.tensor_tensor(ob[:], x1b[:], bett[:], ALU.add))
                p.add("v", lambda v: v.tensor_tensor(znorm[:], znorm[:], gamt[:], ALU.mult))
                p.add("v", lambda v, ob=ob: v.tensor_tensor(ob[:], ob[:], znorm[:], ALU.add))
                ln3_marks["x1"][par] = p.val("sv")
                p.wait("y", "sv")
                p.inc("y", lambda y, b=b, ti=ti, ob=ob: y.dma_start(
                    out_h[b, 128 * ti:128 * (ti + 1), :], ob[:]), "qo" if par == 0 else "qo1", 16)
                ln3_marks["o1"][par] = p.val("qo" if par == 0 else "qo1")

        es2.close()
        ffn_marks = {"mask": 0, "maskg": 0, "gelu": {}, "w2": {}}
        p.wait("t", "ss", p.val("ss"))
        for eh in range(2):
            if eh == 1:
                for ee in range(8):
                    e_g = 8 + ee
                    p.wait("g", "st", p.val("st"))
                    p.inc("g", lambda g, ee=ee, e_g=e_g: g.dma_start(
                        w1s[:, ee, :, :], w1_h[e_g].rearrange("(kc p) f -> p kc f", p=128)), "qw", 16)
                    p.inc("g", lambda g, ee=ee, e_g=e_g: g.dma_start(
                        w2s[:, ee, :, :], w2_h[e_g].rearrange("(fc p) d -> p fc d", p=128)), "qw", 16)
            p.wait("t", "qw")
            for b in range(NB):
                p.wait("y", "st", p.val("st"))
                p.wait("y", "qx", p.val("qx"))
                p.inc("y", lambda y, b=b: y.dma_start(
                    h2s.ap().rearrange("p j t -> p (j t)"),
                    h2fm_h[b].rearrange("p j t -> p (j t)")), "qx", 16)
                p.wait("t", "qx")
                for tt in range(4):
                    tsl = slice(512 * tt, 512 * (tt + 1))
                    for ee in range(8):
                        e_g = eh * 8 + ee
                        for fc in range(2):
                            po = 512 * (2 * (ee % 2) + fc)
                            p.wait("t", "ss", ffn_marks["gelu"].get((ee % 2, fc), 0))
                            fns = [
                                (lambda t, ee=ee, kc=kc, fc=fc, po=po, tsl=tsl: t.matmul(
                                    ps_h[:, po:po + 512], w1s[:, ee, kc, 128 * fc:128 * (fc + 1)],
                                    h2s[:, kc, tsl], start=(kc == 0), stop=False))
                                for kc in range(J)]
                            fns.append(lambda t, b=b, e_g=e_g, po=po, tsl=tsl: t.matmul(
                                ps_h[:, po:po + 512], eyes[:, e_g, :], selbs[b][:, tsl],
                                start=False, stop=True))
                            mm_group(p, fns)
                            p.wait("s", "st")
                            p.inc("s", lambda s, ee=ee, fc=fc, po=po, e_g=e_g: s.activation(
                                Hm[:, ee, fc, :], ps_h[:, po:po + 512], AF.Gelu,
                                bias=b1s[:, fc, e_g:e_g + 1]), "ss")
                            ffn_marks["gelu"][(ee % 2, fc)] = p.val("ss")
                    for grp in range(4):
                        p.wait("t", "ss", p.val("ss"))
                        p.wait("t", "sv", ffn_marks.get("oevac", 0))
                        for dc4 in range(2):
                            dc = grp * 2 + dc4
                            po = 512 * dc4
                            fns = []
                            for ee in range(8):
                                for fc in range(2):
                                    first = (ee == 0 and fc == 0)
                                    last_plain = (ee == 7 and fc == 1 and eh == 0)
                                    fns.append(lambda t, ee=ee, fc=fc, dc=dc, po=po,
                                               first=first, last_plain=last_plain: t.matmul(
                                        ps_o[:, po:po + 512],
                                        w2s[:, ee, fc, 128 * dc:128 * (dc + 1)],
                                        Hm[:, ee, fc, :],
                                        start=first, stop=last_plain))
                            if eh == 1:
                                fns.append(lambda t, b=b, dc=dc, po=po, tsl=tsl: t.matmul(
                                    ps_o[:, po:po + 512], b2s[:, 128 * dc:128 * (dc + 1)],
                                    selbs[b][:, tsl], start=False, stop=True))
                            mm_group(p, fns)
                        for dc4 in range(2):
                            dc = grp * 2 + dc4
                            po = 512 * dc4
                            if eh == 0:
                                p.wait("v", "st")
                                p.wait("v", "qz", p.val("qz"))
                                p.inc("v", lambda v, po=po: v.tensor_copy(ztile[:], ps_o[:, po:po + 512]), "sv")
                                ffn_marks["oevac"] = p.val("sv")
                                p.wait("y", "sv")
                                p.inc("y", lambda y, b=b, dc=dc, tt=tt: y.dma_start(
                                    zp_h[b, :, dc, 512 * tt:512 * (tt + 1)], ztile[:]), "qz", 16)
                            else:
                                p.wait("y", "sv", p.val("sv"))
                                p.wait("y", "qz", p.val("qz"))
                                p.inc("y", lambda y, b=b, dc=dc, tt=tt: y.dma_start(
                                    ztile[:], zp_h[b, :, dc, 512 * tt:512 * (tt + 1)]), "qz", 16)
                                p.wait("v", "qz")
                                p.wait("v", "st")
                                p.inc("v", lambda v, po=po: v.tensor_tensor(
                                    ztile[:], ztile[:], ps_o[:, po:po + 512], ALU.add), "sv")
                                ffn_marks["oevac"] = p.val("sv")
                                p.wait("v", "sv")
                                p.wait("v", "qz", p.val("qz"))
                                p.inc("v", lambda v, dc=dc, tsl=tsl: v.tensor_tensor(
                                    ztmp[:], ztile[:], h2s[:, dc, tsl], ALU.add), "sv")
                                p.wait("y", "sv")
                                p.inc("y", lambda y, b=b, dc=dc, tt=tt: y.dma_start(
                                    zfm_h[b, :, dc, 512 * tt:512 * (tt + 1)], ztmp[:]), "qz", 16)
                if eh == 1 and tt == 3:
                    emit_ln3(b, range(NTT))

        p.wait("y", "qo")
        p.wait("y", "qo1")
        p.replay(block)
    return nc


_NC_CACHE = {}


def _eyee():
    import ml_dtypes
    a = np.zeros((16, 16, 128), np.float32)
    for e in range(16):
        a[e, e, :] = 256.0
    return a.astype(ml_dtypes.bfloat16)


def kernel(L0_attn_x, L0_attn_time_decay, L0_attn_time_first, L0_attn_time_mix_k,
           L0_attn_time_mix_v, L0_attn_time_mix_r, L0_attn_Wk, L0_attn_Wv,
           L0_attn_Wr, L0_attn_Wo, L0_ffn_wg, L0_ffn_bg, L0_ffn_w1, L0_ffn_b1,
           L0_ffn_w2, L0_ffn_b2, L0_ffn_gamma, L0_ffn_beta):
    f = np.float32
    cc = np.ascontiguousarray
    x = cc(np.asarray(L0_attn_x, f))
    ident = np.eye(128, dtype=f)
    import ml_dtypes as _mld
    iota1 = np.tile(np.arange(1, 257, dtype=f)[None, :], (128, 1)).astype(_mld.bfloat16)

    if "l1" not in _NC_CACHE:
        _NC_CACHE["l1"] = build_l1()
        _NC_CACHE["l2"] = build_l2()
    nc1, nc2 = _NC_CACHE["l1"], _NC_CACHE["l2"]

    common1 = {
        "wkT": cc(np.asarray(L0_attn_Wk, f).T), "wvT": cc(np.asarray(L0_attn_Wv, f).T),
        "wrT": cc(np.asarray(L0_attn_Wr, f).T), "woT": cc(np.asarray(L0_attn_Wo, f).T),
        "wg": cc(np.asarray(L0_ffn_wg, f)), "bg": cc(np.asarray(L0_ffn_bg, f).reshape(1, E)),
        "tdec": cc(np.asarray(L0_attn_time_decay, f).reshape(C)),
        "tfirst": cc(np.asarray(L0_attn_time_first, f).reshape(C)),
        "tmk": cc(np.asarray(L0_attn_time_mix_k, f).reshape(C)),
        "tmv": cc(np.asarray(L0_attn_time_mix_v, f).reshape(C)),
        "tmr": cc(np.asarray(L0_attn_time_mix_r, f).reshape(C)),
        "ident": ident, "iota1": iota1,
    }
    in1 = [dict(common1, x=cc(x[4 * i:4 * i + 4])) for i in range(8)]
    res1 = run_bass_kernel_spmd(nc1, in1, core_ids=list(range(8)))
    mall = cc(np.concatenate([res1.results[i]["maskedsc"] for i in range(8)], axis=0))

    common2 = {
        "mall": mall,
        "w1": cc(np.asarray(L0_ffn_w1, f)), "w2": cc(np.asarray(L0_ffn_w2, f)),
        "b1": cc(np.asarray(L0_ffn_b1, f).reshape(E, 2, 128).transpose(2, 1, 0)),
        "b2": cc(np.asarray(L0_ffn_b2, f)),
        "eyee": _eyee(), "gam": cc(np.asarray(L0_ffn_gamma, f).reshape(1, C)),
        "bet": cc(np.asarray(L0_ffn_beta, f).reshape(1, C)),
        "ident": ident,
    }
    in2 = [dict(common2, x1=res1.results[i]["x1"], h2fm=res1.results[i]["h2fm"],
                mymsk=res1.results[i]["maskedsc"]) for i in range(8)]
    res2 = run_bass_kernel_spmd(nc2, in2, core_ids=list(range(8)))
    out = np.concatenate([res2.results[i]["out"] for i in range(8)], axis=0)
    return out.astype(np.float32)


# version 21
# speedup vs baseline: 1.0170x; 1.0027x over previous
import numpy as np
import contextlib
import concourse.bass as bass
import concourse.mybir as mybir
from concourse.bass_utils import run_bass_kernel_spmd

F32 = mybir.dt.float32
BF16 = mybir.dt.bfloat16
AF = mybir.ActivationFunctionType
ALU = mybir.AluOpType
AX = mybir.AxisListType

NB, T, C, J = 4, 2048, 1024, 8
TH, NCK, E, FP = 1024, 4, 16, 256
LN_EPS, EPS_GATE = 1e-5, 1e-6
NTT = T // 128


class Prog:
    def __init__(self, nc):
        self.nc = nc
        self.ops = {k: [] for k in ("v", "s", "t", "g", "y")}
        self.sems, self.cnt = {}, {}

    def sem(self, es, name):
        self.sems[name] = es.enter_context(self.nc.semaphore(name))
        self.cnt[name] = 0

    AUTO = {"v": "sv", "s": "ss", "g": "sg"}

    def add(self, eng, fn, dep=True):
        if eng in self.AUTO:
            sem = self.AUTO[eng]
            h = self.sems[sem]
            pre = self.cnt[sem]
            self.cnt[sem] += 1
            def wrapped(e, fn=fn, h=h, pre=pre, dep=dep):
                if dep and pre > 0:
                    e.wait_ge(h, pre)
                fn(e).then_inc(h, 1)
            self.ops[eng].append(wrapped)
        else:
            self.ops[eng].append(fn)

    def inc(self, eng, fn, sem, n=1):
        if eng in self.AUTO and sem == self.AUTO[eng]:
            self.add(eng, fn)
            return
        h = self.sems[sem]
        self.cnt[sem] += n
        self.ops[eng].append(lambda e, fn=fn, h=h, n=n: fn(e).then_inc(h, n))

    def wait(self, eng, sem, val=None):
        h = self.sems[sem]
        v = self.cnt[sem] if val is None else val
        if v > 0:
            self.ops[eng].append(lambda e, h=h, v=v: e.wait_ge(h, v))

    def val(self, sem):
        return self.cnt[sem]

    def replay(self, block):
        ops = self.ops

        @block.vector
        def _(v):
            for f in ops["v"]:
                f(v)

        @block.scalar
        def _(s):
            for f in ops["s"]:
                f(s)

        @block.tensor
        def _(t):
            for f in ops["t"]:
                f(t)

        @block.gpsimd
        def _(g):
            for f in ops["g"]:
                f(g)

        @block.sync
        def _(y):
            for f in ops["y"]:
                f(y)


def emit_ln(p, x_ap, out_ap, B):
    bst, mv, rstd = B["bst"], B["mv"], B["rstd"]
    p.add("v", lambda v: v.bn_stats(bst[:, 0:6], x_ap[:, 0:512]))
    p.add("v", lambda v: v.bn_stats(bst[:, 6:12], x_ap[:, 512:1024]))
    p.add("v", lambda v: v.bn_aggr(mv[:], bst[:]))
    p.wait("s", "sv")
    p.add("s", lambda s: s.activation(rstd[:], mv[:, 1:2], AF.Ln, bias=B["epst"][:]))
    p.add("s", lambda s: s.activation(rstd[:], rstd[:], AF.Exp, scale=-0.5))
    p.wait("v", "ss")
    p.add("v", lambda v: v.tensor_scalar(out_ap, x_ap, mv[:, 0:1], rstd[:], ALU.subtract, ALU.mult))


def mm_group(p, fns):
    for i, fn in enumerate(fns):
        if i == len(fns) - 1:
            p.inc("t", fn, "st")
        else:
            p.add("t", fn)


def build_l1():
    nc = bass.Bass("TRN2", target_bir_lowering=False)
    x_h = nc.dram_tensor("x", [NB, T, C], F32, kind="ExternalInput").ap()
    wkT_h = nc.dram_tensor("wkT", [C, C], F32, kind="ExternalInput").ap()
    wvT_h = nc.dram_tensor("wvT", [C, C], F32, kind="ExternalInput").ap()
    wrT_h = nc.dram_tensor("wrT", [C, C], F32, kind="ExternalInput").ap()
    woT_h = nc.dram_tensor("woT", [C, C], F32, kind="ExternalInput").ap()
    wg_h = nc.dram_tensor("wg", [C, E], F32, kind="ExternalInput").ap()
    bg_h = nc.dram_tensor("bg", [1, E], F32, kind="ExternalInput").ap()
    td_h = nc.dram_tensor("tdec", [C], F32, kind="ExternalInput").ap()
    tf_h = nc.dram_tensor("tfirst", [C], F32, kind="ExternalInput").ap()
    tmk_h = nc.dram_tensor("tmk", [C], F32, kind="ExternalInput").ap()
    tmv_h = nc.dram_tensor("tmv", [C], F32, kind="ExternalInput").ap()
    tmr_h = nc.dram_tensor("tmr", [C], F32, kind="ExternalInput").ap()
    ident_h = nc.dram_tensor("ident", [128, 128], F32, kind="ExternalInput").ap()
    iota1_h = nc.dram_tensor("iota1", [128, 256], BF16, kind="ExternalInput").ap()

    x1_h = nc.dram_tensor("x1", [NB, T, C], F32, kind="ExternalOutput").ap()
    h2fm_h = nc.dram_tensor("h2fm", [NB, 128, J, T], BF16, kind="ExternalOutput").ap()
    msk_h = nc.dram_tensor("maskedsc", [NB, T, E], F32, kind="ExternalOutput").ap()

    p = Prog(nc)
    es = contextlib.ExitStack()
    with es:
        sb = lambda n, sh, dt=BF16: es.enter_context(nc.sbuf_tensor(n, sh, dt))
        ps_ = lambda n, sh, dt=F32: es.enter_context(nc.psum_tensor(n, sh, dt))

        wk = sb("wk", [128, J, C])
        wv = sb("wv", [128, J, C])
        wr = sb("wr", [128, J, C])
        wo = sb("wo", [128, J, C])
        wgs = sb("wgs", [128, J, E])
        bgrow = sb("bgrow", [1, E])
        ones_col = sb("ones_col", [1, 128])
        idb = sb("idb", [128, 128])
        tmks = sb("tmks", [128, J], F32)
        tmvs = sb("tmvs", [128, J], F32)
        tmrs = sb("tmrs", [128, J], F32)
        us = sb("us", [128, J], F32)
        wpos = sb("wpos", [128, J], F32)
        lamall = sb("lamall", [128, J], F32)
        ewj = sb("ewj", [128, 256], F32)
        wneg = sb("wneg", [128, J], F32)
        iota1 = sb("iota1s", [128, 256])

        xt = sb("xt", [128, C], F32)
        htile = sb("htile", [128, C])
        B = dict(bst=sb("bst", [128, 12], F32), mv=sb("mv", [128, 2], F32),
                 rstd=sb("rstd", [128, 1], F32), epst=sb("epst", [128, 1], F32))
        hfm = sb("hfm", [128, J, T + 1])
        blk = sb("blk", [128, J, TH])
        blv = sb("blv", [128, J, TH])
        blr = sb("blr", [128, J, TH])
        kpl = sb("kpl", [128, 2, TH])
        vpl = sb("vpl", [128, 2, TH])
        srpl = sb("srpl", [128, 2, TH])
        eks = sb("eks", [128, TH + 1])
        ekvs = sb("ekvs", [128, TH + 1])
        kap = sb("kap", [128, 1], F32)
        gp2 = sb("gp2", [128, 2], F32)
        gab_all = sb("gab_all", [128, J, 2], F32)
        lastBA = sb("lastBA", [128, 2 * NCK], F32)
        GBA = sb("GBA", [128, 2 * NCK], F32)
        lamq = sb("lamq", [128, J, 2 * NCK], F32)
        nk = sb("nk", [128, 3], F32)
        us3 = sb("us3", [128, J, 3], F32)
        ksc = sb("ksc", [128, 1], F32)
        ekl_all = sb("ekl_all", [128, J], F32)
        ekvl_all = sb("ekvl_all", [128, J], F32)
        EA = sb("EA", [128, TH])
        EB = sb("EB", [128, TH])
        bl_d = sb("bl_d", [128, 512]).ap()
        euk = sb("euk", [128, TH])
        rwkv = sb("rwkv", [128, J, TH])
        afm = sb("afm", [128, J, 512])
        atile = sb("atile", [128, C])
        x1t = sb("x1t", [128, C], F32)
        h2t = atile
        stag3 = htile.ap().rearrange("p (j c) -> p j c", j=J)
        scst = sb("scst", [128, E], F32)
        mx8 = sb("mx8", [128, 8], F32)
        nmx = sb("nmx", [128, 1], F32)
        ssum = sb("ssum", [128, 1], F32)
        rsum = sb("rsum", [128, 1], F32)
        maskt = sb("maskt", [128, E], F32)
        mskd = sb("mskd", [128, E], F32)

        ps_proj = ps_("ps_proj", [128, 1024])
        ps_tp = ps_("ps_tp", [128, 2048], BF16)
        ps_wo = ps_("ps_wo", [128, 512])
        ps_gate = ps_("ps_gate", [128, E])

        for s_ in ("sv", "ss", "st", "sg", "qx", "qxa", "qxb", "qw", "qo", "qh"):
            p.sem(es, s_)
        block = es.enter_context(nc.Block())

        for w_s, w_hh in ((wk, wkT_h), (wv, wvT_h), (wr, wrT_h), (wo, woT_h)):
            p.inc("g", lambda g, a=w_s, b_=w_hh: g.dma_start(
                a[:], b_.rearrange("(kc p) d -> p kc d", p=128)), "qw", 16)
        p.inc("g", lambda g: g.dma_start(wgs[:], wg_h.rearrange("(kc p) e -> p kc e", p=128)), "qw", 16)
        p.inc("g", lambda g: g.dma_start(bgrow[:], bg_h[:]), "qw", 16)
        p.inc("g", lambda g: g.dma_start(idb[:], ident_h[:]), "qw", 16)
        for v_s, v_hh in ((tmks, tmk_h), (tmvs, tmv_h), (tmrs, tmr_h), (us, tf_h), (wpos, td_h)):
            p.inc("y", lambda y, a=v_s, b_=v_hh: y.dma_start(
                a[:], b_.rearrange("(j p) -> p j", p=128), allow_slow_non_contiguous=True), "qx", 16)
        p.inc("y", lambda y: y.dma_start(iota1[:], iota1_h[:]), "qx", 16)
        p.wait("s", "qx")
        p.inc("s", lambda s: s.activation(wpos[:], wpos[:], AF.Exp), "ss")
        p.wait("v", "ss")
        p.add("v", lambda v: v.memset(ones_col[:], 0))
        p.add("v", lambda v: v.tensor_scalar(ones_col[:], ones_col[:], 0.0, 1.0, ALU.mult, ALU.add))
        p.add("v", lambda v: v.memset(B["epst"][:], float(LN_EPS)))
        p.add("v", lambda v: v.memset(us3[:], 0))
        for j in range(J):
            p.add("v", lambda v, j=j: v.tensor_copy(us3[:, j, 1:2], us[:, j:j + 1]))
        for j in range(J):
            p.add("v", lambda v, j=j: v.memset(lamq[:, j, :], 0))
            p.add("v", lambda v, j=j: v.tensor_scalar(
                lamq[:, j, :], lamq[:, j, :], wpos[:, j:j + 1], None, ALU.subtract))
            p.inc("v", lambda v, j=j: v.memset(lamq[:, j, NCK:NCK + 1], -1e5), "sv")
        p.wait("s", "sv")
        for j in range(J):
            p.inc("s", lambda s, j=j: s.activation(lamq[:, j, :], lamq[:, j, :], AF.Exp, scale=256.0), "ss")
        p.inc("s", lambda s: s.activation(lamall[:], wpos[:], AF.Exp, scale=-1.0), "ss")
        p.inc("v", lambda v: v.tensor_scalar(
            wneg[:], wpos[:], -1.0, None, ALU.mult), "sv")
        wneg_mark = p.val("sv")
        p.wait("v", "ss")
        p.inc("v", lambda v: v.memset(hfm[:], 0), "sv")
        p.wait("t", "sv")
        p.wait("t", "qw")
        p.wait("y", "sv")

        bank_mark = {0: 0, 1: 0}
        tp_mark = {0: 0, 1: 0}
        wo_mark = [0]
        afm_free = [0]
        wkv_marks = {}
        last_wkv_sg = [0]
        xbuf_free = {0: 0, 1: 0}

        def emit_ln1_tile(b, ti):
            par = ti % 2
            xbuf = xt if par == 0 else x1t
            qsem = "qxa" if par == 0 else "qxb"
            p.wait("y", "sv", xbuf_free[par])
            p.inc("y", lambda y, b=b, ti=ti, xbuf=xbuf: y.dma_start(
                xbuf[:], x_h[b, 128 * ti:128 * (ti + 1), :]), qsem, 16)
            p.wait("v", qsem)
            p.wait("v", "st", p.val("st"))
            p.wait("v", "qh", p.val("qh"))
            emit_ln(p, xbuf[:], htile[:], B)
            xbuf_free[par] = p.val("sv")
            tpo = 1024 * par
            p.wait("t", "sv", p.val("sv"))
            p.wait("t", "ss", tp_mark[par])
            mm_group(p, [
                (lambda t, j=j, tpo=tpo: t.transpose(
                    ps_tp[:, tpo + 128 * j:tpo + 128 * (j + 1)],
                    htile[:, 128 * j:128 * (j + 1)], idb[:]))
                for j in range(J)])
            p.wait("s", "st")
            p.inc("s", lambda s, ti=ti, tpo=tpo: s.copy(
                hfm.ap()[:, :, 1 + 128 * ti:1 + 128 * (ti + 1)],
                ps_tp[:, tpo:tpo + 1024].rearrange("p (j c) -> p j c", j=J)), "ss")
            tp_mark[par] = p.val("ss")

        def emit_blends(b, h):
            p.wait("v", "st", p.val("st"))
            t0 = h * TH
            for tt in range(2):
                ts0 = t0 + 512 * tt
                osl = slice(512 * tt, 512 * (tt + 1))
                for j in range(J):
                    h_sl = hfm[:, j, 1 + ts0:1 + ts0 + 512]
                    hs_sl = hfm[:, j, ts0:ts0 + 512]
                    p.add("v", lambda v, h_sl=h_sl, hs_sl=hs_sl: v.tensor_tensor(
                        bl_d, h_sl, hs_sl, ALU.subtract))
                    p.add("v", lambda v, j=j, osl=osl, hs_sl=hs_sl: v.scalar_tensor_tensor(
                        blk[:, j, osl], bl_d, tmks[:, j:j + 1], hs_sl, ALU.mult, ALU.add))
                    p.add("v", lambda v, j=j, osl=osl, hs_sl=hs_sl: v.scalar_tensor_tensor(
                        blv[:, j, osl], bl_d, tmvs[:, j:j + 1], hs_sl, ALU.mult, ALU.add), dep=False)
                    p.add("v", lambda v, j=j, osl=osl, hs_sl=hs_sl: v.scalar_tensor_tensor(
                        blr[:, j, osl], bl_d, tmrs[:, j:j + 1], hs_sl, ALU.mult, ALU.add), dep=False)
            return p.val("sv")

        def emit_proj(step, bl_ready):
            h, j = divmod(step, J)
            par = step % 2
            pm = wkv_marks.get(step - 2)
            first = True
            for pi, (wmat, bl, dst, actf) in enumerate((
                    (wk, blk, kpl, AF.Copy), (wv, blv, vpl, AF.Copy),
                    (wr, blr, srpl, AF.Sigmoid))):
                for tt in range(2):
                    bank = (pi * 2 + tt) % 2
                    po = 512 * bank
                    p.wait("t", "ss", bank_mark[bank])
                    if first:
                        p.wait("t", "sv", bl_ready)
                        if pm:
                            p.wait("t", "sv", pm["sg"])
                            p.wait("t", "ss", pm["ss"])
                        first = False
                    mm_group(p, [
                        (lambda t, wmat=wmat, bl=bl, kc=kc, j=j, po=po, tt=tt: t.matmul(
                            ps_proj[:, po:po + 512],
                            wmat[:, kc, 128 * j:128 * (j + 1)],
                            bl[:, kc, 512 * tt:512 * (tt + 1)],
                            start=(kc == 0), stop=(kc == J - 1)))
                        for kc in range(J)])
                    p.wait("s", "st")
                    p.inc("s", lambda s, dst=dst, par=par, po=po, actf=actf, tt=tt: s.activation(
                        dst[:, par, 512 * tt:512 * (tt + 1)], ps_proj[:, po:po + 512], actf), "ss")
                    bank_mark[bank] = p.val("ss")

        def emit_wkv(step, rwkv_free=0):
            h, j = divmod(step, J)
            par = step % 2
            hf = h
            kp = kpl[:, par, :]
            vp = vpl[:, par, :]
            srp = srpl[:, par, :]
            p.wait("s", "sv", max(last_wkv_sg[0], wneg_mark))
            p.add("s", lambda s, j=j: s.activation(
                ewj[:], iota1[:], AF.Exp, scale=wneg[:, j:j + 1]))
            p.wait("v", "ss", bank_mark[0])
            p.wait("v", "ss", bank_mark[1])
            p.add("v", lambda v, kp=kp: v.tensor_reduce(kap[:], kp, axis=AX.X, op=ALU.max))
            if hf == 0:
                p.add("v", lambda v, j=j: v.tensor_copy(us3[:, j, 2:3], kap[:]))
                p.add("v", lambda v, j=j: v.tensor_scalar(
                    nk[:, 0:2], us3[:, j, 0:2], kap[:], None, ALU.subtract))
            else:
                p.add("v", lambda v, j=j: v.tensor_scalar(
                    nk[:], us3[:, j, :], kap[:], None, ALU.subtract))
            p.wait("s", "sv")
            if hf == 1:
                p.add("s", lambda s: s.activation(ksc[:], nk[:, 2:3], AF.Exp))
            p.add("s", lambda s, kp=kp: s.activation(eks[:, 1:], kp, AF.Exp, bias=nk.ap()[:, 0:1]))
            p.add("s", lambda s, kp=kp: s.activation(euk[:], kp, AF.Exp, bias=nk.ap()[:, 1:2]))
            p.wait("v", "ss")
            p.add("v", lambda v, vp=vp: v.tensor_tensor(ekvs[:, 1:], eks[:, 1:], vp, ALU.mult))
            if hf == 0:
                p.add("v", lambda v: v.memset(eks[:, 0:1], 0))
                p.add("v", lambda v: v.memset(ekvs[:, 0:1], 0))
                p.add("v", lambda v: v.memset(gp2[:], 0))
            else:
                p.add("v", lambda v, j=j: v.tensor_tensor(
                    eks[:, 0:1], ekl_all[:, j:j + 1], ksc[:], ALU.mult))
                p.add("v", lambda v, j=j: v.tensor_tensor(
                    ekvs[:, 0:1], ekvl_all[:, j:j + 1], ksc[:], ALU.mult))
                p.add("v", lambda v, j=j: v.tensor_scalar(
                    gp2[:], gab_all[:, j, :], ksc[:], None, ALU.mult))
            p.add("v", lambda v, j=j: v.tensor_copy(ekl_all[:, j:j + 1], eks[:, TH:TH + 1]))
            p.add("v", lambda v, j=j: v.tensor_copy(ekvl_all[:, j:j + 1], ekvs[:, TH:TH + 1]))
            lamb = lamall.ap()[:, j:j + 1].broadcast_to((128, 256))
            for c in range(NCK):
                sl = slice(256 * c, 256 * (c + 1))
                p.add("v", lambda v, sl=sl, lamb=lamb: v.tensor_tensor_scan(
                    EB[:, sl], lamb, eks[:, sl], 0.0, ALU.mult, ALU.add), dep=(c == 0))
            for c in range(NCK):
                sl = slice(256 * c, 256 * (c + 1))
                p.add("v", lambda v, sl=sl, lamb=lamb: v.tensor_tensor_scan(
                    EA[:, sl], lamb, ekvs[:, sl], 0.0, ALU.mult, ALU.add), dep=False)
            EBr = EB.ap().rearrange("p (c l) -> p c l", l=256)
            EAr = EA.ap().rearrange("p (c l) -> p c l", l=256)
            p.add("v", lambda v, EBr=EBr: v.tensor_copy(lastBA[:, 0:NCK], EBr[:, :, 255:256]))
            p.add("v", lambda v, EAr=EAr: v.tensor_copy(lastBA[:, NCK:2 * NCK], EAr[:, :, 255:256]))
            p.add("v", lambda v, j=j: v.scalar_tensor_tensor(
                lastBA[:, 0:1], lamq[:, j, 0:1], gp2.ap()[:, 0:1], lastBA[:, 0:1], ALU.mult, ALU.add))
            p.add("v", lambda v, j=j: v.scalar_tensor_tensor(
                lastBA[:, NCK:NCK + 1], lamq[:, j, 0:1], gp2.ap()[:, 1:2],
                lastBA[:, NCK:NCK + 1], ALU.mult, ALU.add))
            p.add("v", lambda v, j=j: v.tensor_tensor_scan(
                GBA[:], lamq[:, j, :], lastBA[:], 0.0, ALU.mult, ALU.add))
            p.add("v", lambda v, j=j: v.tensor_copy(
                gab_all[:, j, :], GBA.ap()[:, NCK - 1:2 * NCK:NCK]))
            for c in range(NCK):
                sl = slice(256 * c, 256 * (c + 1))
                p.add("v", lambda v, sl=sl, c=c: v.scalar_tensor_tensor(
                    EB[:, sl], ewj[:],
                    (gp2.ap()[:, 0:1] if c == 0 else GBA[:, c - 1:c]),
                    EB[:, sl], ALU.mult, ALU.add), dep=(c == 0))
            p.add("v", lambda v: v.tensor_tensor(EB[:], EB[:], euk[:], ALU.add))
            p.wait("s", "sv")
            p.add("s", lambda s: s.activation(EB[:], EB[:], AF.Ln))
            p.add("s", lambda s: s.activation(EB[:], EB[:], AF.Exp, scale=-1.0))
            for c in range(NCK):
                sl = slice(256 * c, 256 * (c + 1))
                p.add("v", lambda v, sl=sl, c=c: v.scalar_tensor_tensor(
                    EA[:, sl], ewj[:],
                    (gp2.ap()[:, 1:2] if c == 0 else GBA[:, NCK + c - 1:NCK + c]),
                    EA[:, sl], ALU.mult, ALU.add), dep=(c == 0))
            p.add("v", lambda v, vp=vp: v.tensor_tensor(euk[:], euk[:], vp, ALU.mult))
            p.add("v", lambda v: v.tensor_tensor(EA[:], EA[:], euk[:], ALU.add))
            p.wait("v", "ss")
            p.add("v", lambda v: v.tensor_tensor(EA[:], EA[:], EB[:], ALU.mult))
            p.wait("v", "st", rwkv_free)
            p.add("v", lambda v, j=j, srp=srp: v.tensor_tensor(
                rwkv[:, j, :], EA[:], srp, ALU.mult))
            wkv_marks[step] = dict(sv=p.val("sv"), ss=p.val("ss"), sg=p.val("sv"))
            last_wkv_sg[0] = p.val("sv")

        def emit_woq(b, h, qq):
            p.wait("t", "sv", wkv_marks[h * J + J - 1]["sg"])
            for m in range(J):
                p.wait("t", "ss", wo_mark[0])
                mm_group(p, [
                    (lambda t, kc=kc, m=m, qq=qq: t.matmul(
                        ps_wo[:], wo[:, kc, 128 * m:128 * (m + 1)],
                        rwkv[:, kc, 512 * qq:512 * (qq + 1)],
                        start=(kc == 0), stop=(kc == J - 1)))
                    for kc in range(J)])
                p.wait("s", "st")
                p.wait("s", "st", afm_free[0])
                p.inc("s", lambda s, m=m: s.copy(afm[:, m, :], ps_wo[:]), "ss")
                wo_mark[0] = p.val("ss")
            return p.val("st")

        def emit_epi_tile(b, h, qq, ti4):
            ti = h * 8 + qq * 4 + ti4
            par = ti4 % 2
            tpo = 1024 * par
            p.wait("t", "ss", wo_mark[0])
            p.wait("t", "ss", tp_mark[par])
            mm_group(p, [
                (lambda t, j=j, ti4=ti4, tpo=tpo: t.transpose(
                    ps_tp[:, tpo + 128 * j:tpo + 128 * (j + 1)],
                    afm[:, j, 128 * ti4:128 * (ti4 + 1)], idb[:]))
                for j in range(J)])
            afm_free[0] = p.val("st")
            p.wait("s", "st")
            p.inc("s", lambda s, tpo=tpo: s.copy(atile[:], ps_tp[:, tpo:tpo + 1024]), "ss")
            tp_mark[par] = p.val("ss")
            atile_ss = p.val("ss")
            p.wait("y", "sv", xbuf_free[0])
            p.inc("y", lambda y, b=b, ti=ti: y.dma_start(
                xt[:], x_h[b, 128 * ti:128 * (ti + 1), :]), "qxa", 16)
            p.wait("v", "qxa")
            p.wait("v", "ss", atile_ss)
            p.wait("v", "qo", p.val("qo"))
            p.add("v", lambda v: v.tensor_tensor(x1t[:], xt[:], atile[:], ALU.add))
            xbuf_free[0] = p.val("sv")
            p.wait("y", "sv")
            p.inc("y", lambda y, b=b, ti=ti: y.dma_start(
                x1_h[b, 128 * ti:128 * (ti + 1), :], x1t[:]), "qo", 16)
            p.wait("v", "st", p.val("st"))
            p.wait("v", "qh", p.val("qh"))
            emit_ln(p, x1t[:], h2t[:], B)
            h2t_sv = p.val("sv")
            tpo2 = 1024 * ((ti4 + 1) % 2)
            p.wait("t", "sv", h2t_sv)
            p.wait("t", "ss", tp_mark[(ti4 + 1) % 2])
            mm_group(p, [
                (lambda t, j=j, tpo2=tpo2: t.transpose(
                    ps_tp[:, tpo2 + 128 * j:tpo2 + 128 * (j + 1)],
                    h2t[:, 128 * j:128 * (j + 1)], idb[:]))
                for j in range(J)])
            p.wait("s", "st")
            p.wait("s", "qh", p.val("qh"))
            p.inc("s", lambda s, tpo2=tpo2: s.copy(
                htile.ap(), ps_tp[:, tpo2:tpo2 + 1024]), "ss")
            tp_mark[(ti4 + 1) % 2] = p.val("ss")
            stag_ss = p.val("ss")
            p.wait("y", "ss", stag_ss)
            p.inc("y", lambda y, b=b, ti=ti: y.dma_start(
                h2fm_h[b, :, :, 128 * ti:128 * (ti + 1)], stag3), "qh", 16)
            p.wait("t", "ss", stag_ss)
            fns = [(lambda t, j=j: t.matmul(
                ps_gate[:], stag3[:, j, :], wgs[:, j, :], start=(j == 0), stop=False))
                for j in range(J)]
            fns.append(lambda t: t.matmul(ps_gate[:], ones_col[:], bgrow[:],
                                          start=False, stop=True))
            mm_group(p, fns)
            p.wait("v", "st")
            p.add("v", lambda v: v.max(mx8[:], ps_gate[:]))
            p.add("v", lambda v: v.tensor_scalar(nmx[:], mx8[:, 0:1], -1.0, None, ALU.mult))
            p.wait("s", "sv")
            p.add("s", lambda s: s.activation(scst[:], ps_gate[:], AF.Exp, bias=nmx[:]))
            p.wait("v", "ss")
            p.add("v", lambda v: v.reduce_sum(ssum[:], scst[:], axis=AX.X))
            p.add("v", lambda v: v.reciprocal(rsum[:], ssum[:]))
            p.add("v", lambda v: v.tensor_scalar(scst[:], scst[:], rsum[:], None, ALU.mult))
            p.add("v", lambda v: v.max(mx8[:], scst[:]))
            p.add("v", lambda v: v.tensor_scalar(
                maskt[:], scst[:], mx8[:, 1:2], None, ALU.is_ge))
            p.wait("v", "qo", p.val("qo"))
            p.add("v", lambda v: v.tensor_tensor(mskd[:], scst[:], maskt[:], ALU.mult))
            p.wait("y", "sv")
            p.inc("y", lambda y, b=b, ti=ti: y.dma_start(
                msk_h[b, 128 * ti:128 * (ti + 1), :], mskd[:]), "qo", 16)

        rwkv_free_st = [0]
        for b in range(NB):
            for ti in range(NTT):
                emit_ln1_tile(b, ti)
            bl0 = emit_blends(b, 0)
            emit_proj(0, bl0)
            for j in range(1, 8):
                emit_wkv(j - 1, rwkv_free_st[0])
                emit_proj(j, bl0)
            bl1 = emit_blends(b, 1)
            emit_wkv(7, rwkv_free_st[0])
            emit_proj(8, bl1)
            emit_woq(b, 0, 0)
            emit_proj(9, bl1)
            for t4 in range(4):
                emit_epi_tile(b, 0, 0, t4)
            wo_st = emit_woq(b, 0, 1)
            rwkv_free_st[0] = wo_st
            emit_wkv(8, rwkv_free_st[0])
            emit_proj(10, bl1)
            emit_epi_tile(b, 0, 1, 0)
            emit_epi_tile(b, 0, 1, 1)
            emit_wkv(9, rwkv_free_st[0])
            emit_proj(11, bl1)
            emit_epi_tile(b, 0, 1, 2)
            emit_epi_tile(b, 0, 1, 3)
            emit_wkv(10, rwkv_free_st[0])
            emit_wkv(11, rwkv_free_st[0])
            emit_proj(12, bl1)
            emit_wkv(12, rwkv_free_st[0])
            emit_proj(13, bl1)
            emit_wkv(13, rwkv_free_st[0])
            emit_proj(14, bl1)
            emit_wkv(14, rwkv_free_st[0])
            emit_proj(15, bl1)
            emit_wkv(15, rwkv_free_st[0])
            emit_woq(b, 1, 0)
            for t4 in range(4):
                emit_epi_tile(b, 1, 0, t4)
            wo_st = emit_woq(b, 1, 1)
            rwkv_free_st[0] = wo_st
            for t4 in range(4):
                emit_epi_tile(b, 1, 1, t4)
            wkv_marks.clear()

        p.wait("y", "qo")
        p.wait("y", "qh")
        p.replay(block)
    return nc


def build_l2():
    nc = bass.Bass("TRN2", target_bir_lowering=False)
    x1_h = nc.dram_tensor("x1", [NB, T, C], F32, kind="ExternalInput").ap()
    h2fm_h = nc.dram_tensor("h2fm", [NB, 128, J, T], BF16, kind="ExternalInput").ap()
    mall_h = nc.dram_tensor("mall", [32, T, E], F32, kind="ExternalInput").ap()
    mymsk_h = nc.dram_tensor("mymsk", [NB, T, E], F32, kind="ExternalInput").ap()
    w1_h = nc.dram_tensor("w1", [E, C, FP], F32, kind="ExternalInput").ap()
    w2_h = nc.dram_tensor("w2", [E, FP, C], F32, kind="ExternalInput").ap()
    b1_h = nc.dram_tensor("b1", [128, 2, E], F32, kind="ExternalInput").ap()
    b2_h = nc.dram_tensor("b2", [E, C], F32, kind="ExternalInput").ap()
    gam_h = nc.dram_tensor("gam", [1, C], F32, kind="ExternalInput").ap()
    bet_h = nc.dram_tensor("bet", [1, C], F32, kind="ExternalInput").ap()
    ident_h = nc.dram_tensor("ident", [128, 128], F32, kind="ExternalInput").ap()
    eyee_h = nc.dram_tensor("eyee", [16, 16, 128], BF16, kind="ExternalInput").ap()

    out_h = nc.dram_tensor("out", [NB, T, C], F32, kind="ExternalOutput").ap()
    zp_h = nc.dram_tensor("zpart", [NB, 128, J, T], F32, kind="Internal").ap()
    zfm_h = nc.dram_tensor("zfm", [NB, 128, J, T], BF16, kind="Internal").ap()

    p = Prog(nc)
    es = contextlib.ExitStack()
    with es:
        sb = lambda n, sh, dt=BF16: es.enter_context(nc.sbuf_tensor(n, sh, dt))
        ps_ = lambda n, sh, dt=F32: es.enter_context(nc.psum_tensor(n, sh, dt))

        idb = sb("idb", [128, 128])
        idf = sb("idf", [128, 128], F32)
        ones_col = sb("ones_col", [1, 128])
        gamt = sb("gamt", [128, C])
        bett = sb("bett", [128, C])
        gbrow = sb("gbrow", [1, C])
        b2s = sb("b2s", [16, C])
        b1s = sb("b1s", [128, 2, E], F32)
        masksb = sb("masksb", [128, 512])
        selbs = [sb(f"selb{b}", [16, T]) for b in range(NB)]
        eyes = sb("eyes", [16, 16, 128])
        zload0 = sb("zload0", [128, C])
        zload1 = sb("zload1", [128, C])
        znorm = sb("znorm", [128, C])
        x1t0 = sb("x1t0", [128, C], F32)
        x1t1 = sb("x1t1", [128, C], F32)
        o10 = sb("o10", [128, C], F32)
        o11 = sb("o11", [128, C], F32)
        t2 = sb("t2", [128, C])
        ztile = sb("ztile", [128, 512], F32)
        ztmp = sb("ztmp", [128, 512])
        B = dict(bst=sb("bst", [128, 12], F32), mv=sb("mv", [128, 2], F32),
                 rstd=sb("rstd", [128, 1], F32), epst=sb("epst", [128, 1], F32))
        h2s = sb("h2s", [128, J, T])
        Hm = sb("Hm", [128, 8, 2, 512])
        w1s = sb("w1s", [128, 8, J, FP])
        w2s = sb("w2s", [128, 8, 2, C])
        es2 = contextlib.ExitStack()
        sb2 = lambda n, sh, dt=BF16: es2.enter_context(nc.sbuf_tensor(n, sh, dt))
        mt32 = sb2("mt32", [128, 32, E], F32)
        dsum = sb2("dsum", [128, E], F32)
        rdt = sb2("rdt", [128, NTT, E], F32)
        routf = sb2("routf", [128, NTT, E], F32)
        rem = sb2("rem", [64, T], F32)
        rstage = sb2("rstage", [16, T], F32)
        nnz = sb2("nnz", [64, 1], F32)
        mxr = sb2("mxr", [64, 1], F32)
        tlo = sb2("tlo", [64, 1], F32)
        thi = sb2("thi", [64, 1], F32)
        tmid = sb2("tmid", [64, 1], F32)
        thet = sb2("thet", [64, 1], F32)
        cntb = sb2("cntb", [64, 1], F32)
        geb = sb2("geb", [64, 1], F32)
        cdl = sb2("cdl", [64, 1], F32)
        cdh = sb2("cdh", [64, 1], F32)
        selA = sb2("selA", [64, T])
        junk2 = selA
        zind = sb2("zind", [64, T])
        zrank = sb2("zrank", [64, T], mybir.dt.float16)
        zlast = sb2("zlast", [64, J], F32)
        zG = sb2("zG", [64, J], F32)
        onesT = sb2("onesT", [64, 256], F32)
        ones8 = sb2("ones8", [64, J], F32)
        Ke = sb2("Ke", [64, 1], F32)
        Fb = sb2("Fb", [64, 1], F32)
        Fc = sb2("Fc", [64, 1], F32)
        selB = sb2("selB", [64, T])

        ps_h = ps_("ps_h", [128, 2048])
        ps_o = ps_("ps_o", [128, 1024])
        ps_x = ps_("ps_x", [128, 512])
        ps_t = ps_("ps_t", [128, 1024], BF16)

        for s_ in ("sv", "ss", "st", "sg", "qx", "qxa", "qxb", "qw", "qb", "qo", "qo1", "qz"):
            p.sem(es, s_)
        block = es.enter_context(nc.Block())

        p.inc("g", lambda g: g.dma_start(idb[:], ident_h[:]), "qw", 16)
        p.inc("y", lambda y: y.dma_start(idf[:], ident_h[:]), "qx", 16)
        p.inc("g", lambda g: g.dma_start(b2s[:], b2_h[:]), "qw", 16)
        p.inc("y", lambda y: y.dma_start(eyes[:], eyee_h[:]), "qx", 16)
        p.inc("g", lambda g: g.dma_start(b1s[:], b1_h[:]), "qb", 16)
        p.wait("v", "qb")
        p.inc("v", lambda v: v.tensor_scalar(
            b1s.ap().rearrange("p a b -> p (a b)"), b1s.ap().rearrange("p a b -> p (a b)"),
            -256.0, None, ALU.add), "sv")
        p.wait("v", "qx")
        p.add("v", lambda v: v.memset(ones_col[:], 0))
        p.add("v", lambda v: v.tensor_scalar(ones_col[:], ones_col[:], 0.0, 1.0, ALU.mult, ALU.add))
        p.add("v", lambda v: v.memset(onesT[:], 0))
        p.add("v", lambda v: v.tensor_scalar(onesT[:], onesT[:], 0.0, 1.0, ALU.mult, ALU.add))
        p.add("v", lambda v: v.memset(B["epst"][:], float(LN_EPS)))
        p.add("v", lambda v: v.memset(ones8[:], 0))
        p.inc("v", lambda v: v.tensor_scalar(ones8[:], ones8[:], 0.0, 1.0, ALU.mult, ALU.add), "sv")
        p.wait("t", "sv", p.val("sv"))
        for src_h, dst, dty in ((gam_h, gamt, "bf"), (bet_h, bett, "f32")):
            p.inc("g", lambda g, src_h=src_h: g.dma_start(gbrow[:], src_h[:]), "qw", 16)
            p.wait("t", "qw")
            for half in range(2):
                p.wait("t", "ss", p.val("ss"))
                p.inc("t", lambda t, half=half: t.matmul(
                    ps_x[:], ones_col[:], gbrow[:, 512 * half:512 * (half + 1)],
                    start=True, stop=True), "st")
                p.wait("s", "st")
                p.inc("s", lambda s, dst=dst, half=half: s.copy(
                    dst[:, 512 * half:512 * (half + 1)], ps_x[:]), "ss")
            p.wait("g", "ss")

        for ee in range(8):
            p.inc("g", lambda g, ee=ee: g.dma_start(
                w1s[:, ee, :, :], w1_h[ee].rearrange("(kc p) f -> p kc f", p=128)), "qw", 16)
            p.inc("g", lambda g, ee=ee: g.dma_start(
                w2s[:, ee, :, :], w2_h[ee].rearrange("(fc p) d -> p fc d", p=128)), "qw", 16)
        for ti in range(NTT):
            p.wait("y", "sv", p.val("sv"))
            p.inc("y", lambda y, ti=ti: y.dma_start(
                mt32[:], mall_h[:, 128 * ti:128 * (ti + 1), :].rearrange("b p e -> p b e")), "qx", 16)
            p.wait("v", "qx")
            for lvl in (16, 8, 4, 2, 1):
                p.add("v", lambda v, lvl=lvl: v.tensor_tensor(
                    mt32.ap()[:, 0:lvl, :].rearrange("p b e -> p (b e)"),
                    mt32.ap()[:, 0:lvl, :].rearrange("p b e -> p (b e)"),
                    mt32.ap()[:, lvl:2 * lvl, :].rearrange("p b e -> p (b e)"), ALU.add))
            p.add("v", lambda v: v.tensor_scalar(dsum[:], mt32[:, 0, :], EPS_GATE, None, ALU.add))
            p.add("v", lambda v, ti=ti: v.reciprocal(rdt[:, ti, :], dsum[:]))
        for b in range(NB):
            p.wait("y", "sv", p.val("sv"))
            p.wait("y", "st", p.val("st"))
            p.inc("y", lambda y, b=b: y.dma_start(
                routf[:], mymsk_h[b].rearrange("(ti p) e -> p ti e", p=128)), "qx", 16)
            p.wait("v", "qx")
            for ti in range(NTT):
                p.add("v", lambda v, ti=ti: v.scalar_tensor_tensor(
                    routf[:, ti, :], routf[:, ti, :], 32.0, rdt[:, ti, :],
                    ALU.mult, ALU.mult), dep=(ti == 0))
            p.wait("t", "sv")
            for g8 in range(2):
                p.wait("t", "ss", p.val("ss"))
                mm_group(p, [
                    (lambda t, ti=ti, k=k: t.transpose(
                        ps_h.ap()[0:16, 128 * k:128 * (k + 1)], routf[:, ti, :], idf[:]))
                    for k, ti in enumerate(range(8 * g8, 8 * g8 + 8))])
                p.wait("s", "st")
                p.inc("s", lambda s, g8=g8: s.copy(
                    rstage[:, 1024 * g8:1024 * (g8 + 1)], ps_h.ap()[0:16, 0:1024]), "ss")
            p.wait("y", "ss")
            p.inc("y", lambda y, b=b: y.dma_start(rem[16 * b:16 * (b + 1), :], rstage[:]), "qx", 16)
        p.wait("v", "qx")
        p.wait("v", "ss")
        p.inc("v", lambda v: v.tensor_scalar(
            junk2[:], rem[:], 0.0, None, ALU.is_gt, ALU.add, accum_out=nnz[:]), "sv")
        p.add("v", lambda v: v.tensor_reduce(mxr[:], rem[:], axis=AX.X, op=ALU.max))
        p.wait("v", "sv")
        p.inc("v", lambda v: v.tensor_scalar(mxr[:], mxr[:], 1e-30, None, ALU.max), "sv")
        p.wait("s", "sv")
        p.inc("s", lambda s: s.activation(thi[:], mxr[:], AF.Ln), "ss")
        p.wait("v", "ss")
        p.add("v", lambda v: v.tensor_scalar(thi[:], thi[:], 0.01, None, ALU.add))
        p.add("v", lambda v: v.memset(tlo[:], 0))
        p.inc("v", lambda v: v.tensor_scalar(tlo[:], tlo[:], 0.0, -110.0, ALU.mult, ALU.add), "sv")
        for it in range(44):
            p.wait("v", "sv", p.val("sv"))
            p.add("v", lambda v: v.tensor_tensor(tmid[:], tlo[:], thi[:], ALU.add))
            p.inc("v", lambda v: v.tensor_scalar(tmid[:], tmid[:], 0.5, None, ALU.mult), "sv")
            p.wait("s", "sv")
            p.inc("s", lambda s: s.activation(thet[:], tmid[:], AF.Exp), "ss")
            p.wait("v", "ss")
            p.inc("v", lambda v: v.tensor_scalar(
                junk2[:], rem[:], thet[:], None, ALU.is_gt, ALU.add, accum_out=cntb[:]), "sv")
            p.wait("v", "sv")
            p.inc("v", lambda v: v.tensor_scalar(geb[:], cntb[:], 1024.0, None, ALU.is_ge), "sv")
            p.wait("v", "sv")
            p.add("v", lambda v: v.tensor_tensor(cdl[:], tmid[:], tlo[:], ALU.subtract))
            p.add("v", lambda v: v.scalar_tensor_tensor(tlo[:], cdl[:], geb[:], tlo[:], ALU.mult, ALU.add))
            p.add("v", lambda v: v.tensor_tensor(cdh[:], thi[:], tmid[:], ALU.subtract))
            p.inc("v", lambda v: v.scalar_tensor_tensor(thi[:], cdh[:], geb[:], tmid[:], ALU.mult, ALU.add), "sv")
        p.wait("s", "sv")
        p.inc("s", lambda s: s.activation(thet[:], tlo[:], AF.Exp), "ss")
        p.wait("v", "ss")
        p.inc("v", lambda v: v.tensor_scalar(selA[:], rem[:], thet[:], None, ALU.is_gt), "sv")
        p.inc("v", lambda v: v.tensor_scalar(zind[:], rem[:], 0.0, None, ALU.is_le), "sv")
        p.wait("v", "sv")
        for c in range(8):
            sl = slice(256 * c, 256 * (c + 1))
            fn = lambda v, sl=sl: v.tensor_tensor_scan(
                zrank[:, sl], onesT[:], zind[:, sl], 0.0, ALU.mult, ALU.add)
            if c == 7:
                p.inc("v", fn, "sv")
            else:
                p.add("v", fn)
        p.wait("v", "sv")
        zrr = zrank.ap().rearrange("p (c l) -> p c l", l=256)
        p.inc("v", lambda v, zrr=zrr: v.tensor_copy(zlast[:], zrr[:, :, 255:256]), "sv")
        p.wait("v", "sv")
        p.inc("v", lambda v: v.tensor_tensor_scan(zG[:], ones8[:], zlast[:], 0.0, ALU.mult, ALU.add), "sv")
        p.wait("v", "sv")
        for c in range(1, 8):
            sl = slice(256 * c, 256 * (c + 1))
            p.add("v", lambda v, sl=sl, c=c: v.tensor_scalar(
                zrank[:, sl], zrank[:, sl], zG[:, c - 1:c], None, ALU.add))
        p.add("v", lambda v: v.tensor_scalar(Ke[:], nnz[:], -1.0, 1024.0, ALU.mult, ALU.add))
        p.inc("v", lambda v: v.tensor_scalar(Fb[:], nnz[:], 1024.0, None, ALU.is_ge), "sv")
        p.wait("v", "sv")
        p.add("v", lambda v: v.tensor_scalar(Fc[:], Fb[:], -1.0, 1.0, ALU.mult, ALU.add))
        p.inc("v", lambda v: v.tensor_scalar(selB[:], zrank[:], Ke[:], None, ALU.is_le), "sv")
        p.wait("v", "sv")
        p.inc("v", lambda v: v.tensor_tensor(selB[:], selB[:], zind[:], ALU.mult), "sv")
        p.inc("v", lambda v: v.tensor_scalar(zind[:], zind[:], -1.0, 1.0, ALU.mult, ALU.add), "sv")
        p.inc("v", lambda v: v.tensor_tensor(selB[:], selB[:], zind[:], ALU.add), "sv")
        p.wait("v", "sv")
        p.add("v", lambda v: v.tensor_scalar(selA[:], selA[:], Fb[:], None, ALU.mult))
        p.inc("v", lambda v: v.tensor_scalar(selB[:], selB[:], Fc[:], None, ALU.mult), "sv")
        p.wait("v", "sv")
        p.inc("v", lambda v: v.tensor_tensor(selA[:], selA[:], selB[:], ALU.add), "sv")
        p.wait("y", "sv")
        for b in range(NB):
            p.inc("y", lambda y, b=b: y.dma_start(selbs[b][:], selA[16 * b:16 * (b + 1), :]), "qx", 16)

        ln3_marks = {"zl": {0: 0, 1: 0}, "x1": {0: 0, 1: 0}, "o1": {0: 0, 1: 0}, "t2": 0}

        def emit_ln3(b, tis):
            p.wait("y", "qz", p.val("qz"))
            for ti in tis:
                par = ti % 2
                zl = zload0 if par == 0 else zload1
                x1b = x1t0 if par == 0 else x1t1
                ob = o10 if par == 0 else o11
                qz2 = "qxa" if par == 0 else "qxb"
                p.wait("y", "st", ln3_marks["zl"][par])
                p.inc("y", lambda y, b=b, ti=ti, zl=zl: y.dma_start(
                    zl.ap().rearrange("p (j c) -> p j c", j=J),
                    zfm_h[b, :, :, 128 * ti:128 * (ti + 1)]), qz2, 16)
                p.wait("y", "sv", ln3_marks["x1"][par])
                p.inc("y", lambda y, b=b, ti=ti, x1b=x1b: y.dma_start(
                    x1b[:], x1_h[b, 128 * ti:128 * (ti + 1), :]), qz2, 16)
                p.wait("t", qz2)
                p.wait("t", "ss", p.val("ss"))
                mm_group(p, [
                    (lambda t, j=j, zl=zl: t.transpose(
                        ps_t[:, 128 * j:128 * (j + 1)], zl[:, 128 * j:128 * (j + 1)], idb[:]))
                    for j in range(J)])
                ln3_marks["zl"][par] = p.val("st")
                p.wait("s", "st")
                p.wait("s", "sv", ln3_marks["t2"])
                p.inc("s", lambda s: s.copy(t2[:], ps_t[:]), "ss")
                p.wait("v", "ss")
                emit_ln(p, t2[:], znorm[:], B)
                ln3_marks["t2"] = p.val("sv")
                p.wait("v", qz2)
                p.wait("v", "qo" if par == 0 else "qo1", ln3_marks["o1"][par])
                p.add("v", lambda v, x1b=x1b, ob=ob: v.tensor_tensor(ob[:], x1b[:], bett[:], ALU.add))
                p.add("v", lambda v: v.tensor_tensor(znorm[:], znorm[:], gamt[:], ALU.mult))
                p.add("v", lambda v, ob=ob: v.tensor_tensor(ob[:], ob[:], znorm[:], ALU.add))
                ln3_marks["x1"][par] = p.val("sv")
                p.wait("y", "sv")
                p.inc("y", lambda y, b=b, ti=ti, ob=ob: y.dma_start(
                    out_h[b, 128 * ti:128 * (ti + 1), :], ob[:]), "qo" if par == 0 else "qo1", 16)
                ln3_marks["o1"][par] = p.val("qo" if par == 0 else "qo1")

        es2.close()
        ffn_marks = {"mask": 0, "maskg": 0, "gelu": {}, "w2": {}}
        p.wait("t", "ss", p.val("ss"))
        for eh in range(2):
            if eh == 1:
                for ee in range(8):
                    e_g = 8 + ee
                    p.wait("g", "st", p.val("st"))
                    p.inc("g", lambda g, ee=ee, e_g=e_g: g.dma_start(
                        w1s[:, ee, :, :], w1_h[e_g].rearrange("(kc p) f -> p kc f", p=128)), "qw", 16)
                    p.inc("g", lambda g, ee=ee, e_g=e_g: g.dma_start(
                        w2s[:, ee, :, :], w2_h[e_g].rearrange("(fc p) d -> p fc d", p=128)), "qw", 16)
            p.wait("t", "qw")
            for b in range(NB):
                p.wait("y", "st", p.val("st"))
                p.wait("y", "qx", p.val("qx"))
                p.inc("y", lambda y, b=b: y.dma_start(
                    h2s.ap().rearrange("p j t -> p (j t)"),
                    h2fm_h[b].rearrange("p j t -> p (j t)")), "qx", 16)
                p.wait("t", "qx")
                for tt in range(4):
                    tsl = slice(512 * tt, 512 * (tt + 1))
                    for ee in range(8):
                        e_g = eh * 8 + ee
                        for fc in range(2):
                            po = 512 * (2 * (ee % 2) + fc)
                            p.wait("t", "ss", ffn_marks["gelu"].get((ee % 2, fc), 0))
                            fns = [
                                (lambda t, ee=ee, kc=kc, fc=fc, po=po, tsl=tsl: t.matmul(
                                    ps_h[:, po:po + 512], w1s[:, ee, kc, 128 * fc:128 * (fc + 1)],
                                    h2s[:, kc, tsl], start=(kc == 0), stop=False))
                                for kc in range(J)]
                            fns.append(lambda t, b=b, e_g=e_g, po=po, tsl=tsl: t.matmul(
                                ps_h[:, po:po + 512], eyes[:, e_g, :], selbs[b][:, tsl],
                                start=False, stop=True))
                            mm_group(p, fns)
                            p.wait("s", "st")
                            p.inc("s", lambda s, ee=ee, fc=fc, po=po, e_g=e_g: s.activation(
                                Hm[:, ee, fc, :], ps_h[:, po:po + 512], AF.Gelu,
                                bias=b1s[:, fc, e_g:e_g + 1]), "ss")
                            ffn_marks["gelu"][(ee % 2, fc)] = p.val("ss")
                    for grp in range(4):
                        p.wait("t", "ss", p.val("ss"))
                        p.wait("t", "sv", ffn_marks.get("oevac", 0))
                        for dc4 in range(2):
                            dc = grp * 2 + dc4
                            po = 512 * dc4
                            fns = []
                            for ee in range(8):
                                for fc in range(2):
                                    first = (ee == 0 and fc == 0)
                                    last_plain = (ee == 7 and fc == 1 and eh == 0)
                                    fns.append(lambda t, ee=ee, fc=fc, dc=dc, po=po,
                                               first=first, last_plain=last_plain: t.matmul(
                                        ps_o[:, po:po + 512],
                                        w2s[:, ee, fc, 128 * dc:128 * (dc + 1)],
                                        Hm[:, ee, fc, :],
                                        start=first, stop=last_plain))
                            if eh == 1:
                                fns.append(lambda t, b=b, dc=dc, po=po, tsl=tsl: t.matmul(
                                    ps_o[:, po:po + 512], b2s[:, 128 * dc:128 * (dc + 1)],
                                    selbs[b][:, tsl], start=False, stop=True))
                            mm_group(p, fns)
                        for dc4 in range(2):
                            dc = grp * 2 + dc4
                            po = 512 * dc4
                            if eh == 0:
                                p.wait("v", "st")
                                p.wait("v", "qz", p.val("qz"))
                                p.inc("v", lambda v, po=po: v.tensor_copy(ztile[:], ps_o[:, po:po + 512]), "sv")
                                ffn_marks["oevac"] = p.val("sv")
                                p.wait("y", "sv")
                                p.inc("y", lambda y, b=b, dc=dc, tt=tt: y.dma_start(
                                    zp_h[b, :, dc, 512 * tt:512 * (tt + 1)], ztile[:]), "qz", 16)
                            else:
                                p.wait("y", "sv", p.val("sv"))
                                p.wait("y", "qz", p.val("qz"))
                                p.inc("y", lambda y, b=b, dc=dc, tt=tt: y.dma_start(
                                    ztile[:], zp_h[b, :, dc, 512 * tt:512 * (tt + 1)]), "qz", 16)
                                p.wait("v", "qz")
                                p.wait("v", "st")
                                p.inc("v", lambda v, po=po: v.tensor_tensor(
                                    ztile[:], ztile[:], ps_o[:, po:po + 512], ALU.add), "sv")
                                ffn_marks["oevac"] = p.val("sv")
                                p.wait("v", "sv")
                                p.wait("v", "qz", p.val("qz"))
                                p.inc("v", lambda v, dc=dc, tsl=tsl: v.tensor_tensor(
                                    ztmp[:], ztile[:], h2s[:, dc, tsl], ALU.add), "sv")
                                p.wait("y", "sv")
                                p.inc("y", lambda y, b=b, dc=dc, tt=tt: y.dma_start(
                                    zfm_h[b, :, dc, 512 * tt:512 * (tt + 1)], ztmp[:]), "qz", 16)
                if eh == 1 and tt == 3:
                    emit_ln3(b, range(NTT))

        p.wait("y", "qo")
        p.wait("y", "qo1")
        p.replay(block)
    return nc


_NC_CACHE = {}


def _eyee():
    import ml_dtypes
    a = np.zeros((16, 16, 128), np.float32)
    for e in range(16):
        a[e, e, :] = 256.0
    return a.astype(ml_dtypes.bfloat16)


def kernel(L0_attn_x, L0_attn_time_decay, L0_attn_time_first, L0_attn_time_mix_k,
           L0_attn_time_mix_v, L0_attn_time_mix_r, L0_attn_Wk, L0_attn_Wv,
           L0_attn_Wr, L0_attn_Wo, L0_ffn_wg, L0_ffn_bg, L0_ffn_w1, L0_ffn_b1,
           L0_ffn_w2, L0_ffn_b2, L0_ffn_gamma, L0_ffn_beta):
    f = np.float32
    cc = np.ascontiguousarray
    x = cc(np.asarray(L0_attn_x, f))
    ident = np.eye(128, dtype=f)
    import ml_dtypes as _mld
    iota1 = np.tile(np.arange(1, 257, dtype=f)[None, :], (128, 1)).astype(_mld.bfloat16)

    if "l1" not in _NC_CACHE:
        _NC_CACHE["l1"] = build_l1()
        _NC_CACHE["l2"] = build_l2()
    nc1, nc2 = _NC_CACHE["l1"], _NC_CACHE["l2"]

    common1 = {
        "wkT": cc(np.asarray(L0_attn_Wk, f).T), "wvT": cc(np.asarray(L0_attn_Wv, f).T),
        "wrT": cc(np.asarray(L0_attn_Wr, f).T), "woT": cc(np.asarray(L0_attn_Wo, f).T),
        "wg": cc(np.asarray(L0_ffn_wg, f)), "bg": cc(np.asarray(L0_ffn_bg, f).reshape(1, E)),
        "tdec": cc(np.asarray(L0_attn_time_decay, f).reshape(C)),
        "tfirst": cc(np.asarray(L0_attn_time_first, f).reshape(C)),
        "tmk": cc(np.asarray(L0_attn_time_mix_k, f).reshape(C)),
        "tmv": cc(np.asarray(L0_attn_time_mix_v, f).reshape(C)),
        "tmr": cc(np.asarray(L0_attn_time_mix_r, f).reshape(C)),
        "ident": ident, "iota1": iota1,
    }
    in1 = [dict(common1, x=cc(x[4 * i:4 * i + 4])) for i in range(8)]
    res1 = run_bass_kernel_spmd(nc1, in1, core_ids=list(range(8)))
    mall = cc(np.concatenate([res1.results[i]["maskedsc"] for i in range(8)], axis=0))

    common2 = {
        "mall": mall,
        "w1": cc(np.asarray(L0_ffn_w1, f)), "w2": cc(np.asarray(L0_ffn_w2, f)),
        "b1": cc(np.asarray(L0_ffn_b1, f).reshape(E, 2, 128).transpose(2, 1, 0)),
        "b2": cc(np.asarray(L0_ffn_b2, f)),
        "eyee": _eyee(), "gam": cc(np.asarray(L0_ffn_gamma, f).reshape(1, C)),
        "bet": cc(np.asarray(L0_ffn_beta, f).reshape(1, C)),
        "ident": ident,
    }
    in2 = [dict(common2, x1=res1.results[i]["x1"], h2fm=res1.results[i]["h2fm"],
                mymsk=res1.results[i]["maskedsc"]) for i in range(8)]
    res2 = run_bass_kernel_spmd(nc2, in2, core_ids=list(range(8)))
    out = np.concatenate([res2.results[i]["out"] for i in range(8)], axis=0)
    return out.astype(np.float32)


# version 22 (final)
# speedup vs baseline: 1.0344x; 1.0172x over previous
import numpy as np
import contextlib
import concourse.bass as bass
import concourse.mybir as mybir
from concourse.bass_utils import run_bass_kernel_spmd

F32 = mybir.dt.float32
BF16 = mybir.dt.bfloat16
AF = mybir.ActivationFunctionType
ALU = mybir.AluOpType
AX = mybir.AxisListType

NB, T, C, J = 4, 2048, 1024, 8
TH, NCK, E, FP = 1024, 4, 16, 256
LN_EPS, EPS_GATE = 1e-5, 1e-6
NTT = T // 128


class Prog:
    def __init__(self, nc):
        self.nc = nc
        self.ops = {k: [] for k in ("v", "s", "t", "g", "y")}
        self.sems, self.cnt = {}, {}

    def sem(self, es, name):
        self.sems[name] = es.enter_context(self.nc.semaphore(name))
        self.cnt[name] = 0

    AUTO = {"v": "sv", "s": "ss", "g": "sg"}

    def add(self, eng, fn, dep=True):
        if eng in self.AUTO:
            sem = self.AUTO[eng]
            h = self.sems[sem]
            pre = self.cnt[sem]
            self.cnt[sem] += 1
            def wrapped(e, fn=fn, h=h, pre=pre, dep=dep):
                if dep and pre > 0:
                    e.wait_ge(h, pre)
                fn(e).then_inc(h, 1)
            self.ops[eng].append(wrapped)
        else:
            self.ops[eng].append(fn)

    def inc(self, eng, fn, sem, n=1):
        if eng in self.AUTO and sem == self.AUTO[eng]:
            self.add(eng, fn)
            return
        h = self.sems[sem]
        self.cnt[sem] += n
        self.ops[eng].append(lambda e, fn=fn, h=h, n=n: fn(e).then_inc(h, n))

    def wait(self, eng, sem, val=None):
        h = self.sems[sem]
        v = self.cnt[sem] if val is None else val
        if v > 0:
            self.ops[eng].append(lambda e, h=h, v=v: e.wait_ge(h, v))

    def val(self, sem):
        return self.cnt[sem]

    def replay(self, block):
        ops = self.ops

        @block.vector
        def _(v):
            for f in ops["v"]:
                f(v)

        @block.scalar
        def _(s):
            for f in ops["s"]:
                f(s)

        @block.tensor
        def _(t):
            for f in ops["t"]:
                f(t)

        @block.gpsimd
        def _(g):
            for f in ops["g"]:
                f(g)

        @block.sync
        def _(y):
            for f in ops["y"]:
                f(y)


def emit_ln(p, x_ap, out_ap, B):
    bst, mv, rstd = B["bst"], B["mv"], B["rstd"]
    p.add("v", lambda v: v.bn_stats(bst[:, 0:6], x_ap[:, 0:512]))
    p.add("v", lambda v: v.bn_stats(bst[:, 6:12], x_ap[:, 512:1024]))
    p.add("v", lambda v: v.bn_aggr(mv[:], bst[:]))
    p.wait("s", "sv")
    p.add("s", lambda s: s.activation(rstd[:], mv[:, 1:2], AF.Ln, bias=B["epst"][:]))
    p.add("s", lambda s: s.activation(rstd[:], rstd[:], AF.Exp, scale=-0.5))
    p.wait("v", "ss")
    p.add("v", lambda v: v.tensor_scalar(out_ap, x_ap, mv[:, 0:1], rstd[:], ALU.subtract, ALU.mult))


def mm_group(p, fns):
    for i, fn in enumerate(fns):
        if i == len(fns) - 1:
            p.inc("t", fn, "st")
        else:
            p.add("t", fn)


def build_l1():
    nc = bass.Bass("TRN2", target_bir_lowering=False)
    x_h = nc.dram_tensor("x", [NB, T, C], F32, kind="ExternalInput").ap()
    wkT_h = nc.dram_tensor("wkT", [C, C], F32, kind="ExternalInput").ap()
    wvT_h = nc.dram_tensor("wvT", [C, C], F32, kind="ExternalInput").ap()
    wrT_h = nc.dram_tensor("wrT", [C, C], F32, kind="ExternalInput").ap()
    woT_h = nc.dram_tensor("woT", [C, C], F32, kind="ExternalInput").ap()
    wg_h = nc.dram_tensor("wg", [C, E], F32, kind="ExternalInput").ap()
    bg_h = nc.dram_tensor("bg", [1, E], F32, kind="ExternalInput").ap()
    td_h = nc.dram_tensor("tdec", [C], F32, kind="ExternalInput").ap()
    tf_h = nc.dram_tensor("tfirst", [C], F32, kind="ExternalInput").ap()
    tmk_h = nc.dram_tensor("tmk", [C], F32, kind="ExternalInput").ap()
    tmv_h = nc.dram_tensor("tmv", [C], F32, kind="ExternalInput").ap()
    tmr_h = nc.dram_tensor("tmr", [C], F32, kind="ExternalInput").ap()
    ident_h = nc.dram_tensor("ident", [128, 128], F32, kind="ExternalInput").ap()
    iota1_h = nc.dram_tensor("iota1", [128, 256], BF16, kind="ExternalInput").ap()

    x1_h = nc.dram_tensor("x1", [NB, T, C], F32, kind="ExternalOutput").ap()
    h2fm_h = nc.dram_tensor("h2fm", [NB, 128, J, T], BF16, kind="ExternalOutput").ap()
    msk_h = nc.dram_tensor("maskedsc", [NB, T, E], F32, kind="ExternalOutput").ap()

    p = Prog(nc)
    es = contextlib.ExitStack()
    with es:
        sb = lambda n, sh, dt=BF16: es.enter_context(nc.sbuf_tensor(n, sh, dt))
        ps_ = lambda n, sh, dt=F32: es.enter_context(nc.psum_tensor(n, sh, dt))

        wk = sb("wk", [128, J, C])
        wv = sb("wv", [128, J, C])
        wr = sb("wr", [128, J, C])
        wo = sb("wo", [128, J, C])
        wgs = sb("wgs", [128, J, E])
        bgrow = sb("bgrow", [1, E])
        ones_col = sb("ones_col", [1, 128])
        idb = sb("idb", [128, 128])
        tmks = sb("tmks", [128, J], F32)
        tmvs = sb("tmvs", [128, J], F32)
        tmrs = sb("tmrs", [128, J], F32)
        us = sb("us", [128, J], F32)
        wpos = sb("wpos", [128, J], F32)
        lamall = sb("lamall", [128, J], F32)
        ewj = sb("ewj", [128, 256], F32)
        wneg = sb("wneg", [128, J], F32)
        iota1 = sb("iota1s", [128, 256])

        xt = sb("xt", [128, C], F32)
        htile = sb("htile", [128, C])
        B = dict(bst=sb("bst", [128, 12], F32), mv=sb("mv", [128, 2], F32),
                 rstd=sb("rstd", [128, 1], F32), epst=sb("epst", [128, 1], F32))
        hfm = sb("hfm", [128, J, T + 1])
        blk = sb("blk", [128, J, TH])
        blv = sb("blv", [128, J, TH])
        blr = sb("blr", [128, J, TH])
        kpl = sb("kpl", [128, 2, TH])
        vpl = sb("vpl", [128, 2, TH])
        srpl = sb("srpl", [128, 2, TH])
        eks = sb("eks", [128, TH + 1])
        ekvs = sb("ekvs", [128, TH + 1])
        kap = sb("kap", [128, 1], F32)
        gp2 = sb("gp2", [128, 2], F32)
        gab_all = sb("gab_all", [128, J, 2], F32)
        lastBA = sb("lastBA", [128, 2 * NCK], F32)
        GBA = sb("GBA", [128, 2 * NCK], F32)
        lamq = sb("lamq", [128, J, 2 * NCK], F32)
        nk = sb("nk", [128, 3], F32)
        us3 = sb("us3", [128, J, 3], F32)
        ksc = sb("ksc", [128, 1], F32)
        ekl_all = sb("ekl_all", [128, J], F32)
        ekvl_all = sb("ekvl_all", [128, J], F32)
        EA = sb("EA", [128, TH])
        EB = sb("EB", [128, TH])
        bl_d = sb("bl_d", [128, 512]).ap()
        euk = sb("euk", [128, TH])
        rwkv = sb("rwkv", [128, J, TH])
        afm = sb("afm", [128, J, 512])
        atile = sb("atile", [128, C])
        x1t = sb("x1t", [128, C], F32)
        h2t = atile
        stag3 = htile.ap().rearrange("p (j c) -> p j c", j=J)
        scst = sb("scst", [128, E], F32)
        mx8 = sb("mx8", [128, 8], F32)
        nmx = sb("nmx", [128, 1], F32)
        ssum = sb("ssum", [128, 1], F32)
        rsum = sb("rsum", [128, 1], F32)
        maskt = sb("maskt", [128, E], F32)
        mskd = sb("mskd", [128, E], F32)

        ps_proj = ps_("ps_proj", [128, 1024])
        ps_tp = ps_("ps_tp", [128, 2048], BF16)
        ps_wo = ps_("ps_wo", [128, 512])
        ps_gate = ps_("ps_gate", [128, E])

        for s_ in ("sv", "ss", "st", "sg", "qx", "qxa", "qxb", "qw", "qo", "qh"):
            p.sem(es, s_)
        block = es.enter_context(nc.Block())

        for w_s, w_hh in ((wk, wkT_h), (wv, wvT_h), (wr, wrT_h), (wo, woT_h)):
            p.inc("g", lambda g, a=w_s, b_=w_hh: g.dma_start(
                a[:], b_.rearrange("(kc p) d -> p kc d", p=128)), "qw", 16)
        p.inc("g", lambda g: g.dma_start(wgs[:], wg_h.rearrange("(kc p) e -> p kc e", p=128)), "qw", 16)
        p.inc("g", lambda g: g.dma_start(bgrow[:], bg_h[:]), "qw", 16)
        p.inc("g", lambda g: g.dma_start(idb[:], ident_h[:]), "qw", 16)
        for v_s, v_hh in ((tmks, tmk_h), (tmvs, tmv_h), (tmrs, tmr_h), (us, tf_h), (wpos, td_h)):
            p.inc("y", lambda y, a=v_s, b_=v_hh: y.dma_start(
                a[:], b_.rearrange("(j p) -> p j", p=128), allow_slow_non_contiguous=True), "qx", 16)
        p.inc("y", lambda y: y.dma_start(iota1[:], iota1_h[:]), "qx", 16)
        p.wait("s", "qx")
        p.inc("s", lambda s: s.activation(wpos[:], wpos[:], AF.Exp), "ss")
        p.wait("v", "ss")
        p.add("v", lambda v: v.memset(ones_col[:], 0))
        p.add("v", lambda v: v.tensor_scalar(ones_col[:], ones_col[:], 0.0, 1.0, ALU.mult, ALU.add))
        p.add("v", lambda v: v.memset(B["epst"][:], float(LN_EPS)))
        p.add("v", lambda v: v.memset(us3[:], 0))
        for j in range(J):
            p.add("v", lambda v, j=j: v.tensor_copy(us3[:, j, 1:2], us[:, j:j + 1]))
        for j in range(J):
            p.add("v", lambda v, j=j: v.memset(lamq[:, j, :], 0))
            p.add("v", lambda v, j=j: v.tensor_scalar(
                lamq[:, j, :], lamq[:, j, :], wpos[:, j:j + 1], None, ALU.subtract))
            p.inc("v", lambda v, j=j: v.memset(lamq[:, j, NCK:NCK + 1], -1e5), "sv")
        p.wait("s", "sv")
        for j in range(J):
            p.inc("s", lambda s, j=j: s.activation(lamq[:, j, :], lamq[:, j, :], AF.Exp, scale=256.0), "ss")
        p.inc("s", lambda s: s.activation(lamall[:], wpos[:], AF.Exp, scale=-1.0), "ss")
        p.inc("v", lambda v: v.tensor_scalar(
            wneg[:], wpos[:], -1.0, None, ALU.mult), "sv")
        wneg_mark = p.val("sv")
        p.wait("v", "ss")
        p.inc("v", lambda v: v.memset(hfm[:], 0), "sv")
        p.wait("t", "sv")
        p.wait("t", "qw")
        p.wait("y", "sv")

        bank_mark = {0: 0, 1: 0}
        tp_mark = {0: 0, 1: 0}
        wo_mark = [0]
        afm_free = [0]
        wkv_marks = {}
        last_wkv_sg = [0]
        xbuf_free = {0: 0, 1: 0}

        def emit_ln1_tile(b, ti):
            par = ti % 2
            xbuf = xt if par == 0 else x1t
            qsem = "qxa" if par == 0 else "qxb"
            p.wait("y", "sv", xbuf_free[par])
            p.inc("y", lambda y, b=b, ti=ti, xbuf=xbuf: y.dma_start(
                xbuf[:], x_h[b, 128 * ti:128 * (ti + 1), :]), qsem, 16)
            p.wait("v", qsem)
            p.wait("v", "st", p.val("st"))
            p.wait("v", "qh", p.val("qh"))
            emit_ln(p, xbuf[:], htile[:], B)
            xbuf_free[par] = p.val("sv")
            tpo = 1024 * par
            p.wait("t", "sv", p.val("sv"))
            p.wait("t", "ss", tp_mark[par])
            mm_group(p, [
                (lambda t, j=j, tpo=tpo: t.transpose(
                    ps_tp[:, tpo + 128 * j:tpo + 128 * (j + 1)],
                    htile[:, 128 * j:128 * (j + 1)], idb[:]))
                for j in range(J)])
            p.wait("s", "st")
            p.inc("s", lambda s, ti=ti, tpo=tpo: s.copy(
                hfm.ap()[:, :, 1 + 128 * ti:1 + 128 * (ti + 1)],
                ps_tp[:, tpo:tpo + 1024].rearrange("p (j c) -> p j c", j=J)), "ss")
            tp_mark[par] = p.val("ss")

        def emit_blends(b, h):
            p.wait("v", "st", p.val("st"))
            t0 = h * TH
            for tt in range(2):
                ts0 = t0 + 512 * tt
                osl = slice(512 * tt, 512 * (tt + 1))
                for j in range(J):
                    h_sl = hfm[:, j, 1 + ts0:1 + ts0 + 512]
                    hs_sl = hfm[:, j, ts0:ts0 + 512]
                    p.add("v", lambda v, h_sl=h_sl, hs_sl=hs_sl: v.tensor_tensor(
                        bl_d, h_sl, hs_sl, ALU.subtract))
                    p.add("v", lambda v, j=j, osl=osl, hs_sl=hs_sl: v.scalar_tensor_tensor(
                        blk[:, j, osl], bl_d, tmks[:, j:j + 1], hs_sl, ALU.mult, ALU.add))
                    p.add("v", lambda v, j=j, osl=osl, hs_sl=hs_sl: v.scalar_tensor_tensor(
                        blv[:, j, osl], bl_d, tmvs[:, j:j + 1], hs_sl, ALU.mult, ALU.add), dep=False)
                    p.add("v", lambda v, j=j, osl=osl, hs_sl=hs_sl: v.scalar_tensor_tensor(
                        blr[:, j, osl], bl_d, tmrs[:, j:j + 1], hs_sl, ALU.mult, ALU.add), dep=False)
            return p.val("sv")

        def emit_proj(step, bl_ready):
            h, j = divmod(step, J)
            par = step % 2
            pm = wkv_marks.get(step - 2)
            first = True
            for pi, (wmat, bl, dst, actf) in enumerate((
                    (wk, blk, kpl, AF.Copy), (wv, blv, vpl, AF.Copy),
                    (wr, blr, srpl, AF.Sigmoid))):
                for tt in range(2):
                    bank = (pi * 2 + tt) % 2
                    po = 512 * bank
                    p.wait("t", "ss", bank_mark[bank])
                    if first:
                        p.wait("t", "sv", bl_ready)
                        if pm:
                            p.wait("t", "sv", pm["sg"])
                            p.wait("t", "ss", pm["ss"])
                        first = False
                    mm_group(p, [
                        (lambda t, wmat=wmat, bl=bl, kc=kc, j=j, po=po, tt=tt: t.matmul(
                            ps_proj[:, po:po + 512],
                            wmat[:, kc, 128 * j:128 * (j + 1)],
                            bl[:, kc, 512 * tt:512 * (tt + 1)],
                            start=(kc == 0), stop=(kc == J - 1)))
                        for kc in range(J)])
                    p.wait("s", "st")
                    p.inc("s", lambda s, dst=dst, par=par, po=po, actf=actf, tt=tt: s.activation(
                        dst[:, par, 512 * tt:512 * (tt + 1)], ps_proj[:, po:po + 512], actf), "ss")
                    bank_mark[bank] = p.val("ss")

        def emit_wkv(step, rwkv_free=0):
            h, j = divmod(step, J)
            par = step % 2
            hf = h
            kp = kpl[:, par, :]
            vp = vpl[:, par, :]
            srp = srpl[:, par, :]
            p.wait("s", "sv", max(last_wkv_sg[0], wneg_mark))
            p.add("s", lambda s, j=j: s.activation(
                ewj[:], iota1[:], AF.Exp, scale=wneg[:, j:j + 1]))
            p.wait("v", "ss", bank_mark[0])
            p.wait("v", "ss", bank_mark[1])
            p.add("v", lambda v, kp=kp: v.tensor_reduce(kap[:], kp, axis=AX.X, op=ALU.max))
            if hf == 0:
                p.add("v", lambda v, j=j: v.tensor_copy(us3[:, j, 2:3], kap[:]))
                p.add("v", lambda v, j=j: v.tensor_scalar(
                    nk[:, 0:2], us3[:, j, 0:2], kap[:], None, ALU.subtract))
            else:
                p.add("v", lambda v, j=j: v.tensor_scalar(
                    nk[:], us3[:, j, :], kap[:], None, ALU.subtract))
            p.wait("s", "sv")
            if hf == 1:
                p.add("s", lambda s: s.activation(ksc[:], nk[:, 2:3], AF.Exp))
            p.add("s", lambda s, kp=kp: s.activation(eks[:, 1:], kp, AF.Exp, bias=nk.ap()[:, 0:1]))
            eks_mark = p.val("ss")
            p.add("s", lambda s, kp=kp: s.activation(euk[:], kp, AF.Exp, bias=nk.ap()[:, 1:2]))
            euk_mark = p.val("ss")
            p.wait("v", "ss", eks_mark)
            p.add("v", lambda v, vp=vp: v.tensor_tensor(ekvs[:, 1:], eks[:, 1:], vp, ALU.mult))
            if hf == 0:
                p.add("v", lambda v: v.memset(eks[:, 0:1], 0))
                p.add("v", lambda v: v.memset(ekvs[:, 0:1], 0))
                p.add("v", lambda v: v.memset(gp2[:], 0))
            else:
                p.add("v", lambda v, j=j: v.tensor_tensor(
                    eks[:, 0:1], ekl_all[:, j:j + 1], ksc[:], ALU.mult))
                p.add("v", lambda v, j=j: v.tensor_tensor(
                    ekvs[:, 0:1], ekvl_all[:, j:j + 1], ksc[:], ALU.mult))
                p.add("v", lambda v, j=j: v.tensor_scalar(
                    gp2[:], gab_all[:, j, :], ksc[:], None, ALU.mult))
            p.add("v", lambda v, j=j: v.tensor_copy(ekl_all[:, j:j + 1], eks[:, TH:TH + 1]))
            p.add("v", lambda v, j=j: v.tensor_copy(ekvl_all[:, j:j + 1], ekvs[:, TH:TH + 1]))
            lamb = lamall.ap()[:, j:j + 1].broadcast_to((128, 256))
            for c in range(NCK):
                sl = slice(256 * c, 256 * (c + 1))
                p.add("v", lambda v, sl=sl, lamb=lamb: v.tensor_tensor_scan(
                    EB[:, sl], lamb, eks[:, sl], 0.0, ALU.mult, ALU.add), dep=(c == 0))
            for c in range(NCK):
                sl = slice(256 * c, 256 * (c + 1))
                p.add("v", lambda v, sl=sl, lamb=lamb: v.tensor_tensor_scan(
                    EA[:, sl], lamb, ekvs[:, sl], 0.0, ALU.mult, ALU.add), dep=False)
            EBr = EB.ap().rearrange("p (c l) -> p c l", l=256)
            EAr = EA.ap().rearrange("p (c l) -> p c l", l=256)
            p.add("v", lambda v, EBr=EBr: v.tensor_copy(lastBA[:, 0:NCK], EBr[:, :, 255:256]))
            p.add("v", lambda v, EAr=EAr: v.tensor_copy(lastBA[:, NCK:2 * NCK], EAr[:, :, 255:256]))
            p.add("v", lambda v, j=j: v.scalar_tensor_tensor(
                lastBA[:, 0:1], lamq[:, j, 0:1], gp2.ap()[:, 0:1], lastBA[:, 0:1], ALU.mult, ALU.add))
            p.add("v", lambda v, j=j: v.scalar_tensor_tensor(
                lastBA[:, NCK:NCK + 1], lamq[:, j, 0:1], gp2.ap()[:, 1:2],
                lastBA[:, NCK:NCK + 1], ALU.mult, ALU.add))
            p.add("v", lambda v, j=j: v.tensor_tensor_scan(
                GBA[:], lamq[:, j, :], lastBA[:], 0.0, ALU.mult, ALU.add))
            p.add("v", lambda v, j=j: v.tensor_copy(
                gab_all[:, j, :], GBA.ap()[:, NCK - 1:2 * NCK:NCK]))
            for c in range(NCK):
                sl = slice(256 * c, 256 * (c + 1))
                p.add("v", lambda v, sl=sl, c=c: v.scalar_tensor_tensor(
                    EB[:, sl], ewj[:],
                    (gp2.ap()[:, 0:1] if c == 0 else GBA[:, c - 1:c]),
                    EB[:, sl], ALU.mult, ALU.add), dep=(c == 0))
            p.wait("v", "ss", euk_mark)
            p.add("v", lambda v: v.tensor_tensor(EB[:], EB[:], euk[:], ALU.add))
            p.wait("s", "sv")
            p.add("s", lambda s: s.activation(EB[:], EB[:], AF.Ln))
            p.add("s", lambda s: s.activation(EB[:], EB[:], AF.Exp, scale=-1.0))
            for c in range(NCK):
                sl = slice(256 * c, 256 * (c + 1))
                p.add("v", lambda v, sl=sl, c=c: v.scalar_tensor_tensor(
                    EA[:, sl], ewj[:],
                    (gp2.ap()[:, 1:2] if c == 0 else GBA[:, NCK + c - 1:NCK + c]),
                    EA[:, sl], ALU.mult, ALU.add), dep=(c == 0))
            p.add("v", lambda v, vp=vp: v.tensor_tensor(euk[:], euk[:], vp, ALU.mult))
            p.add("v", lambda v: v.tensor_tensor(EA[:], EA[:], euk[:], ALU.add))
            p.wait("v", "ss")
            p.add("v", lambda v: v.tensor_tensor(EA[:], EA[:], EB[:], ALU.mult))
            p.wait("v", "st", rwkv_free)
            p.add("v", lambda v, j=j, srp=srp: v.tensor_tensor(
                rwkv[:, j, :], EA[:], srp, ALU.mult))
            wkv_marks[step] = dict(sv=p.val("sv"), ss=p.val("ss"), sg=p.val("sv"))
            last_wkv_sg[0] = p.val("sv")

        def emit_woq(b, h, qq):
            p.wait("t", "sv", wkv_marks[h * J + J - 1]["sg"])
            for m in range(J):
                p.wait("t", "ss", wo_mark[0])
                mm_group(p, [
                    (lambda t, kc=kc, m=m, qq=qq: t.matmul(
                        ps_wo[:], wo[:, kc, 128 * m:128 * (m + 1)],
                        rwkv[:, kc, 512 * qq:512 * (qq + 1)],
                        start=(kc == 0), stop=(kc == J - 1)))
                    for kc in range(J)])
                p.wait("s", "st")
                p.wait("s", "st", afm_free[0])
                p.inc("s", lambda s, m=m: s.copy(afm[:, m, :], ps_wo[:]), "ss")
                wo_mark[0] = p.val("ss")
            return p.val("st")

        def emit_epi_tile(b, h, qq, ti4):
            ti = h * 8 + qq * 4 + ti4
            par = ti4 % 2
            tpo = 1024 * par
            p.wait("t", "ss", wo_mark[0])
            p.wait("t", "ss", tp_mark[par])
            mm_group(p, [
                (lambda t, j=j, ti4=ti4, tpo=tpo: t.transpose(
                    ps_tp[:, tpo + 128 * j:tpo + 128 * (j + 1)],
                    afm[:, j, 128 * ti4:128 * (ti4 + 1)], idb[:]))
                for j in range(J)])
            afm_free[0] = p.val("st")
            p.wait("s", "st")
            p.inc("s", lambda s, tpo=tpo: s.copy(atile[:], ps_tp[:, tpo:tpo + 1024]), "ss")
            tp_mark[par] = p.val("ss")
            atile_ss = p.val("ss")
            p.wait("y", "sv", xbuf_free[0])
            p.inc("y", lambda y, b=b, ti=ti: y.dma_start(
                xt[:], x_h[b, 128 * ti:128 * (ti + 1), :]), "qxa", 16)
            p.wait("v", "qxa")
            p.wait("v", "ss", atile_ss)
            p.wait("v", "qo", p.val("qo"))
            p.add("v", lambda v: v.tensor_tensor(x1t[:], xt[:], atile[:], ALU.add))
            xbuf_free[0] = p.val("sv")
            p.wait("y", "sv")
            p.inc("y", lambda y, b=b, ti=ti: y.dma_start(
                x1_h[b, 128 * ti:128 * (ti + 1), :], x1t[:]), "qo", 16)
            p.wait("v", "st", p.val("st"))
            p.wait("v", "qh", p.val("qh"))
            emit_ln(p, x1t[:], h2t[:], B)
            h2t_sv = p.val("sv")
            tpo2 = 1024 * ((ti4 + 1) % 2)
            p.wait("t", "sv", h2t_sv)
            p.wait("t", "ss", tp_mark[(ti4 + 1) % 2])
            mm_group(p, [
                (lambda t, j=j, tpo2=tpo2: t.transpose(
                    ps_tp[:, tpo2 + 128 * j:tpo2 + 128 * (j + 1)],
                    h2t[:, 128 * j:128 * (j + 1)], idb[:]))
                for j in range(J)])
            p.wait("s", "st")
            p.wait("s", "qh", p.val("qh"))
            p.inc("s", lambda s, tpo2=tpo2: s.copy(
                htile.ap(), ps_tp[:, tpo2:tpo2 + 1024]), "ss")
            tp_mark[(ti4 + 1) % 2] = p.val("ss")
            stag_ss = p.val("ss")
            p.wait("y", "ss", stag_ss)
            p.inc("y", lambda y, b=b, ti=ti: y.dma_start(
                h2fm_h[b, :, :, 128 * ti:128 * (ti + 1)], stag3), "qh", 16)
            p.wait("t", "ss", stag_ss)
            fns = [(lambda t, j=j: t.matmul(
                ps_gate[:], stag3[:, j, :], wgs[:, j, :], start=(j == 0), stop=False))
                for j in range(J)]
            fns.append(lambda t: t.matmul(ps_gate[:], ones_col[:], bgrow[:],
                                          start=False, stop=True))
            mm_group(p, fns)
            p.wait("v", "st")
            p.add("v", lambda v: v.max(mx8[:], ps_gate[:]))
            p.add("v", lambda v: v.tensor_scalar(nmx[:], mx8[:, 0:1], -1.0, None, ALU.mult))
            p.wait("s", "sv")
            p.add("s", lambda s: s.activation(scst[:], ps_gate[:], AF.Exp, bias=nmx[:]))
            p.wait("v", "ss")
            p.add("v", lambda v: v.reduce_sum(ssum[:], scst[:], axis=AX.X))
            p.add("v", lambda v: v.reciprocal(rsum[:], ssum[:]))
            p.add("v", lambda v: v.tensor_scalar(scst[:], scst[:], rsum[:], None, ALU.mult))
            p.add("v", lambda v: v.max(mx8[:], scst[:]))
            p.add("v", lambda v: v.tensor_scalar(
                maskt[:], scst[:], mx8[:, 1:2], None, ALU.is_ge))
            p.wait("v", "qo", p.val("qo"))
            p.add("v", lambda v: v.tensor_tensor(mskd[:], scst[:], maskt[:], ALU.mult))
            p.wait("y", "sv")
            p.inc("y", lambda y, b=b, ti=ti: y.dma_start(
                msk_h[b, 128 * ti:128 * (ti + 1), :], mskd[:]), "qo", 16)

        rwkv_free_st = [0]
        for b in range(NB):
            for ti in range(NTT):
                emit_ln1_tile(b, ti)
            bl0 = emit_blends(b, 0)
            emit_proj(0, bl0)
            for j in range(1, 8):
                emit_wkv(j - 1, rwkv_free_st[0])
                emit_proj(j, bl0)
            bl1 = emit_blends(b, 1)
            emit_wkv(7, rwkv_free_st[0])
            emit_proj(8, bl1)
            emit_woq(b, 0, 0)
            emit_proj(9, bl1)
            for t4 in range(4):
                emit_epi_tile(b, 0, 0, t4)
            wo_st = emit_woq(b, 0, 1)
            rwkv_free_st[0] = wo_st
            emit_wkv(8, rwkv_free_st[0])
            emit_proj(10, bl1)
            emit_epi_tile(b, 0, 1, 0)
            emit_epi_tile(b, 0, 1, 1)
            emit_wkv(9, rwkv_free_st[0])
            emit_proj(11, bl1)
            emit_epi_tile(b, 0, 1, 2)
            emit_epi_tile(b, 0, 1, 3)
            emit_wkv(10, rwkv_free_st[0])
            emit_wkv(11, rwkv_free_st[0])
            emit_proj(12, bl1)
            emit_wkv(12, rwkv_free_st[0])
            emit_proj(13, bl1)
            emit_wkv(13, rwkv_free_st[0])
            emit_proj(14, bl1)
            emit_wkv(14, rwkv_free_st[0])
            emit_proj(15, bl1)
            emit_wkv(15, rwkv_free_st[0])
            emit_woq(b, 1, 0)
            for t4 in range(4):
                emit_epi_tile(b, 1, 0, t4)
            wo_st = emit_woq(b, 1, 1)
            rwkv_free_st[0] = wo_st
            for t4 in range(4):
                emit_epi_tile(b, 1, 1, t4)
            wkv_marks.clear()

        p.wait("y", "qo")
        p.wait("y", "qh")
        p.replay(block)
    return nc


def build_l2():
    nc = bass.Bass("TRN2", target_bir_lowering=False)
    x1_h = nc.dram_tensor("x1", [NB, T, C], F32, kind="ExternalInput").ap()
    h2fm_h = nc.dram_tensor("h2fm", [NB, 128, J, T], BF16, kind="ExternalInput").ap()
    mall_h = nc.dram_tensor("mall", [32, T, E], F32, kind="ExternalInput").ap()
    mymsk_h = nc.dram_tensor("mymsk", [NB, T, E], F32, kind="ExternalInput").ap()
    w1_h = nc.dram_tensor("w1", [E, C, FP], F32, kind="ExternalInput").ap()
    w2_h = nc.dram_tensor("w2", [E, FP, C], F32, kind="ExternalInput").ap()
    b1_h = nc.dram_tensor("b1", [128, 2, E], F32, kind="ExternalInput").ap()
    b2_h = nc.dram_tensor("b2", [E, C], F32, kind="ExternalInput").ap()
    gam_h = nc.dram_tensor("gam", [1, C], F32, kind="ExternalInput").ap()
    bet_h = nc.dram_tensor("bet", [1, C], F32, kind="ExternalInput").ap()
    ident_h = nc.dram_tensor("ident", [128, 128], F32, kind="ExternalInput").ap()
    eyee_h = nc.dram_tensor("eyee", [16, 16, 128], BF16, kind="ExternalInput").ap()

    out_h = nc.dram_tensor("out", [NB, T, C], F32, kind="ExternalOutput").ap()
    zp_h = nc.dram_tensor("zpart", [NB, 128, J, T], F32, kind="Internal").ap()
    zfm_h = nc.dram_tensor("zfm", [NB, 128, J, T], BF16, kind="Internal").ap()

    p = Prog(nc)
    es = contextlib.ExitStack()
    with es:
        sb = lambda n, sh, dt=BF16: es.enter_context(nc.sbuf_tensor(n, sh, dt))
        ps_ = lambda n, sh, dt=F32: es.enter_context(nc.psum_tensor(n, sh, dt))

        idb = sb("idb", [128, 128])
        idf = sb("idf", [128, 128], F32)
        ones_col = sb("ones_col", [1, 128])
        gamt = sb("gamt", [128, C])
        bett = sb("bett", [128, C])
        gbrow = sb("gbrow", [1, C])
        b2s = sb("b2s", [16, C])
        b1s = sb("b1s", [128, 2, E], F32)
        masksb = sb("masksb", [128, 512])
        selbs = [sb(f"selb{b}", [16, T]) for b in range(NB)]
        eyes = sb("eyes", [16, 16, 128])
        zload0 = sb("zload0", [128, C])
        zload1 = sb("zload1", [128, C])
        znorm = sb("znorm", [128, C])
        x1t0 = sb("x1t0", [128, C], F32)
        x1t1 = sb("x1t1", [128, C], F32)
        o10 = sb("o10", [128, C], F32)
        o11 = sb("o11", [128, C], F32)
        t2 = sb("t2", [128, C])
        ztile = sb("ztile", [128, 512], F32)
        ztmp = sb("ztmp", [128, 512])
        B = dict(bst=sb("bst", [128, 12], F32), mv=sb("mv", [128, 2], F32),
                 rstd=sb("rstd", [128, 1], F32), epst=sb("epst", [128, 1], F32))
        h2s = sb("h2s", [128, J, T])
        Hm = sb("Hm", [128, 8, 2, 512])
        w1s = sb("w1s", [128, 8, J, FP])
        w2s = sb("w2s", [128, 8, 2, C])
        es2 = contextlib.ExitStack()
        sb2 = lambda n, sh, dt=BF16: es2.enter_context(nc.sbuf_tensor(n, sh, dt))
        mt32 = sb2("mt32", [128, 32, E], F32)
        dsum = sb2("dsum", [128, E], F32)
        rdt = sb2("rdt", [128, NTT, E], F32)
        routf = sb2("routf", [128, NTT, E], F32)
        rem = sb2("rem", [64, T], F32)
        rstage = sb2("rstage", [16, T], F32)
        nnz = sb2("nnz", [64, 1], F32)
        mxr = sb2("mxr", [64, 1], F32)
        tlo = sb2("tlo", [64, 1], F32)
        thi = sb2("thi", [64, 1], F32)
        tmid = sb2("tmid", [64, 1], F32)
        thet = sb2("thet", [64, 1], F32)
        cntb = sb2("cntb", [64, 1], F32)
        geb = sb2("geb", [64, 1], F32)
        cdl = sb2("cdl", [64, 1], F32)
        cdh = sb2("cdh", [64, 1], F32)
        selA = sb2("selA", [64, T])
        junk2 = selA
        zind = sb2("zind", [64, T])
        zrank = sb2("zrank", [64, T], mybir.dt.float16)
        zlast = sb2("zlast", [64, J], F32)
        zG = sb2("zG", [64, J], F32)
        onesT = sb2("onesT", [64, 256], F32)
        ones8 = sb2("ones8", [64, J], F32)
        Ke = sb2("Ke", [64, 1], F32)
        Fb = sb2("Fb", [64, 1], F32)
        Fc = sb2("Fc", [64, 1], F32)
        selB = sb2("selB", [64, T])

        ps_h = ps_("ps_h", [128, 2048])
        ps_o = ps_("ps_o", [128, 1024])
        ps_x = ps_("ps_x", [128, 512])
        ps_t = ps_("ps_t", [128, 1024], BF16)

        for s_ in ("sv", "ss", "st", "sg", "qx", "qxa", "qxb", "qw", "qb", "qo", "qo1", "qz"):
            p.sem(es, s_)
        block = es.enter_context(nc.Block())

        p.inc("g", lambda g: g.dma_start(idb[:], ident_h[:]), "qw", 16)
        p.inc("y", lambda y: y.dma_start(idf[:], ident_h[:]), "qx", 16)
        p.inc("g", lambda g: g.dma_start(b2s[:], b2_h[:]), "qw", 16)
        p.inc("y", lambda y: y.dma_start(eyes[:], eyee_h[:]), "qx", 16)
        p.inc("g", lambda g: g.dma_start(b1s[:], b1_h[:]), "qb", 16)
        p.wait("v", "qb")
        p.inc("v", lambda v: v.tensor_scalar(
            b1s.ap().rearrange("p a b -> p (a b)"), b1s.ap().rearrange("p a b -> p (a b)"),
            -256.0, None, ALU.add), "sv")
        p.wait("v", "qx")
        p.add("v", lambda v: v.memset(ones_col[:], 0))
        p.add("v", lambda v: v.tensor_scalar(ones_col[:], ones_col[:], 0.0, 1.0, ALU.mult, ALU.add))
        p.add("v", lambda v: v.memset(onesT[:], 0))
        p.add("v", lambda v: v.tensor_scalar(onesT[:], onesT[:], 0.0, 1.0, ALU.mult, ALU.add))
        p.add("v", lambda v: v.memset(B["epst"][:], float(LN_EPS)))
        p.add("v", lambda v: v.memset(ones8[:], 0))
        p.inc("v", lambda v: v.tensor_scalar(ones8[:], ones8[:], 0.0, 1.0, ALU.mult, ALU.add), "sv")
        p.wait("t", "sv", p.val("sv"))
        for src_h, dst, dty in ((gam_h, gamt, "bf"), (bet_h, bett, "f32")):
            p.inc("g", lambda g, src_h=src_h: g.dma_start(gbrow[:], src_h[:]), "qw", 16)
            p.wait("t", "qw")
            for half in range(2):
                p.wait("t", "ss", p.val("ss"))
                p.inc("t", lambda t, half=half: t.matmul(
                    ps_x[:], ones_col[:], gbrow[:, 512 * half:512 * (half + 1)],
                    start=True, stop=True), "st")
                p.wait("s", "st")
                p.inc("s", lambda s, dst=dst, half=half: s.copy(
                    dst[:, 512 * half:512 * (half + 1)], ps_x[:]), "ss")
            p.wait("g", "ss")

        for ee in range(8):
            p.inc("g", lambda g, ee=ee: g.dma_start(
                w1s[:, ee, :, :], w1_h[ee].rearrange("(kc p) f -> p kc f", p=128)), "qw", 16)
            p.inc("g", lambda g, ee=ee: g.dma_start(
                w2s[:, ee, :, :], w2_h[ee].rearrange("(fc p) d -> p fc d", p=128)), "qw", 16)
        for ti in range(NTT):
            p.wait("y", "sv", p.val("sv"))
            p.inc("y", lambda y, ti=ti: y.dma_start(
                mt32[:], mall_h[:, 128 * ti:128 * (ti + 1), :].rearrange("b p e -> p b e")), "qx", 16)
            p.wait("v", "qx")
            for lvl in (16, 8, 4, 2, 1):
                p.add("v", lambda v, lvl=lvl: v.tensor_tensor(
                    mt32.ap()[:, 0:lvl, :].rearrange("p b e -> p (b e)"),
                    mt32.ap()[:, 0:lvl, :].rearrange("p b e -> p (b e)"),
                    mt32.ap()[:, lvl:2 * lvl, :].rearrange("p b e -> p (b e)"), ALU.add))
            p.add("v", lambda v: v.tensor_scalar(dsum[:], mt32[:, 0, :], EPS_GATE, None, ALU.add))
            p.add("v", lambda v, ti=ti: v.reciprocal(rdt[:, ti, :], dsum[:]))
        for b in range(NB):
            p.wait("y", "sv", p.val("sv"))
            p.wait("y", "st", p.val("st"))
            p.inc("y", lambda y, b=b: y.dma_start(
                routf[:], mymsk_h[b].rearrange("(ti p) e -> p ti e", p=128)), "qx", 16)
            p.wait("v", "qx")
            for ti in range(NTT):
                p.add("v", lambda v, ti=ti: v.scalar_tensor_tensor(
                    routf[:, ti, :], routf[:, ti, :], 32.0, rdt[:, ti, :],
                    ALU.mult, ALU.mult), dep=(ti == 0))
            p.wait("t", "sv")
            for g8 in range(2):
                p.wait("t", "ss", p.val("ss"))
                mm_group(p, [
                    (lambda t, ti=ti, k=k: t.transpose(
                        ps_h.ap()[0:16, 128 * k:128 * (k + 1)], routf[:, ti, :], idf[:]))
                    for k, ti in enumerate(range(8 * g8, 8 * g8 + 8))])
                p.wait("s", "st")
                p.inc("s", lambda s, g8=g8: s.copy(
                    rstage[:, 1024 * g8:1024 * (g8 + 1)], ps_h.ap()[0:16, 0:1024]), "ss")
            p.wait("y", "ss")
            p.inc("y", lambda y, b=b: y.dma_start(rem[16 * b:16 * (b + 1), :], rstage[:]), "qx", 16)
        p.wait("v", "qx")
        p.wait("v", "ss")
        p.inc("v", lambda v: v.tensor_scalar(
            junk2[:], rem[:], 0.0, None, ALU.is_gt, ALU.add, accum_out=nnz[:]), "sv")
        p.add("v", lambda v: v.tensor_reduce(mxr[:], rem[:], axis=AX.X, op=ALU.max))
        p.wait("v", "sv")
        p.inc("v", lambda v: v.tensor_scalar(mxr[:], mxr[:], 1e-30, None, ALU.max), "sv")
        p.wait("s", "sv")
        p.inc("s", lambda s: s.activation(thi[:], mxr[:], AF.Ln), "ss")
        p.wait("v", "ss")
        p.add("v", lambda v: v.tensor_scalar(thi[:], thi[:], 0.01, None, ALU.add))
        p.add("v", lambda v: v.memset(tlo[:], 0))
        p.inc("v", lambda v: v.tensor_scalar(tlo[:], tlo[:], 0.0, -110.0, ALU.mult, ALU.add), "sv")
        for it in range(44):
            p.wait("v", "sv", p.val("sv"))
            p.add("v", lambda v: v.tensor_tensor(tmid[:], tlo[:], thi[:], ALU.add))
            p.inc("v", lambda v: v.tensor_scalar(tmid[:], tmid[:], 0.5, None, ALU.mult), "sv")
            p.wait("s", "sv")
            p.inc("s", lambda s: s.activation(thet[:], tmid[:], AF.Exp), "ss")
            p.wait("v", "ss")
            p.inc("v", lambda v: v.tensor_scalar(
                junk2[:], rem[:], thet[:], None, ALU.is_gt, ALU.add, accum_out=cntb[:]), "sv")
            p.wait("v", "sv")
            p.inc("v", lambda v: v.tensor_scalar(geb[:], cntb[:], 1024.0, None, ALU.is_ge), "sv")
            p.wait("v", "sv")
            p.add("v", lambda v: v.tensor_tensor(cdl[:], tmid[:], tlo[:], ALU.subtract))
            p.add("v", lambda v: v.scalar_tensor_tensor(tlo[:], cdl[:], geb[:], tlo[:], ALU.mult, ALU.add))
            p.add("v", lambda v: v.tensor_tensor(cdh[:], thi[:], tmid[:], ALU.subtract))
            p.inc("v", lambda v: v.scalar_tensor_tensor(thi[:], cdh[:], geb[:], tmid[:], ALU.mult, ALU.add), "sv")
        p.wait("s", "sv")
        p.inc("s", lambda s: s.activation(thet[:], tlo[:], AF.Exp), "ss")
        p.wait("v", "ss")
        p.inc("v", lambda v: v.tensor_scalar(selA[:], rem[:], thet[:], None, ALU.is_gt), "sv")
        p.inc("v", lambda v: v.tensor_scalar(zind[:], rem[:], 0.0, None, ALU.is_le), "sv")
        p.wait("v", "sv")
        for c in range(8):
            sl = slice(256 * c, 256 * (c + 1))
            fn = lambda v, sl=sl: v.tensor_tensor_scan(
                zrank[:, sl], onesT[:], zind[:, sl], 0.0, ALU.mult, ALU.add)
            if c == 7:
                p.inc("v", fn, "sv")
            else:
                p.add("v", fn)
        p.wait("v", "sv")
        zrr = zrank.ap().rearrange("p (c l) -> p c l", l=256)
        p.inc("v", lambda v, zrr=zrr: v.tensor_copy(zlast[:], zrr[:, :, 255:256]), "sv")
        p.wait("v", "sv")
        p.inc("v", lambda v: v.tensor_tensor_scan(zG[:], ones8[:], zlast[:], 0.0, ALU.mult, ALU.add), "sv")
        p.wait("v", "sv")
        for c in range(1, 8):
            sl = slice(256 * c, 256 * (c + 1))
            p.add("v", lambda v, sl=sl, c=c: v.tensor_scalar(
                zrank[:, sl], zrank[:, sl], zG[:, c - 1:c], None, ALU.add))
        p.add("v", lambda v: v.tensor_scalar(Ke[:], nnz[:], -1.0, 1024.0, ALU.mult, ALU.add))
        p.inc("v", lambda v: v.tensor_scalar(Fb[:], nnz[:], 1024.0, None, ALU.is_ge), "sv")
        p.wait("v", "sv")
        p.add("v", lambda v: v.tensor_scalar(Fc[:], Fb[:], -1.0, 1.0, ALU.mult, ALU.add))
        p.inc("v", lambda v: v.tensor_scalar(selB[:], zrank[:], Ke[:], None, ALU.is_le), "sv")
        p.wait("v", "sv")
        p.inc("v", lambda v: v.tensor_tensor(selB[:], selB[:], zind[:], ALU.mult), "sv")
        p.inc("v", lambda v: v.tensor_scalar(zind[:], zind[:], -1.0, 1.0, ALU.mult, ALU.add), "sv")
        p.inc("v", lambda v: v.tensor_tensor(selB[:], selB[:], zind[:], ALU.add), "sv")
        p.wait("v", "sv")
        p.add("v", lambda v: v.tensor_scalar(selA[:], selA[:], Fb[:], None, ALU.mult))
        p.inc("v", lambda v: v.tensor_scalar(selB[:], selB[:], Fc[:], None, ALU.mult), "sv")
        p.wait("v", "sv")
        p.inc("v", lambda v: v.tensor_tensor(selA[:], selA[:], selB[:], ALU.add), "sv")
        p.wait("y", "sv")
        for b in range(NB):
            p.inc("y", lambda y, b=b: y.dma_start(selbs[b][:], selA[16 * b:16 * (b + 1), :]), "qx", 16)

        ln3_marks = {"zl": {0: 0, 1: 0}, "x1": {0: 0, 1: 0}, "o1": {0: 0, 1: 0}, "t2": 0}

        def emit_ln3(b, tis):
            p.wait("y", "qz", p.val("qz"))
            for ti in tis:
                par = ti % 2
                zl = zload0 if par == 0 else zload1
                x1b = x1t0 if par == 0 else x1t1
                ob = o10 if par == 0 else o11
                qz2 = "qxa" if par == 0 else "qxb"
                p.wait("y", "st", ln3_marks["zl"][par])
                p.inc("y", lambda y, b=b, ti=ti, zl=zl: y.dma_start(
                    zl.ap().rearrange("p (j c) -> p j c", j=J),
                    zfm_h[b, :, :, 128 * ti:128 * (ti + 1)]), qz2, 16)
                p.wait("y", "sv", ln3_marks["x1"][par])
                p.inc("y", lambda y, b=b, ti=ti, x1b=x1b: y.dma_start(
                    x1b[:], x1_h[b, 128 * ti:128 * (ti + 1), :]), qz2, 16)
                p.wait("t", qz2)
                p.wait("t", "ss", p.val("ss"))
                mm_group(p, [
                    (lambda t, j=j, zl=zl: t.transpose(
                        ps_t[:, 128 * j:128 * (j + 1)], zl[:, 128 * j:128 * (j + 1)], idb[:]))
                    for j in range(J)])
                ln3_marks["zl"][par] = p.val("st")
                p.wait("s", "st")
                p.wait("s", "sv", ln3_marks["t2"])
                p.inc("s", lambda s: s.copy(t2[:], ps_t[:]), "ss")
                p.wait("v", "ss")
                emit_ln(p, t2[:], znorm[:], B)
                ln3_marks["t2"] = p.val("sv")
                p.wait("v", qz2)
                p.wait("v", "qo" if par == 0 else "qo1", ln3_marks["o1"][par])
                p.add("v", lambda v, x1b=x1b, ob=ob: v.tensor_tensor(ob[:], x1b[:], bett[:], ALU.add))
                p.add("v", lambda v: v.tensor_tensor(znorm[:], znorm[:], gamt[:], ALU.mult))
                p.add("v", lambda v, ob=ob: v.tensor_tensor(ob[:], ob[:], znorm[:], ALU.add))
                ln3_marks["x1"][par] = p.val("sv")
                p.wait("y", "sv")
                p.inc("y", lambda y, b=b, ti=ti, ob=ob: y.dma_start(
                    out_h[b, 128 * ti:128 * (ti + 1), :], ob[:]), "qo" if par == 0 else "qo1", 16)
                ln3_marks["o1"][par] = p.val("qo" if par == 0 else "qo1")

        es2.close()
        ffn_marks = {"mask": 0, "maskg": 0, "gelu": {}, "w2": {}}
        p.wait("t", "ss", p.val("ss"))
        for eh in range(2):
            if eh == 1:
                for ee in range(8):
                    e_g = 8 + ee
                    p.wait("g", "st", p.val("st"))
                    p.inc("g", lambda g, ee=ee, e_g=e_g: g.dma_start(
                        w1s[:, ee, :, :], w1_h[e_g].rearrange("(kc p) f -> p kc f", p=128)), "qw", 16)
                    p.inc("g", lambda g, ee=ee, e_g=e_g: g.dma_start(
                        w2s[:, ee, :, :], w2_h[e_g].rearrange("(fc p) d -> p fc d", p=128)), "qw", 16)
            p.wait("t", "qw")
            for b in range(NB):
                p.wait("y", "st", p.val("st"))
                p.wait("y", "qx", p.val("qx"))
                p.inc("y", lambda y, b=b: y.dma_start(
                    h2s.ap().rearrange("p j t -> p (j t)"),
                    h2fm_h[b].rearrange("p j t -> p (j t)")), "qx", 16)
                p.wait("t", "qx")
                for tt in range(4):
                    tsl = slice(512 * tt, 512 * (tt + 1))
                    for ee in range(8):
                        e_g = eh * 8 + ee
                        for fc in range(2):
                            po = 512 * (2 * (ee % 2) + fc)
                            p.wait("t", "ss", ffn_marks["gelu"].get((ee % 2, fc), 0))
                            fns = [
                                (lambda t, ee=ee, kc=kc, fc=fc, po=po, tsl=tsl: t.matmul(
                                    ps_h[:, po:po + 512], w1s[:, ee, kc, 128 * fc:128 * (fc + 1)],
                                    h2s[:, kc, tsl], start=(kc == 0), stop=False))
                                for kc in range(J)]
                            fns.append(lambda t, b=b, e_g=e_g, po=po, tsl=tsl: t.matmul(
                                ps_h[:, po:po + 512], eyes[:, e_g, :], selbs[b][:, tsl],
                                start=False, stop=True))
                            mm_group(p, fns)
                            p.wait("s", "st")
                            p.inc("s", lambda s, ee=ee, fc=fc, po=po, e_g=e_g: s.activation(
                                Hm[:, ee, fc, :], ps_h[:, po:po + 512], AF.Gelu,
                                bias=b1s[:, fc, e_g:e_g + 1]), "ss")
                            ffn_marks["gelu"][(ee % 2, fc)] = p.val("ss")
                    for grp in range(4):
                        p.wait("t", "ss", p.val("ss"))
                        p.wait("t", "sv", ffn_marks.get("oevac", 0))
                        for dc4 in range(2):
                            dc = grp * 2 + dc4
                            po = 512 * dc4
                            fns = []
                            for ee in range(8):
                                for fc in range(2):
                                    first = (ee == 0 and fc == 0)
                                    last_plain = (ee == 7 and fc == 1 and eh == 0)
                                    fns.append(lambda t, ee=ee, fc=fc, dc=dc, po=po,
                                               first=first, last_plain=last_plain: t.matmul(
                                        ps_o[:, po:po + 512],
                                        w2s[:, ee, fc, 128 * dc:128 * (dc + 1)],
                                        Hm[:, ee, fc, :],
                                        start=first, stop=last_plain))
                            if eh == 1:
                                fns.append(lambda t, b=b, dc=dc, po=po, tsl=tsl: t.matmul(
                                    ps_o[:, po:po + 512], b2s[:, 128 * dc:128 * (dc + 1)],
                                    selbs[b][:, tsl], start=False, stop=True))
                            mm_group(p, fns)
                        for dc4 in range(2):
                            dc = grp * 2 + dc4
                            po = 512 * dc4
                            if eh == 0:
                                p.wait("v", "st")
                                p.wait("v", "qz", p.val("qz"))
                                p.inc("v", lambda v, po=po: v.tensor_copy(ztile[:], ps_o[:, po:po + 512]), "sv")
                                ffn_marks["oevac"] = p.val("sv")
                                p.wait("y", "sv")
                                p.inc("y", lambda y, b=b, dc=dc, tt=tt: y.dma_start(
                                    zp_h[b, :, dc, 512 * tt:512 * (tt + 1)], ztile[:]), "qz", 16)
                            else:
                                p.wait("y", "sv", p.val("sv"))
                                p.wait("y", "qz", p.val("qz"))
                                p.inc("y", lambda y, b=b, dc=dc, tt=tt: y.dma_start(
                                    ztile[:], zp_h[b, :, dc, 512 * tt:512 * (tt + 1)]), "qz", 16)
                                p.wait("v", "qz")
                                p.wait("v", "st")
                                p.inc("v", lambda v, po=po: v.tensor_tensor(
                                    ztile[:], ztile[:], ps_o[:, po:po + 512], ALU.add), "sv")
                                ffn_marks["oevac"] = p.val("sv")
                                p.wait("v", "sv")
                                p.wait("v", "qz", p.val("qz"))
                                p.inc("v", lambda v, dc=dc, tsl=tsl: v.tensor_tensor(
                                    ztmp[:], ztile[:], h2s[:, dc, tsl], ALU.add), "sv")
                                p.wait("y", "sv")
                                p.inc("y", lambda y, b=b, dc=dc, tt=tt: y.dma_start(
                                    zfm_h[b, :, dc, 512 * tt:512 * (tt + 1)], ztmp[:]), "qz", 16)
                if eh == 1 and tt == 3:
                    emit_ln3(b, range(NTT))

        p.wait("y", "qo")
        p.wait("y", "qo1")
        p.replay(block)
    return nc


_NC_CACHE = {}


def _eyee():
    import ml_dtypes
    a = np.zeros((16, 16, 128), np.float32)
    for e in range(16):
        a[e, e, :] = 256.0
    return a.astype(ml_dtypes.bfloat16)


def kernel(L0_attn_x, L0_attn_time_decay, L0_attn_time_first, L0_attn_time_mix_k,
           L0_attn_time_mix_v, L0_attn_time_mix_r, L0_attn_Wk, L0_attn_Wv,
           L0_attn_Wr, L0_attn_Wo, L0_ffn_wg, L0_ffn_bg, L0_ffn_w1, L0_ffn_b1,
           L0_ffn_w2, L0_ffn_b2, L0_ffn_gamma, L0_ffn_beta):
    f = np.float32
    cc = np.ascontiguousarray
    x = cc(np.asarray(L0_attn_x, f))
    ident = np.eye(128, dtype=f)
    import ml_dtypes as _mld
    iota1 = np.tile(np.arange(1, 257, dtype=f)[None, :], (128, 1)).astype(_mld.bfloat16)

    if "l1" not in _NC_CACHE:
        _NC_CACHE["l1"] = build_l1()
        _NC_CACHE["l2"] = build_l2()
    nc1, nc2 = _NC_CACHE["l1"], _NC_CACHE["l2"]

    common1 = {
        "wkT": cc(np.asarray(L0_attn_Wk, f).T), "wvT": cc(np.asarray(L0_attn_Wv, f).T),
        "wrT": cc(np.asarray(L0_attn_Wr, f).T), "woT": cc(np.asarray(L0_attn_Wo, f).T),
        "wg": cc(np.asarray(L0_ffn_wg, f)), "bg": cc(np.asarray(L0_ffn_bg, f).reshape(1, E)),
        "tdec": cc(np.asarray(L0_attn_time_decay, f).reshape(C)),
        "tfirst": cc(np.asarray(L0_attn_time_first, f).reshape(C)),
        "tmk": cc(np.asarray(L0_attn_time_mix_k, f).reshape(C)),
        "tmv": cc(np.asarray(L0_attn_time_mix_v, f).reshape(C)),
        "tmr": cc(np.asarray(L0_attn_time_mix_r, f).reshape(C)),
        "ident": ident, "iota1": iota1,
    }
    in1 = [dict(common1, x=cc(x[4 * i:4 * i + 4])) for i in range(8)]
    res1 = run_bass_kernel_spmd(nc1, in1, core_ids=list(range(8)))
    mall = cc(np.concatenate([res1.results[i]["maskedsc"] for i in range(8)], axis=0))

    common2 = {
        "mall": mall,
        "w1": cc(np.asarray(L0_ffn_w1, f)), "w2": cc(np.asarray(L0_ffn_w2, f)),
        "b1": cc(np.asarray(L0_ffn_b1, f).reshape(E, 2, 128).transpose(2, 1, 0)),
        "b2": cc(np.asarray(L0_ffn_b2, f)),
        "eyee": _eyee(), "gam": cc(np.asarray(L0_ffn_gamma, f).reshape(1, C)),
        "bet": cc(np.asarray(L0_ffn_beta, f).reshape(1, C)),
        "ident": ident,
    }
    in2 = [dict(common2, x1=res1.results[i]["x1"], h2fm=res1.results[i]["h2fm"],
                mymsk=res1.results[i]["maskedsc"]) for i in range(8)]
    res2 = run_bass_kernel_spmd(nc2, in2, core_ids=list(range(8)))
    out = np.concatenate([res2.results[i]["out"] for i in range(8)], axis=0)
    return out.astype(np.float32)
